# version 17
# baseline (speedup 1.0000x reference)
import numpy as np
import ml_dtypes
import concourse.bass as bass
import concourse.mybir as mybir
from concourse.bass_utils import run_bass_kernel_spmd

F32 = mybir.dt.float32
BF16 = mybir.dt.bfloat16
FP8 = mybir.dt.float8e4
I32 = mybir.dt.int32
AF = mybir.ActivationFunctionType
ALU = mybir.AluOpType
AX = mybir.AxisListType

N_CORES = 8
T = 2048
E = 1024
H = 16
D = 64
HID = 4096
NT = T // 128
NE = E // 128
NH = HID // 128
EPS = 1e-5

_DEBUG_TAPS = []
_PHASES = 99
_P5_SEQ = False
_ROPE = True

ENGINES = ("tensor", "vector", "scalar", "gpsimd", "sync")


class _Op:
    __slots__ = ("eng", "fn", "reads", "writes", "gidx", "eidx", "dma_sem",
                 "dma_thresh", "marked", "waits", "tag")

    def __init__(self, eng, fn, reads, writes, gidx, tag=""):
        self.eng = eng
        self.fn = fn
        self.reads = reads
        self.writes = writes
        self.gidx = gidx
        self.eidx = -1
        self.dma_sem = None
        self.dma_thresh = 0
        self.marked = -1
        self.waits = []
        self.tag = tag


class Builder:
    def __init__(self, nc):
        self.nc = nc
        self.ops = []
        self.eops = {e: [] for e in ENGINES}
        self.res = {}
        self.prog_sems = {e: nc.alloc_semaphore(name=f"prog_{e}") for e in ENGINES}
        self.dma_pools = {}
        self.dma_state = {}
        for q in ("gpsimd", "sync"):
            self.dma_pools[q] = [nc.alloc_semaphore(name=f"dma_{q}_{i}") for i in range(8)]
            self.dma_state[q] = {"i": 0, "tot": [0] * 8}
        self._deps_of = {}

    def op(self, eng, fn, reads=(), writes=(), tag=""):
        o = _Op(eng, fn, list(reads), list(writes), len(self.ops), tag)
        self._track(o)
        return o

    def dma(self, queue, out_ap, in_ap, reads=(), writes=(), tag="", transpose=False):
        pool = self.dma_pools[queue]
        st = self.dma_state[queue]
        slot = st["i"] % 8
        st["i"] += 1
        sem = pool[slot]
        prior = st["tot"][slot]
        st["tot"][slot] = prior + 16
        thresh = st["tot"][slot]

        def fn(eng, _o=out_ap, _i=in_ap, _t=transpose):
            if _t:
                return eng.dma_start_transpose(_o, _i)
            return eng.dma_start(_o, _i)

        o = _Op(queue, fn, list(reads), list(writes), len(self.ops), tag)
        o.dma_sem = sem
        o.dma_thresh = thresh
        if prior > 0:
            o.waits.append((sem, prior))
        self._track(o)
        return o

    def _track(self, o):
        deps = set()
        for r in o.reads:
            ent = self.res.get(r)
            if ent is not None and ent[0] is not None:
                deps.add(ent[0])
        for w in o.writes:
            ent = self.res.get(w)
            if ent is not None:
                if ent[0] is not None:
                    deps.add(ent[0])
                deps.update(ent[1])
        deps = {d for d in deps if d.eng != o.eng or d.dma_sem is not None}
        self._deps_of[o] = deps
        self.ops.append(o)
        self.eops[o.eng].append(o)
        o.eidx = len(self.eops[o.eng]) - 1
        for r in o.reads:
            self.res.setdefault(r, [None, []])[1].append(o)
        for w in o.writes:
            self.res[w] = [o, []]

    def accessors(self, name):
        out = []
        for k, (w, rs) in self.res.items():
            if k == name or k.startswith(name):
                if w is not None:
                    out.append(w)
                out.extend(rs)
        return out

    def seed_pseudo(self, key, dep_ops):
        if dep_ops:
            self.res[key] = [None, list(dep_ops)]

    def finalize_and_emit(self):
        nc = self.nc
        for o in self.ops:
            for d in self._deps_of[o]:
                if d.dma_sem is None and d.marked < 0:
                    d.marked = 0
        for e in ENGINES:
            n = 0
            for o in self.eops[e]:
                if o.dma_sem is None and o.marked == 0:
                    n += 1
                    o.marked = n
        for o in self.ops:
            need = {}
            for d in self._deps_of[o]:
                if d.dma_sem is not None:
                    key = ("dma", id(d.dma_sem))
                    sem, val = d.dma_sem, d.dma_thresh
                else:
                    key = ("eng", d.eng)
                    sem, val = self.prog_sems[d.eng], d.marked
                    assert d.marked > 0
                if key not in need or need[key][1] < val:
                    need[key] = (sem, val)
            o.waits.extend(need.values())

        waited = {e: {} for e in ENGINES}

        def run_engine(ename):
            def body(eng):
                wd = waited[ename]
                for o in self.eops[ename]:
                    for sem, val in o.waits:
                        k = id(sem)
                        if wd.get(k, -1) >= val:
                            continue
                        eng.wait_ge(sem, val)
                        wd[k] = val
                    inst = o.fn(eng)
                    if o.dma_sem is not None:
                        inst.then_inc(o.dma_sem, 16)
                    elif o.marked > 0:
                        if inst is None:
                            eng.sem_inc(self.prog_sems[ename], 1)
                        else:
                            inst.then_inc(self.prog_sems[ename], 1)
            return body

        with nc.Block() as blk:
            blk.tensor(run_engine("tensor"))
            blk.vector(run_engine("vector"))
            blk.scalar(run_engine("scalar"))
            blk.gpsimd(run_engine("gpsimd"))
            blk.sync(run_engine("sync"))


class Arena:
    def __init__(self, nc, builder):
        self.nc = nc
        self.b = builder
        self.base = (nc.sbuf_base + 31) & ~31
        self.cur = 0
        self.limit = nc.sbuf_top - self.base
        self.tensors = {}
        self.free_list = []
        self.peak = 0

    def alloc(self, name, shape, dtype):
        bpp = int(np.prod(shape[1:])) * mybir.dt.size(dtype)
        bpp = (bpp + 31) & ~31
        off = None
        names = set()
        for i, (fo, fs, nm) in enumerate(self.free_list):
            if fs >= bpp:
                off = fo
                names = nm
                if fs == bpp:
                    self.free_list.pop(i)
                else:
                    self.free_list[i] = (fo + bpp, fs - bpp, set(nm))
                break
        if off is None:
            off = self.cur
            self.cur += bpp
            self.peak = max(self.peak, self.cur)
            assert self.cur <= self.limit, (
                f"sbuf overflow at {name}: {self.cur} > {self.limit}")
        h = self.nc.alloc_sbuf_tensor_at(name, list(shape), dtype,
                                         offset=self.base + off)
        self.tensors[name] = (h, off, bpp)
        okey = None
        if names:
            dep_ops = []
            for nm in names:
                dep_ops.extend(self.b.accessors(nm))
            if dep_ops:
                okey = f"__ov_{name}"
                self.b.seed_pseudo(okey, dep_ops)
        return h.ap(), okey

    def free(self, name, keys=None):
        h, off, bpp = self.tensors.pop(name)
        self.free_list.append((off, bpp, set(keys) if keys else {name}))
        self.free_list.sort(key=lambda x: x[0])
        merged = []
        for fo, fs, nm in self.free_list:
            if merged and merged[-1][0] + merged[-1][1] == fo:
                po, ps, pn = merged[-1]
                merged[-1] = (po, ps + fs, pn | nm)
            else:
                merged.append((fo, fs, set(nm)))
        if merged and merged[-1][0] + merged[-1][1] == self.cur:
            pass
        self.free_list = merged


def _wk(name, okey, *extra):
    out = [name] + list(extra)
    if okey is not None:
        out.append(okey)
    return out



def _host_prep(inputs):
    x = np.ascontiguousarray(inputs["L0_attn_x"])
    Wqkv = np.asarray(inputs["L0_attn_Wqkv"])
    bqkv = np.asarray(inputs["L0_attn_bqkv"])
    Wproj = np.ascontiguousarray(inputs["L0_attn_Wproj"])
    bproj = np.asarray(inputs["L0_attn_bproj"])

    perm = np.concatenate([np.arange(h * D, (h + 1) * D).reshape(D // 2, 2).T.ravel()
                           for h in range(H)])
    Wq = np.ascontiguousarray(Wqkv[:, 0:E][:, perm])
    Wk = np.ascontiguousarray(Wqkv[:, E:2 * E][:, perm])
    Wv = np.ascontiguousarray(Wqkv[:, 2 * E:3 * E])
    bq = np.ascontiguousarray(bqkv[0:E][perm].reshape(NE, 128).T)
    bk = np.ascontiguousarray(bqkv[E:2 * E][perm].reshape(NE, 128).T)
    bv_row = np.ascontiguousarray(bqkv[2 * E:3 * E].reshape(1, E))
    bproj_row = np.ascontiguousarray(bproj.reshape(1, E))

    pos = np.arange(T, dtype=np.float64)
    inv = 10000.0 ** (-np.arange(0, D, 2, dtype=np.float64) / D)
    fr = pos[:, None] * inv[None, :]
    cos = np.cos(fr).astype(np.float32).T
    sin = np.sin(fr).astype(np.float32).T
    cosrep = np.ascontiguousarray(np.tile(cos, (4, 1)))
    sinpm = np.ascontiguousarray(np.tile(np.concatenate([-sin, sin], 0), (2, 1)))

    km = np.arange(128)[:, None]
    qm = np.arange(128)[None, :]
    M = np.where(km > qm, -1e30, 0.0).astype(np.float32)
    maskT = np.ascontiguousarray(M.T)
    ident = np.eye(128, dtype=np.float32)
    perm32 = np.zeros((128, 128), np.float32)
    perm32[np.arange(128) ^ 32, np.arange(128)] = 1.0
    ones_col = np.ones((65, 128), np.float32)
    epsrep = np.full((128, 1), 1e-5, np.float32)
    onesf = np.ones((128, 128), np.float32)
    onesb = np.ones((128, 1), np.float32)
    selA = np.zeros((33, 256), np.float32)
    selA[0, :64] = 1.0
    selA[0, 192:] = 1.0
    selA[32] = selA[0]

    w1 = np.asarray(inputs["L0_ffn_w1"])
    w2 = np.asarray(inputs["L0_ffn_w2"])

    shared = {
        "wq": Wq, "wk": Wk, "wv": Wv, "bq": bq, "bk": bk,
        "bv_row": bv_row, "wproj": Wproj, "bproj_row": bproj_row,
        "cosrep": cosrep, "sinpm": sinpm, "maskt": maskT, "ident": ident, "perm32": perm32,
        "ones_col": ones_col, "selA": selA, "epsrep": epsrep, "onesf": onesf, "onesb": onesb,
        "w1t": np.ascontiguousarray(w1.T),
        "w2t": np.ascontiguousarray(w2.T),
        "b1_row": np.ascontiguousarray(np.asarray(inputs["L0_ffn_b1"]).reshape(1, HID)),
        "b2_row": np.ascontiguousarray(np.asarray(inputs["L0_ffn_b2"]).reshape(1, E)),
        "ln1w_rep": np.ascontiguousarray(
            np.broadcast_to(np.asarray(inputs["L0_ffn_ln1_w"]), (128, E))),
        "ln1b_rep": np.ascontiguousarray(
            np.broadcast_to(np.asarray(inputs["L0_ffn_ln1_b"]), (128, E))),
        "ln2w_rep": np.ascontiguousarray(
            np.broadcast_to(np.asarray(inputs["L0_ffn_ln2_w"]), (128, HID))),
        "ln2b_rep": np.ascontiguousarray(
            np.broadcast_to(np.asarray(inputs["L0_ffn_ln2_b"]), (128, HID))),
    }
    in_maps = []
    for c in range(N_CORES):
        m = dict(shared)
        m["xt"] = np.ascontiguousarray(x[c].T)
        m["xtok"] = np.ascontiguousarray(x[c])
        in_maps.append(m)
    return in_maps


def build_program():
    nc = bass.Bass("TRN2", target_bir_lowering=False,
                   detect_race_conditions=False)
    b = Builder(nc)
    ar = Arena(nc, b)

    def din(name, shape, dtype=F32):
        return nc.dram_tensor(name, list(shape), dtype, kind="ExternalInput").ap()

    xt_d = din("xt", (E, T))
    xtok_d = din("xtok", (T, E))
    wq_d, wk_d, wv_d = din("wq", (E, E)), din("wk", (E, E)), din("wv", (E, E))
    bq_d, bk_d = din("bq", (128, NE)), din("bk", (128, NE))
    bv_row_d = din("bv_row", (1, E))
    wproj_d = din("wproj", (E, E))
    bproj_row_d = din("bproj_row", (1, E))
    cos_d, sin_d = din("cosrep", (128, T)), din("sinpm", (128, T))
    maskt_d = din("maskt", (128, 128))
    ident_d = din("ident", (128, 128))
    perm32_d = din("perm32", (128, 128))
    ones_col_d = din("ones_col", (65, 128))
    eps_d = din("epsrep", (128, 1))
    onesf_d = din("onesf", (128, 128))
    onesb_d = din("onesb", (128, 1))
    selA_d = din("selA", (33, 256))
    w1t_d = din("w1t", (E, HID))
    w2t_d = din("w2t", (HID, E))
    b1_row_d = din("b1_row", (1, HID))
    b2_row_d = din("b2_row", (1, E))
    ln1w_d, ln1b_d = din("ln1w_rep", (128, E)), din("ln1b_rep", (128, E))
    ln2w_d, ln2b_d = din("ln2w_rep", (128, HID)), din("ln2b_rep", (128, HID))

    y_d = nc.dram_tensor("y", [T, E], F32, kind="ExternalOutput").ap()
    taps = {}
    for tname, shape, dt in _DEBUG_TAPS:
        taps[tname] = nc.dram_tensor(tname, list(shape), dt,
                                     kind="ExternalOutput").ap()

    def store_tap(name, src_ap, reads):
        if name in taps:
            b.dma("sync", taps[name], src_ap, reads=reads, tag=f"tap_{name}")

    PO = nc.alloc_psum_tensor("PO", [128, 2048], F32).ap()
    PB = [nc.alloc_psum_tensor(f"PB{i}", [128, 512], F32).ap() for i in range(4)]
    save_base = nc.psum_base
    nc.psum_base = save_base - 2
    PT = [nc.alloc_psum_tensor(f"PT{i}", [128, 128], BF16).ap() for i in range(2)]
    nc.psum_base = save_base

    xtb = [ar.alloc(f"xtb{i}", [128, T], BF16)[0] for i in range(NE)]
    wsb = {w: [ar.alloc(f"{w}sb{i}", [128, E], BF16)[0] for i in range(NE)]
           for w in ("wq", "wk", "wv")}
    qT = [ar.alloc(f"qT{i}", [128, T], BF16)[0] for i in range(NE)]
    kT = [ar.alloc(f"kT{i}", [128, T], BF16)[0] for i in range(NE)]
    vst = [ar.alloc(f"vst{t}", [128, H * (D + 1)], BF16)[0] for t in range(NT)]
    cosr = ar.alloc("cosr", [128, T], BF16)[0]
    sinr = ar.alloc("sinr", [128, T], BF16)[0]
    part = [ar.alloc(f"part{i}", [128, T], BF16)[0] for i in range(2)]
    bq_sb = ar.alloc("bq_sb", [128, NE], F32)[0]
    bk_sb = ar.alloc("bk_sb", [128, NE], F32)[0]
    bv_sb = ar.alloc("bv_sb", [1, E], BF16)[0]
    ones_sb = ar.alloc("ones_sb", [65, 128], BF16)[0]
    eps_sb = ar.alloc("eps_sb", [128, 1], F32)[0]
    onesf_sb = ar.alloc("onesf_sb", [128, 128], F32)[0]
    onesb_sb = ar.alloc("onesb_sb", [128, 1], BF16)[0]
    maskt_sb = ar.alloc("maskt_sb", [128, 128], BF16)[0]
    ident_sb = ar.alloc("ident_sb", [128, 128], BF16)[0]
    perm32_sb = ar.alloc("perm32_sb", [128, 128], BF16)[0]
    selA_sb = ar.alloc("selA_sb", [33, 256], BF16)[0]

    for i in range(NE):
        b.dma("gpsimd", xtb[i], xt_d[i * 128:(i + 1) * 128, :],
              writes=[f"xtb{i}"], tag=f"ld_xtb{i}")
    for wname, d in (("wq", wq_d), ("wk", wk_d), ("wv", wv_d)):
        for i in range(NE):
            b.dma("gpsimd", wsb[wname][i], d[i * 128:(i + 1) * 128, :],
                  writes=[f"{wname}sb{i}"], tag=f"ld_{wname}{i}")
    b.dma("gpsimd", cosr, cos_d, writes=["cosr"])
    b.dma("gpsimd", sinr, sin_d, writes=["sinr"])
    b.dma("gpsimd", bq_sb, bq_d, writes=["bq_sb"])
    b.dma("gpsimd", bk_sb, bk_d, writes=["bk_sb"])
    b.dma("gpsimd", bv_sb, bv_row_d, writes=["bv_sb"])
    b.dma("gpsimd", ones_sb, ones_col_d, writes=["ones_sb"])
    b.dma("gpsimd", eps_sb, eps_d, writes=["eps_sb"])
    b.dma("gpsimd", onesf_sb, onesf_d, writes=["onesf_sb"])
    b.dma("gpsimd", onesb_sb, onesb_d, writes=["onesb_sb"])
    b.dma("gpsimd", maskt_sb, maskt_d, writes=["maskt_sb"])
    b.dma("gpsimd", ident_sb, ident_d, writes=["ident_sb"])
    b.dma("gpsimd", perm32_sb, perm32_d, writes=["perm32_sb"])
    b.dma("gpsimd", selA_sb, selA_d, writes=["selA_sb"])

    for t in range(NT):
        vv = vst[t].rearrange("p (h u) -> p h u", u=D + 1)
        b.op("vector", lambda e, _a=vv: e.memset(_a[:, :, D:D + 1], 1.0),
             writes=[f"vst{t}:ones"], tag=f"ones_v{t}")

    def bcast128(val_ap, dst_ap, rkey, wkeys, tag):
        b.op("tensor",
             lambda e: e.matmul(PB[0][:, 0:1], onesf_sb[0:1, :], val_ap,
                                start=True, stop=True),
             reads=[rkey, "onesf_sb"], writes=["PB0"], tag=f"{tag}_mm")
        b.op("vector", lambda e: (e.tensor_copy(dst_ap, PB[0][:, 0:1]),
                                  e.drain())[-1],
             reads=["PB0"], writes=list(wkeys) + ["PB0"], tag=f"{tag}_ev")

    pb_i = [0]

    def next_pb():
        i = pb_i[0] % 4
        pb_i[0] += 1
        return PB[i], f"PB{i}"

    def qk_block(which, wt, bias_sb, outT, ob):
        for c in range(4):
            ps, pkey = next_pb()
            for kk in range(NE):
                b.op("tensor",
                     lambda e, _ps=ps, _l=wt[kk][:, ob * 128:(ob + 1) * 128],
                            _r=xtb[kk][:, c * 512:(c + 1) * 512],
                            _st=(kk == 0), _sp=(kk == NE - 1):
                     e.matmul(_ps, _l, _r, start=_st, stop=_sp),
                     reads=[f"w{which}sb{kk}", f"xtb{kk}"],
                     writes=[pkey] if (kk == 0 or kk == NE - 1) else [],
                     tag=f"mm_{which}{ob}c{c}k{kk}")
            b.op("scalar",
                 lambda e, _o=outT[ob][:, c * 512:(c + 1) * 512], _i=ps,
                        _b=bias_sb[:, ob:ob + 1]:
                 e.activation(_o, _i, AF.Identity, bias=_b),
                 reads=[pkey, f"b{which}_sb"],
                 writes=[f"{which}T{ob}:pre{c}"],
                 tag=f"evac_{which}{ob}c{c}")

    def rope_block(which, outT, ob, pi):
        pbuf = part[pi]
        pre = [f"{which}T{ob}:pre{c}" for c in range(4)]
        for g in range(4):
            ps, pkey = next_pb()
            b.op("tensor",
                 lambda e, _ps=ps, _r=outT[ob][:, g * 512:(g + 1) * 512]:
                 e.matmul(_ps, perm32_sb, _r, start=True, stop=True),
                 reads=pre + ["perm32_sb"], writes=[pkey],
                 tag=f"pshm_{which}{ob}g{g}")
            b.op("scalar",
                 lambda e, _o=pbuf[:, g * 512:(g + 1) * 512], _i=ps:
                 e.activation(_o, _i, AF.Identity),
                 reads=[pkey],
                 writes=[f"part{pi}:g{g}", f"part{pi}:m", pkey],
                 tag=f"pshe_{which}{ob}g{g}")
        pr = [f"part{pi}:g{g}" for g in range(4)]
        b.op("vector", lambda e, _p=pbuf, _s=sinr: e.tensor_mul(_p, _p, _s),
             reads=pr + ["sinr"], writes=[f"part{pi}:m"],
             tag=f"rope_s_{which}{ob}")
        b.op("vector", lambda e, _q=outT[ob], _c=cosr: e.tensor_mul(_q, _q, _c),
             reads=pre + ["cosr"] + pr, writes=[f"{which}T{ob}:cosd"],
             tag=f"rope_c_{which}{ob}")
        b.op("vector", lambda e, _q=outT[ob], _p=pbuf: e.tensor_add(_q, _q, _p),
             reads=[f"part{pi}:m", f"{which}T{ob}:cosd"],
             writes=[f"{which}T{ob}"], tag=f"rope_a_{which}{ob}")

    for ob in range(NE):
        qk_block("q", wsb["wq"], bq_sb, qT, ob)
        qk_block("k", wsb["wk"], bk_sb, kT, ob)
        if _ROPE:
            rope_block("q", qT, ob, 0)
            rope_block("k", kT, ob, 1)

    for t in range(NT):
        for c in range(2):
            ps, pkey = next_pb()
            for kk in range(NE):
                b.op("tensor",
                     lambda e, _ps=ps, _l=xtb[kk][:, t * 128:(t + 1) * 128],
                            _r=wsb["wv"][kk][:, c * 512:(c + 1) * 512],
                            _st=(kk == 0):
                     e.matmul(_ps, _l, _r, start=_st, stop=False),
                     reads=[f"wvsb{kk}", f"xtb{kk}"],
                     writes=[pkey] if kk == 0 else [],
                     tag=f"mm_v{t}c{c}k{kk}")
            b.op("tensor",
                 lambda e, _ps=ps, _r=bv_sb[:, c * 512:(c + 1) * 512]:
                 e.matmul(_ps, ones_sb[0:1, :], _r, start=False, stop=True),
                 reads=["ones_sb", "bv_sb"], writes=[pkey], tag=f"mm_vb{t}c{c}")
            vv = vst[t].rearrange("p (h u) -> p h u", u=D + 1)
            pv = ps.rearrange("p (h u) -> p h u", u=D)
            b.op("vector",
                 lambda e, _o=vv[:, c * 8:(c + 1) * 8, 0:D], _i=pv:
                 e.tensor_copy(_o, _i),
                 reads=[pkey], writes=[f"vst{t}:c{c}"], tag=f"evac_v{t}c{c}")

    for ob, which, outT in ([(o, w, t) for o in range(NE)
                             for w, t in (("q", qT), ("k", kT))]
                            if _ROPE else ()):
        if True:
            pbuf = part[ob % 2]
            pre = [f"{which}T{ob}:pre{c}" for c in range(4)]
            for g in range(4):
                ps, pkey = next_pb()
                b.op("tensor",
                     lambda e, _ps=ps, _r=outT[ob][:, g * 512:(g + 1) * 512]:
                     e.matmul(_ps, perm32_sb, _r, start=True, stop=True),
                     reads=pre + ["perm32_sb"], writes=[pkey],
                     tag=f"pshm_{which}{ob}g{g}")
                b.op("scalar",
                     lambda e, _o=pbuf[:, g * 512:(g + 1) * 512], _i=ps:
                     e.activation(_o, _i, AF.Identity),
                     reads=[pkey],
                     writes=[f"part{ob % 2}:g{g}", f"part{ob % 2}:m", pkey],
                     tag=f"pshe_{which}{ob}g{g}")
            pr = [f"part{ob % 2}:g{g}" for g in range(4)]
            b.op("vector", lambda e, _p=pbuf, _s=sinr: e.tensor_mul(_p, _p, _s),
                 reads=pr + ["sinr"], writes=[f"part{ob % 2}:m"],
                 tag=f"rope_s_{which}{ob}")
            b.op("vector", lambda e, _q=outT[ob], _c=cosr: e.tensor_mul(_q, _q, _c),
                 reads=pre + ["cosr"] + pr, writes=[f"{which}T{ob}:cosd"],
                 tag=f"rope_c_{which}{ob}")
            b.op("vector", lambda e, _q=outT[ob], _p=pbuf: e.tensor_add(_q, _q, _p),
                 reads=[f"part{ob % 2}:m", f"{which}T{ob}:cosd"],
                 writes=[f"{which}T{ob}"], tag=f"rope_a_{which}{ob}")

    if _PHASES >= 1:
        for ob in range(NE):
            qkeys = [f"qT{ob}"] if _ROPE else [f"qT{ob}:pre{c}" for c in range(4)]
            kkeys = [f"kT{ob}"] if _ROPE else [f"kT{ob}:pre{c}" for c in range(4)]
            store_tap(f"tap_qT{ob}", qT[ob], qkeys)
            store_tap(f"tap_kT{ob}", kT[ob], kkeys)
        for i in range(NE):
            store_tap(f"tap_xtb{i}", xtb[i], [f"xtb{i}"])
        for t in range(NT):
            store_tap(f"tap_v{t}", vst[t],
                      [f"vst{t}:c0", f"vst{t}:c1", f"vst{t}:ones"])

    qdone = lambda a: ([f"qT{a}"] if _ROPE else [f"qT{a}:pre{c}" for c in range(4)])
    kdone = lambda a: ([f"kT{a}"] if _ROPE else [f"kT{a}:pre{c}" for c in range(4)])

    def wquant(wid, srcs, dsts, win, win_key,
               q_sb, ok_q, marep, ok_ma, inv_ma, ok_inv, i32c,
               ok_win=None, ok_i32=None):
        steps = []
        npc = len(srcs)
        numel = float(npc * 128 * 1024)
        steps.append(lambda: b.op(
            "vector", lambda e: e.memset(q_sb[:, 0:1], 0.0),
            writes=_wk(f"wqa{wid}:acc", ok_q), tag=f"wq{wid}_z"))

        def p1step(i):
            wj = win[i % 2]
            wk_ = f"{win_key}{i % 2}"
            b.dma("gpsimd", wj, srcs[i],
                  reads=[f"wqa{wid}:rd{i - 2}"] if i > 1 else [],
                  writes=_wk(wk_, ok_win[i] if i < 2 else None),
                  tag=f"wq{wid}_ld{i}")
            b.op("vector", lambda e, _w=wj: (
                 e.reduce_sum(q_sb[:, 1:2], _w, AX.X, apply_absolute_value=True),
                 e.drain(),
                 e.tensor_tensor(q_sb[:, 0:1], q_sb[:, 0:1], q_sb[:, 1:2],
                                 ALU.add),
                 e.drain())[-1],
                 reads=[wk_, f"wqa{wid}:acc"],
                 writes=[f"wqa{wid}:acc", f"wqa{wid}:rd{i}"], tag=f"wq{wid}_s{i}")
        for i in range(npc):
            steps.append(lambda _i=i: p1step(_i))

        def mastep():
            b.op("tensor",
                 lambda e: e.matmul(PB[0][0:1, 0:1], q_sb[:, 0:1],
                                    onesf_sb[:, 0:1], start=True, stop=True),
                 reads=[f"wqa{wid}:acc", "onesf_sb"], writes=["PB0"],
                 tag=f"wq{wid}_tot")
            b.op("vector", lambda e: (
                 e.tensor_copy(q_sb[0:1, 1:2], PB[0][0:1, 0:1]),
                 e.drain(),
                 e.tensor_scalar(q_sb[0:1, 2:3], q_sb[0:1, 1:2], 1.0 / numel,
                                 EPS, ALU.mult, ALU.max),
                 e.drain(),
                 e.reciprocal(q_sb[0:1, 3:4], q_sb[0:1, 2:3]),
                 e.drain())[-1],
                 reads=["PB0"], writes=[f"wqa{wid}:ma", "PB0"],
                 tag=f"wq{wid}_ma")
            bcast128(q_sb[0:1, 2:3], marep, f"wqa{wid}:ma",
                     _wk(f"marep{wid}", ok_ma), f"wq{wid}_mar")
            bcast128(q_sb[0:1, 3:4], inv_ma, f"wqa{wid}:ma",
                     _wk(f"invma{wid}", ok_inv), f"wq{wid}_invr")
        steps.append(mastep)

        def p2step(i):
            wj = win[i % 2]
            wk_ = f"{win_key}{i % 2}"
            b.dma("gpsimd", wj, srcs[i],
                  reads=[f"wqb{wid}:done{i - 2}"] if i > 1
                        else [f"wqa{wid}:rd{npc - 2 + i}"],
                  writes=[wk_], tag=f"wq{wid}_ld2_{i}")
            dst_ap, dkey = dsts[i]
            for hh in range(2):
                b.op("vector",
                     lambda e, _w=wj[:, hh * 512:(hh + 1) * 512]:
                     e.tensor_scalar(i32c, _w, inv_ma[:, 0:1], None, ALU.mult),
                     reads=[wk_, f"invma{wid}", f"wqb{wid}:q"],
                     writes=_wk("i32s", ok_i32 if (i == 0 and hh == 0) else None),
                     tag=f"wq{wid}_cv{i}.{hh}")
                b.op("vector",
                     lambda e, _o=dst_ap[:, hh * 512:(hh + 1) * 512]:
                     e.tensor_scalar(_o, i32c, 1.0, -1.0, ALU.min, ALU.max),
                     reads=["i32s"],
                     writes=[dkey, f"wqb{wid}:q"] +
                            ([f"wqb{wid}:done{i}"] if hh == 1 else []),
                     tag=f"wq{wid}_cl{i}.{hh}")
        for i in range(npc):
            steps.append(lambda _i=i: p2step(_i))
        return steps




    wq1_steps = []
    wq2_steps = []
    if _PHASES >= 2:
        for i in range(NE):
            ar.free(f"xtb{i}")
        for w in ("wq", "wk", "wv"):
            for i in range(NE):
                ar.free(f"{w}sb{i}")
        ar.free("cosr")
        ar.free("sinr")
        ar.free("part0")
        ar.free("part1")

        sumsP, ok_sp = [], []
        for i in range(2):
            apx, ok = ar.alloc(f"sumsP{i}", [33, 1024], BF16)
            sumsP.append(apx)
            ok_sp.append(ok)
        recb = [ar.alloc(f"recb{i}", [128, 512], BF16)[0] for i in range(2)]
        Esl, ok_E = [], []
        for i in range(4):
            apx, ok = ar.alloc(f"Esl{i}", [128, 1024], BF16)
            Esl.append(apx)
            ok_E.append(ok)
        attnT, ok_at = [], []
        for a2 in range(NE):
            apx, ok = ar.alloc(f"attnT{a2}", [128, T], BF16)
            attnT.append(apx)
            ok_at.append(ok)
        negC, ok_nc = [], []
        for i in range(2):
            apx, ok = ar.alloc(f"negC{i}", [128, 1], F32)
            negC.append(apx)
            ok_nc.append(ok)
        sqb_sb, ok_sqb = ar.alloc("sqb_sb", [128, T], BF16)
        nrm_sb, ok_nrm = ar.alloc("nrm_sb", [1, 12], F32)

        def cprep(a):
            for which, src_t, dk, col0 in (("q", qT[a], qdone(a), 0),
                                           ("k", kT[a], kdone(a), 4)):
                b.op("vector", lambda e, _s=src_t: e.tensor_tensor(
                     sqb_sb, _s, _s, ALU.mult),
                     reads=dk + ["sqb:rd"],
                     writes=_wk("sqb", ok_sqb if (a == 0 and which == "q")
                                else None), tag=f"sq{which}{a}")
                for cch in range(4):
                    pbn = PB[cch % 2]
                    pkn = f"PB{cch % 2}"
                    b.op("tensor",
                         lambda e, _p=pbn[0:1, :],
                                _r=sqb_sb[:, cch * 512:(cch + 1) * 512]:
                         e.matmul(_p, onesb_sb, _r, start=True, stop=True),
                         reads=["sqb", "onesb_sb"], writes=[pkn],
                         tag=f"nq{which}{a}.{cch}")
                    b.op("vector",
                         lambda e, _o=nrm_sb[:, col0 + cch:col0 + cch + 1],
                                _p=pbn[0:1, :]:
                         e.reduce_max(_o, _p, AX.X),
                         reads=[pkn], writes=[f"nrm:{which}{cch}", pkn,
                                              "sqb:rd"],
                         tag=f"nr{which}{a}.{cch}")
            b.op("vector", lambda e: (
                 e.drain(),
                 e.reduce_max(nrm_sb[:, 8:9], nrm_sb[:, 0:4], AX.X),
                 e.reduce_max(nrm_sb[:, 9:10], nrm_sb[:, 4:8], AX.X),
                 e.drain(),
                 e.tensor_scalar(nrm_sb[:, 10:11], nrm_sb[:, 8:9],
                                 nrm_sb[:, 9:10], None, ALU.mult),
                 e.drain())[-1],
                 reads=[f"nrm:{w2}{cch}" for w2 in ("q", "k")
                        for cch in range(4)],
                 writes=["nrm:prod"], tag=f"cb{a}")
            b.op("scalar", lambda e: (
                 e.activation(nrm_sb[:, 11:12], nrm_sb[:, 10:11], AF.Ln),
                 e.activation(nrm_sb[:, 11:12], nrm_sb[:, 11:12], AF.Exp,
                              scale=0.5))[-1],
                 reads=["nrm:prod"], writes=["nrm:sqrt"], tag=f"cs{a}")
            b.op("vector", lambda e: (e.tensor_scalar(
                 nrm_sb[:, 10:11], nrm_sb[:, 11:12], -0.125, None, ALU.mult),
                 e.drain())[-1],
                 reads=["nrm:sqrt"], writes=["nrm:negc"], tag=f"cn{a}")
            bcast128(nrm_sb[:, 10:11], negC[a % 2], "nrm:negc",
                     _wk(f"negC{a % 2}", ok_nc[a % 2] if a < 2 else None),
                     f"crep{a}")

        cprep(0)
        for a in range(NE):
            for qh in range(2):
                kt_max = 8 * qh + 7
                sched = [("sc", 0)]
                for kt in range(1, kt_max + 1):
                    sched.append(("sc", kt))
                    sched.append(("pv", kt - 1))
                sched.append(("pv", kt_max))

                def chunk_list(kt, qh=qh):
                    out = []
                    for lc in range(2):
                        c = 2 * qh + lc
                        if 4 * c + 3 < kt:
                            continue
                        off = max(0, kt * 128 - c * 512)
                        out.append((lc, c, off, c == kt // 4))
                    return out

                for kind, kt in sched:
                    if kind == "pv" and a >= 1 and wq1_steps:
                        wq1_steps.pop(0)()
                    for h01 in range(2):
                        h = 2 * a + h01
                        slot = (2 * kt + h01) % 4
                        pbi = 2 * (kt % 2) + h01
                        pb = PB[pbi]
                        pkey = f"PB{pbi}"
                        for lc, c, off, diag in chunk_list(kt):
                            n = 512 - off
                            ek = f"Esl{slot}:c{lc}"
                            if kind == "sc":
                                if diag:
                                    b.op("tensor",
                                         lambda e, _p=pb[:, off:off + 128]:
                                         e.matmul(_p, maskt_sb, ident_sb,
                                                  start=True, stop=False),
                                         reads=["maskt_sb", "ident_sb"],
                                         writes=[pkey],
                                         tag=f"msk{a}.{qh}.{kt}.{h01}")
                                b.op("tensor",
                                     lambda e, _p=pb[:, off:512],
                                            _l=kT[a][64 * h01:64 * h01 + 64,
                                                     kt * 128:(kt + 1) * 128],
                                            _r=qT[a][64 * h01:64 * h01 + 64,
                                                     c * 512 + off:(c + 1) * 512],
                                            _st=not diag, _tp=(64 * h01, 0):
                                     e.matmul(_p, _l, _r, start=_st, stop=True,
                                              tile_position=_tp),
                                     reads=qdone(a) + kdone(a),
                                     writes=[pkey],
                                     tag=f"sc{a}.{qh}.{kt}.{h01}.{c}")
                                b.op("scalar",
                                     lambda e, _o=Esl[slot][:, lc * 512:lc * 512 + n],
                                            _i=pb[:, off:512], _b=negC[a % 2]:
                                     e.activation(_o, _i, AF.Exp, bias=_b,
                                                  scale=0.125),
                                     reads=[pkey, f"negC{a % 2}"],
                                     writes=[ek],
                                     tag=f"exp{a}.{qh}.{kt}.{h01}.{c}")
                            else:
                                st = (kt == 0)
                                sp = (kt == min(4 * c + 3, kt_max))
                                b.op("tensor",
                                     lambda e,
                                            _p=PO[0:65, h01 * 1024 + lc * 512 + off:
                                                  h01 * 1024 + (lc + 1) * 512],
                                            _l=vst[kt][:, h * 65:h * 65 + 65],
                                            _r=Esl[slot][:, lc * 512:lc * 512 + n],
                                            _st=st, _sp=sp:
                                     e.matmul(_p, _l, _r, start=_st, stop=_sp),
                                     reads=[ek, f"vst{kt}:c0", f"vst{kt}:c1",
                                            f"vst{kt}:ones"],
                                     writes=[f"PO:{h01}.{lc}"],
                                     tag=f"pv{a}.{qh}.{kt}.{h01}.{c}")
                for h01 in range(2):
                    h = 2 * a + h01
                    b.op("scalar",
                         lambda e, _o=attnT[a][64 * h01:64 * h01 + 64,
                                               qh * 1024:(qh + 1) * 1024],
                                _i=PO[0:64, h01 * 1024:(h01 + 1) * 1024]:
                         e.activation(_o, _i, AF.Identity),
                         reads=[f"PO:{h01}.0", f"PO:{h01}.1"],
                         writes=_wk(f"attnT{a}:u{h01}.{qh}",
                                    ok_at[a] if (h01 == 0 and qh == 0) else None),
                         tag=f"oev{a}.{qh}.{h01}")
                    b.op("scalar",
                         lambda e, _o=sumsP[h01][qh * 32:qh * 32 + 1, :],
                                _i=PO[64:65, h01 * 1024:(h01 + 1) * 1024]:
                         e.activation(_o, _i, AF.Identity),
                         reads=[f"PO:{h01}.0", f"PO:{h01}.1", f"sums{h01}:used"],
                         writes=_wk(f"sums:{h01}.{qh}",
                                    ok_sp[h01] if a == 0 and qh == 0 else None,
                                    f"PO:{h01}.0", f"PO:{h01}.1"),
                         tag=f"sev{a}.{qh}.{h01}")

            if a + 1 < NE:
                cprep(a + 1)
            for lc in range(4):
                qh2 = lc // 2
                col = (lc % 2) * 512
                po = PO[:, (lc % 2) * 512:(lc % 2) * 512 + 512]
                pkey = f"PO:0.{lc % 2}"
                for h01 in range(2):
                    sl = selA_sb[qh2 * 32:qh2 * 32 + 1,
                                 h01 * 128:(h01 + 1) * 128]
                    sr = sumsP[h01][qh2 * 32:qh2 * 32 + 1, col:col + 512]
                    b.op("tensor",
                         lambda e, _p=po, _l=sl, _r=sr, _st=(h01 == 0):
                         e.matmul(_p, _l, _r, start=_st, stop=(h01 == 1)),
                         reads=[f"sums:{h01}.{qh2}", "selA_sb"],
                         writes=[pkey], tag=f"rep{a}.{lc}.{h01}")
                def _recip(e, _o=recb[lc % 2], _p=po):
                    with nc.allow_low_precision(reason="softmax 1/sum bf16"):
                        e.reciprocal(_o, _p)
                    return e.drain()
                b.op("vector", _recip,
                     reads=[pkey, f"recb{lc % 2}:used"],
                     writes=[f"recb{lc % 2}", pkey], tag=f"rcp{a}.{lc}")
                b.op("vector",
                     lambda e, _o=attnT[a][:, lc * 512:(lc + 1) * 512],
                            _r=recb[lc % 2]:
                     e.tensor_tensor(_o, _o, _r, ALU.mult),
                     reads=[f"recb{lc % 2}"] + [f"attnT{a}:u{h01}.{qh}"
                                                for h01 in range(2)
                                                for qh in range(2)],
                     writes=[f"attnT{a}:n{lc}", f"recb{lc % 2}:used"] +
                            (["sums0:used", "sums1:used"] if lc == 3 else []),
                     tag=f"div{a}.{lc}")
            if a == 0:
                win1a, ok_w1a = ar.alloc("win1a", [128, 1024], F32)
                win1b, ok_w1b = ar.alloc("win1b", [128, 1024], F32)
                wq1aux, ok_q1 = ar.alloc("wq1aux", [128, 132], F32)
                ma1rep, ok_ma1 = ar.alloc("ma1rep", [128, 1], F32)
                invma1, ok_inv1 = ar.alloc("invma1", [128, 1], F32)
                i32w, ok_i32w = ar.alloc("i32w", [128, 512], I32)
                w1q, ok_w1q = [], []
                for i in range(NE):
                    apx, ok = ar.alloc(f"w1q{i}", [128, HID], FP8)
                    w1q.append(apx)
                    ok_w1q.append(ok)
                w1srcs, w1dsts = [], []
                for i in range(NE):
                    for cc in range(4):
                        w1srcs.append(w1t_d[i * 128:(i + 1) * 128,
                                            cc * 1024:(cc + 1) * 1024])
                        w1dsts.append((w1q[i][:, cc * 1024:(cc + 1) * 1024],
                                       f"w1q:{i}.{cc}"))
                wq1_steps.extend(wquant(
                    1, w1srcs, w1dsts, [win1a, win1b], "win1",
                    wq1aux, ok_q1, ma1rep, ok_ma1, invma1, ok_inv1, i32w,
                    ok_win=[ok_w1a, ok_w1b], ok_i32=ok_i32w))

        if _PHASES == 2:
            for a in range(NE):
                store_tap(f"tap_attnT{a}", attnT[a],
                          [f"attnT{a}:n{lc}" for lc in range(4)])
            store_tap("tap_sums0", sumsP[0], [f"sums:0.{qh}" for qh in range(2)])
            store_tap("tap_sums1", sumsP[1], [f"sums:1.{qh}" for qh in range(2)])
            store_tap("tap_negC0", negC[0], ["negC0"])
            store_tap("tap_negC1", negC[1], ["negC1"])
            store_tap("tap_nrm", nrm_sb, ["nrm:negc"])

    def atdone(fb):
        return [f"attnT{fb}:n{lc}" for lc in range(4)]

    if _PHASES >= 2:
        while wq1_steps:
            wq1_steps.pop(0)()

    if _PHASES >= 3:
        for a in range(NE):
            ar.free(f"qT{a}")
            ar.free(f"kT{a}")
        for t in range(NT):
            ar.free(f"vst{t}")
        for i in range(4):
            ar.free(f"Esl{i}")
        ar.free("sumsP0", keys=["sums:0", "sums0:"])
        ar.free("sumsP1", keys=["sums:1", "sums1:"])
        ar.free("recb0")
        ar.free("recb1")
        ar.free("bv_sb")
        ar.free("bq_sb")
        ar.free("bk_sb")
        ar.free("maskt_sb")
        ar.free("sqb_sb", keys=["sqb"])
        ar.free("nrm_sb", keys=["nrm"])
        ar.free("selA_sb")
        ar.free("onesb_sb")
        ar.free("negC0")
        ar.free("negC1")

        wproj_sb, ok_wp = [], []
        for i in range(NE):
            apx, ok = ar.alloc(f"wproj{i}", [128, E], BF16)
            wproj_sb.append(apx)
            ok_wp.append(ok)
        bproj_sb, ok_bp = ar.alloc("bproj_sb", [1, E], BF16)
        for i in range(NE):
            b.dma("gpsimd", wproj_sb[i], wproj_d[i * 128:(i + 1) * 128, :],
                  writes=_wk(f"wproj{i}", ok_wp[i]), tag=f"ld_wproj{i}")
        b.dma("gpsimd", bproj_sb, bproj_row_d, writes=_wk("bproj_sb", ok_bp))

        x2_t, ok_x2 = ar.alloc("x2", [128, NT * E], F32)
        x1b, ok_x1 = [], []
        for i in range(2):
            apx, ok = ar.alloc(f"x1b{i}", [128, E], F32)
            x1b.append(apx)
            ok_x1.append(ok)
        st6, ok_st6 = ar.alloc("st6", [128, 12], F32)
        wrkA, ok_ag2 = ar.alloc("wrkA", [128, 12], F32)
        lnsc, ok_lnsc = ar.alloc("lnsc", [128, 3], F32)

        xtok_r = xtok_d.rearrange("(a p) e -> p a e", p=128)
        for t in range(NT):
            b.dma("gpsimd", x1b[t % 2], xtok_r[:, t, :],
                  reads=[f"x1u:{t % 2}"],
                  writes=_wk(f"x1ld:{t % 2}", ok_x1[t % 2] if t < 2 else None),
                  tag=f"ld_x1_{t}")
            x1 = x1b[t % 2]
            for c in range(2):
                pb = PB[c]
                pkey = f"PB{c}"
                for fb in range(NE):
                    b.op("tensor",
                         lambda e, _p=pb, _l=attnT[fb][:, t * 128:(t + 1) * 128],
                                _r=wproj_sb[fb][:, c * 512:(c + 1) * 512],
                                _st=(fb == 0):
                         e.matmul(_p, _l, _r, start=_st, stop=False),
                         reads=[f"wproj{fb}"] + atdone(fb),
                         writes=[pkey] if fb == 0 else [],
                         tag=f"prj{t}.{c}.{fb}")
                b.op("tensor",
                     lambda e, _p=pb, _r=bproj_sb[:, c * 512:(c + 1) * 512]:
                     e.matmul(_p, ones_sb[0:1, :], _r, start=False, stop=True),
                     reads=["ones_sb", "bproj_sb"], writes=[pkey],
                     tag=f"prjb{t}.{c}")
                b.op("vector",
                     lambda e, _o=x1[:, c * 512:(c + 1) * 512], _p=pb:
                     e.tensor_tensor(_o, _o, _p, ALU.add),
                     reads=[pkey, f"x1ld:{t % 2}"],
                     writes=[f"x1:{t % 2}.{c}"],
                     tag=f"prjev{t}.{c}")
            _ln_group(b, "lnA", x1, st6, wrkA, lnsc, 0, 1,
                      [f"x1:{t % 2}.{c}" for c in range(2)], f"pln{t}", 2,
                      extra_stats_w=[k for k in (ok_st6, ok_ag2, ok_lnsc)
                                     if k] if t == 0 else (), eps_ap=eps_sb,
                      pcol=2)
            _ln_center(b, "lnA", x1, x2_t[:, t * E:(t + 1) * E], lnsc, 0, 2,
                       [f"x1:{t % 2}.{c}" for c in range(2)],
                       _wk(f"x2:{t}", ok_x2 if t == 0 else None,
                           f"x1u:{t % 2}"), f"pln{t}")
            if t == 0 and _PHASES >= 4:
                b.seed_pseudo("__ov_w2w0", b.accessors("win10"))
                b.seed_pseudo("__ov_w2w1", b.accessors("win11"))
                wq2aux, ok_q2 = ar.alloc("wq2aux", [128, 132], F32)
                ma2rep, ok_ma2 = ar.alloc("ma2rep", [128, 1], F32)
                invma2, ok_inv2 = ar.alloc("invma2", [128, 1], F32)
                w2q, ok_w2q = [], []
                for i in range(NH):
                    apx, ok = ar.alloc(f"w2q{i}", [128, E], FP8)
                    w2q.append(apx)
                    ok_w2q.append(ok)
                w2srcs = [w2t_d[i * 128:(i + 1) * 128, :] for i in range(NH)]
                w2dsts = [(w2q[i], f"w2q:{i}.0") for i in range(NH)]
                wq2_steps.extend(wquant(
                    2, w2srcs, w2dsts,
                    [win1a[:, 0:E], win1b[:, 0:E]], "win2",
                    wq2aux, ok_q2, ma2rep, ok_ma2, invma2, ok_inv2, i32w,
                    ok_win=["__ov_w2w0", "__ov_w2w1"], ok_i32=None))
            for _ in range(5 if t >= 1 else 0):
                if wq2_steps:
                    wq2_steps.pop(0)()

        while wq2_steps:
            wq2_steps.pop(0)()
        store_tap("tap_x2", x2_t, [f"x2:{t}" for t in range(NT)])
        store_tap("tap_ag2", wrkA, ["lnA"])
        store_tap("tap_lnsc", lnsc, ["lnA"])
        store_tap("tap_x1b", x1b[15 % 2], ["lnA", "x1:1.0", "x1:1.1"])

    if _PHASES >= 5:
        for i in range(NE):
            ar.free(f"wproj{i}")
        ar.free("win1a", keys=["win10", "win20"])
        ar.free("win1b", keys=["win11", "win21"])
        ar.free("wq1aux", keys=["wqa1", "wqb1"])
        ar.free("x1b0", keys=["x1:0", "x1ld:0", "x1u:0"])
        ar.free("x1b1", keys=["x1:1", "x1ld:1", "x1u:1"])
        for a in range(NE):
            ar.free(f"attnT{a}")
        ar.free("bproj_sb")
        ar.free("wq2aux", keys=["wqa2", "wqb2"])
        ar.free("st6", keys=["lnA"])
        ar.free("wrkA", keys=["lnA"])
        ar.free("lnsc", keys=["lnA"])
        ar.free("onesf_sb")

        hact_b, ok_hact = [], []
        for i in range(2):
            apx, ok = ar.alloc(f"hact{i}", [128, HID], F32)
            hact_b.append(apx)
            ok_hact.append(ok)
        ln2w_sb, ok_l2w = ar.alloc("ln2w_sb", [128, HID], BF16)
        ln2b_sb, ok_l2b = ar.alloc("ln2b_sb", [128, HID], BF16)
        qhT, ok_qhT = ar.alloc("qhT", [128, HID], BF16)
        y1b, ok_y1 = ar.alloc("y1b", [128, E], F32)
        t3, ok_t3 = y1b, None
        ln1w_sb, ok_l1w = ar.alloc("ln1w_sb", [128, E], BF16)
        ln1b_sb, ok_l1b = ar.alloc("ln1b_sb", [128, E], BF16)
        x2qT_b, ok_x2qT = [], []
        for i in range(2):
            apx, ok = ar.alloc(f"x2qT{i}", [128, E], BF16)
            x2qT_b.append(apx)
            ok_x2qT.append(ok)
        qbuf_b, ok_qbuf_b = [], []
        for i in range(2):
            apx, ok = ar.alloc(f"qbuf{i}", [128, 512], BF16)
            qbuf_b.append(apx)
            ok_qbuf_b.append(ok)
        b1_sb, ok_b1 = ar.alloc("b1_sb", [65, 2048], BF16)
        st6f, ok_st6f = ar.alloc("st6f", [128, 48], F32)
        wrkB, ok_ag2f = ar.alloc("wrkB", [128, 48], F32)
        lnscf, ok_lnscf = ar.alloc("lnscf", [128, 13], F32)
        i32a, ok_i32a = i32w, None
        st6_l2, st6_l1, st6_f = st6f, st6f[:, 0:12], st6f[:, 0:12]
        wrk_l2, wrk_l1, wrk_f = wrkB[:, 0:24], wrkB[:, 24:36], wrkB[:, 36:48]

        b.dma("gpsimd", ln1w_sb, ln1w_d, writes=_wk("ln1w_sb", ok_l1w))
        b.dma("gpsimd", ln1b_sb, ln1b_d, writes=_wk("ln1b_sb", ok_l1b))
        b.dma("gpsimd", ln2w_sb, ln2w_d, writes=_wk("ln2w_sb", ok_l2w))
        b.dma("gpsimd", ln2b_sb, ln2b_d, writes=_wk("ln2b_sb", ok_l2b))
        b.dma("gpsimd", b1_sb[0:1, :], b1_row_d[:, 0:2048],
              writes=_wk("b1_sb:raw", ok_b1))
        b.dma("gpsimd", b1_sb[32:33, :], b1_row_d[:, 2048:4096],
              writes=["b1_sb:raw2"])
        b.dma("gpsimd", b1_sb[64:65, 0:1024], b2_row_d[:, 0:1024],
              writes=["b2_sb:raw"])
        b.op("vector", lambda e: e.tensor_scalar(
                 b1_sb[0:33, :], b1_sb[0:33, :], invma1[0:33, 0:1],
                 None, ALU.mult),
             reads=["b1_sb:raw", "b1_sb:raw2", "invma1"], writes=["b1_sb"],
             tag="b1s")
        b.op("vector", lambda e: e.tensor_scalar(
                 b1_sb[64:65, 0:1024], b1_sb[64:65, 0:1024],
                 invma2[64:65, 0:1], None, ALU.mult),
             reads=["b2_sb:raw", "invma2"], writes=["b2_sb"],
             tag="b2s")

        b.seed_pseudo("PO:m0", b.accessors("PO"))
        b.seed_pseudo("PO:m1", b.accessors("PO"))
        b.seed_pseudo("PT0", b.accessors("PB2"))
        b.seed_pseudo("PT1", b.accessors("PB3"))
        _lnB_deps = [op for k in (ok_st6f, ok_ag2f, ok_lnscf) if k
                     for op in b.res.get(k, [None, []])[1]]
        for _g in ("lnL1", "lnL2", "lnF"):
            b.seed_pseudo(_g, _lnB_deps)

        okd = {"t3": ok_t3, "qbuf0": ok_qbuf_b[0], "qbuf1": ok_qbuf_b[1],
               "qhT": ok_qhT, "y1b": ok_y1}
        used = set()

        def fk2(name):
            ok = okd.get(name)
            if ok and name not in used:
                used.add(name)
                return ok
            return None

        def okbuf(oklist, i, nm):
            if oklist[i] and nm not in used:
                used.add(nm)
                return oklist[i]
            return None

        def quant_scales(t, src, src_keys, grp, c_dq, c_q, qkey, tag):
            b.op("vector", lambda e: (
                 e.reduce_max(lnscf[:, c_dq:c_dq + 1], src, AX.X,
                              apply_absolute_value=True),
                 e.drain(),
                 e.reciprocal(lnscf[:, c_q:c_q + 1], lnscf[:, c_dq:c_dq + 1]),
                 e.tensor_scalar(lnscf[:, c_dq:c_dq + 1],
                                 lnscf[:, c_dq:c_dq + 1], 1.0 / 127.0,
                                 None, ALU.mult),
                 e.tensor_scalar(lnscf[:, c_q:c_q + 1],
                                 lnscf[:, c_q:c_q + 1], 127.0, None,
                                 ALU.mult))[-1],
                 reads=list(src_keys) + [grp], writes=[qkey, grp], tag=tag)

        def quant_cvdq(src, ch, c_dq, c_q, qkey, src_keys, trtag, grp):
            b.op("scalar",
                 lambda e, _i=src[:, ch * 512:(ch + 1) * 512]:
                 e.activation(i32a, _i, AF.Identity,
                              scale=lnscf[:, c_q:c_q + 1]),
                 reads=list(src_keys) + [qkey, "i32s", grp], writes=["i32s"],
                 tag=f"{trtag}cv.{ch}")
            qb = qbuf_b[ch % 2]
            qbk = f"qbuf{ch % 2}"
            b.op("scalar",
                 lambda e, _o=qb: e.activation(_o, i32a, AF.Identity,
                                               scale=lnscf[:, c_dq:c_dq + 1]),
                 reads=["i32s", qkey, qbk, grp],
                 writes=_wk(qbk, fk2(qbk)), tag=f"{trtag}dq.{ch}")

        def quant_trs(dst, dkey, ok_dst, ch, trtag):
            qb = qbuf_b[ch % 2]
            qbk = f"qbuf{ch % 2}"
            for b4 in range(4):
                bb = ch * 4 + b4
                pt = PT[bb % 2]
                ptk = f"PT{bb % 2}"
                b.op("tensor",
                     lambda e, _p=pt, _i=qb[:, b4 * 128:(b4 + 1) * 128]:
                     e.transpose(_p, _i, ident_sb),
                     reads=[qbk, "ident_sb", ptk + ":rd"],
                     writes=[ptk], tag=f"{trtag}_{bb}")
                b.op("scalar",
                     lambda e, _o=dst[:, bb * 128:(bb + 1) * 128], _p=pt:
                     e.activation(_o, _p, AF.Identity),
                     reads=[ptk, dkey + ":mm"],
                     writes=_wk(dkey, ok_dst, ptk + ":rd"),
                     tag=f"{trtag}e.{bb}")

        def quant_tr(t, src, src_keys, grp, c_dq, c_q, qkey, dst, dkey,
                     ok_dst, nch, trtag, inter=None):
            quant_cvdq(src, 0, c_dq, c_q, qkey, src_keys, trtag, grp)
            for ch in range(nch):
                quant_trs(dst, dkey, ok_dst if ch == 0 else None, ch, trtag)
                if ch + 1 < nch:
                    quant_cvdq(src, ch + 1, c_dq, c_q, qkey, src_keys, trtag,
                               grp)
                if inter is not None:
                    inter(ch)

        def ln1_q1_tr1(t):
            x2sl = x2_t[:, t * E:(t + 1) * E]
            x2qTt = x2qT_b[t % 2]
            _ln_group(b, "lnL1", x2sl, st6_l1, wrk_l1, lnscf, 0, 1,
                      [f"x2:{t}"], f"l1n{t}", 2, eps_ap=eps_sb, pcol=10)
            _ln_center(b, "lnL1", x2sl, t3, lnscf, 0, 10,
                       [f"x2:{t}", "y1:rd"],
                       _wk("t3", fk2("y1b"), "y1:0", "y1:1"), f"l1n{t}")
            b.op("vector", lambda e: e.tensor_tensor(t3, t3, ln1w_sb, ALU.mult),
                 reads=["t3", "ln1w_sb"], writes=["t3"], tag=f"l1w{t}")
            b.op("vector", lambda e: e.tensor_tensor(t3, t3, ln1b_sb, ALU.add),
                 reads=["t3", "ln1b_sb"], writes=["t3"], tag=f"l1b{t}")
            quant_scales(t, t3, ["t3"], "lnL1", 2, 3, f"q1s{t}", f"l1qs{t}")
            quant_cvdq(t3, 0, 2, 3, f"q1s{t}", ["t3"], f"tr1_{t}", "lnL1")
            quant_cvdq(t3, 1, 2, 3, f"q1s{t}", ["t3"], f"tr1_{t}", "lnL1")

        def ln1_tr(t):
            x2qTt = x2qT_b[t % 2]
            ok = okbuf(ok_x2qT, t % 2, f"x2qT{t % 2}")
            quant_trs(x2qTt, f"x2qT{t % 2}", ok, 0, f"tr1_{t}")
            quant_trs(x2qTt, f"x2qT{t % 2}", None, 1, f"tr1_{t}")

        def mm1_mm(t, n):
            x2qTt = x2qT_b[t % 2]
            pb = PB[n % 2]
            pkey = f"PB{n % 2}"
            for kb in range(NE):
                b.op("tensor",
                     lambda e, _p=pb, _l=x2qTt[:, kb * 128:(kb + 1) * 128],
                            _r=w1q[kb][:, n * 512:(n + 1) * 512],
                            _st=(kb == 0):
                     e.matmul(_p, _l, _r, start=_st, stop=False),
                     reads=[f"x2qT{t % 2}"] +
                           [f"w1q:{kb}.{cc}" for cc in range(4)],
                     writes=[pkey] if kb == 0 else [],
                     tag=f"mm1_{t}.{n}.{kb}")
            b1r = (b1_sb[0:1, n * 512:(n + 1) * 512] if n < 4 else
                   b1_sb[32:33, (n - 4) * 512:(n - 3) * 512])
            onesr = ones_sb[0:1, :] if n < 4 else ones_sb[32:33, :]
            b.op("tensor",
                 lambda e, _p=pb, _r=b1r, _l=onesr:
                 e.matmul(_p, _l, _r, start=False, stop=True),
                 reads=["ones_sb", "b1_sb"], writes=[pkey],
                 tag=f"mm1b_{t}.{n}")

        def mm1_gel(t, n):
            hact = hact_b[t % 2]
            pb = PB[n % 2]
            pkey = f"PB{n % 2}"
            b.op("scalar",
                 lambda e, _o=hact[:, n * 512:(n + 1) * 512], _p=pb:
                 e.activation(_o, _p, AF.Gelu, scale=ma1rep[:, 0:1]),
                 reads=[pkey, "marep1", f"hact{t % 2}:rd"],
                 writes=_wk(f"hact{t % 2}:{n}",
                            okbuf(ok_hact, t % 2, f"hact{t % 2}"),
                            f"hact{t % 2}:n"),
                 tag=f"gel{t}.{n}")

        def mm1_block(t, n):
            mm1_mm(t, n)
            mm1_gel(t, n)

        def mm1_gelu(t):
            for n in range(8):
                mm1_block(t, n)

        def ln2_part1(t):
            hact = hact_b[t % 2]
            hreads = [f"hact{t % 2}:{n}" for n in range(8)]
            _ln_group(b, "lnL2", hact, st6_l2, wrk_l2, lnscf, 6, 7,
                      hreads, f"l2n{t}", 8, eps_ap=eps_sb, pcol=12)

        def p2_head(t):
            hact = hact_b[t % 2]
            hreads = [f"hact{t % 2}:{n}" for n in range(8)]
            _ln_center(b, "lnL2", hact, hact, lnscf, 6, 12, hreads,
                       [f"hact{t % 2}:n"], f"l2n{t}")
            b.op("vector", lambda e, _h=hact:
                 e.tensor_tensor(_h, _h, ln2w_sb, ALU.mult),
                 reads=[f"hact{t % 2}:n", "ln2w_sb"], writes=[f"hact{t % 2}:n"],
                 tag=f"l2w{t}")
            b.op("vector", lambda e, _h=hact:
                 e.tensor_tensor(_h, _h, ln2b_sb, ALU.add),
                 reads=[f"hact{t % 2}:n", "ln2b_sb"],
                 writes=[f"hact{t % 2}:n", f"hact{t % 2}:rd"],
                 tag=f"l2b{t}")
            quant_scales(t, hact, [f"hact{t % 2}:n"], "lnL2", 8, 9,
                         f"q2s{t}", f"l2qs{t}")
            quant_cvdq(hact, 0, 8, 9, f"q2s{t}", [f"hact{t % 2}:n"],
                       f"tr2_{t}", "lnL2")
            quant_cvdq(hact, 1, 8, 9, f"q2s{t}", [f"hact{t % 2}:n"],
                       f"tr2_{t}", "lnL2")

        def p2_chunks(t):
            hact = hact_b[t % 2]
            ok = fk2("qhT")
            for ch in range(8):
                quant_trs(qhT, "qhT", ok if ch == 0 else None, ch, f"tr2_{t}")
                if ch + 2 < 8:
                    quant_cvdq(hact, ch + 2, 8, 9, f"q2s{t}",
                               [f"hact{t % 2}:n"], f"tr2_{t}", "lnL2")
                if t + 2 < NT:
                    mm1_block(t + 2, ch)

        def mm2_out(t):
            x2sl = x2_t[:, t * E:(t + 1) * E]
            for kb in range(NH):
                for c in range(2):
                    po = PO[:, c * 512:(c + 1) * 512]
                    pkey = f"PO:m{c}"
                    b.op("tensor",
                         lambda e, _p=po, _l=qhT[:, kb * 128:(kb + 1) * 128],
                                _r=w2q[kb][:, c * 512:(c + 1) * 512],
                                _st=(kb == 0):
                         e.matmul(_p, _l, _r, start=_st, stop=False),
                         reads=["qhT", f"w2q:{kb}.0"],
                         writes=[pkey] if kb == 0 else [],
                         tag=f"mm2_{t}.{c}.{kb}")
            for c in range(2):
                po = PO[:, c * 512:(c + 1) * 512]
                pkey = f"PO:m{c}"
                b2r = b1_sb[64:65, c * 512:(c + 1) * 512]
                onesr2 = ones_sb[64:65, :]
                b.op("tensor",
                     lambda e, _p=po, _r=b2r, _l=onesr2:
                     e.matmul(_p, _l, _r, start=False, stop=True),
                     reads=["ones_sb", "b2_sb"], writes=[pkey],
                     tag=f"mm2b_{t}.{c}")
                b.op("vector",
                     lambda e, _o=y1b[:, c * 512:(c + 1) * 512], _p=po,
                            _x=x2sl[:, c * 512:(c + 1) * 512]:
                     e.scalar_tensor_tensor(_o, _p, ma2rep[:, 0:1], _x,
                                            ALU.mult, ALU.add),
                     reads=[pkey, "marep2", f"x2:{t}", "y1:rd"],
                     writes=[f"y1:{c}", "t3"],
                     tag=f"m2ev{t}.{c}")
            _ln_group(b, "lnF", y1b, st6_f, wrk_f, lnscf, 4, 5,
                      [f"y1:{c}" for c in range(2)], f"fln{t}", 2,
                      eps_ap=eps_sb, pcol=11)
            _ln_center(b, "lnF", y1b, y1b, lnscf, 4, 11,
                       [f"y1:{c}" for c in range(2)],
                       _wk(f"ob:{t}", None, "y1:rd"), f"fln{t}")
            b.dma("sync", y_d[t * 128:(t + 1) * 128, :], y1b,
                  reads=[f"ob:{t}"], writes=["y1:rd"], tag=f"st_y{t}")

        if _P5_SEQ:
            for t in range(NT):
                ln1_q1_tr1(t)
                ln1_tr(t)
                mm1_gelu(t)
                ln2_part1(t)
                p2_head(t)
                p2_chunks(t)
                mm2_out(t)
        else:
            ln1_q1_tr1(0)
            ln1_tr(0)
            mm1_gelu(0)
            ln1_q1_tr1(1)
            ln1_tr(1)
            ln2_part1(0)
            mm1_gelu(1)
            ln1_q1_tr1(2)
            ln1_tr(2)
            p2_head(0)
            for t in range(NT):
                p2_chunks(t)
                if t + 3 < NT:
                    ln1_q1_tr1(t + 3)
                    ln1_tr(t + 3)
                if t + 1 < NT:
                    ln2_part1(t + 1)
                    p2_head(t + 1)
                mm2_out(t)

        store_tap("tap_qhT0", qhT, ["qhT"])

    def fin(eng):
        for q in ("sync", "gpsimd"):
            st = b.dma_state[q]
            for i, tot in enumerate(st["tot"]):
                if tot > 0:
                    eng.wait_ge(b.dma_pools[q][i], tot)
        return None

    b.op("sync", fin, reads=[], writes=[], tag="final_wait")

    b.finalize_and_emit()
    return nc


def _ln_group(b, grp, src, st6, wrk, lnsc, rcol, mcol, reads, tag, nchunk,
              extra_stats_w=(), eps_ap=None, pcol=None):
    g = 2 * nchunk
    W = 512 * nchunk

    def fn(e):
        for i in range(nchunk):
            e.bn_stats(st6[:, 6 * i:6 * i + 6], src[:, 512 * i:512 * (i + 1)])
        e.drain()
        tr = st6[:, 0:6 * nchunk].rearrange("p (a c) -> p a c", c=3)
        e.reduce_sum(wrk[:, 0:1], tr[:, :, 1], AX.X)
        e.tensor_tensor(wrk[:, 8:8 + g], tr[:, :, 1], tr[:, :, 1], ALU.mult)
        e.reduce_sum(wrk[:, 2:3], tr[:, :, 2], AX.X)
        e.reduce_sum(wrk[:, 1:2], wrk[:, 8:8 + g], AX.X)
        e.tensor_scalar(wrk[:, 3:4], wrk[:, 0:1], wrk[:, 0:1],
                        1.0 / (g * g), ALU.mult, ALU.mult)
        e.scalar_tensor_tensor(wrk[:, 4:5], wrk[:, 1:2], 256.0, wrk[:, 2:3],
                               ALU.mult, ALU.add)
        e.tensor_scalar(lnsc[:, mcol:mcol + 1], wrk[:, 0:1], -1.0 / g, None,
                        ALU.mult)
        return e.scalar_tensor_tensor(wrk[:, 5:6], wrk[:, 4:5], 1.0 / W,
                                      wrk[:, 3:4], ALU.mult, ALU.subtract)
    b.op("vector", fn, reads=list(reads) + [grp],
         writes=[grp, f"{tag}:st", f"{tag}:mu"] + list(extra_stats_w),
         tag=f"{tag}_st")

    def fsc(e):
        e.activation(lnsc[:, rcol:rcol + 1], wrk[:, 5:6], AF.Ln, bias=eps_ap)
        r = e.activation(lnsc[:, rcol:rcol + 1], lnsc[:, rcol:rcol + 1],
                         AF.Exp, scale=-0.5)
        if pcol is not None:
            e.activation(lnsc[:, pcol:pcol + 1], lnsc[:, mcol:mcol + 1],
                         AF.Identity, scale=lnsc[:, rcol:rcol + 1])
            e.activation(wrk[:, 6:7], lnsc[:, mcol:mcol + 1], AF.Identity)
            r = e.activation(wrk[:, 7:8], lnsc[:, mcol:mcol + 1], AF.Identity)
        return r
    b.op("scalar", fsc,
         reads=[f"{tag}:st", f"{tag}:mu", grp, "eps_sb"],
         writes=[grp, f"{tag}:sc"], tag=f"{tag}_sc")


def _ln_center(b, grp, src, out_sl, lnsc, rcol, pcol, reads, wkeys, tag):
    b.op("scalar", lambda e:
         e.activation(out_sl, src, AF.Identity,
                      scale=lnsc[:, rcol:rcol + 1],
                      bias=lnsc[:, pcol:pcol + 1]),
         reads=[f"{tag}:sc", grp] + list(reads),
         writes=[grp] + list(wkeys), tag=f"{tag}_ap")


def kernel(**inputs):
    in_maps = _host_prep(inputs)
    nc = build_program()
    res = run_bass_kernel_spmd(nc, in_maps, core_ids=list(range(N_CORES)))
    outs = [r["y"] for r in res.results]
    out = np.stack(outs, 0).astype(np.float32)
    if _DEBUG_TAPS:
        kernel._tap_results = res.results
    return out


# version 18
# speedup vs baseline: 1.0295x; 1.0295x over previous
import numpy as np
import ml_dtypes
import concourse.bass as bass
import concourse.mybir as mybir
from concourse.bass_utils import run_bass_kernel_spmd

F32 = mybir.dt.float32
BF16 = mybir.dt.bfloat16
FP8 = mybir.dt.float8e4
I32 = mybir.dt.int32
AF = mybir.ActivationFunctionType
ALU = mybir.AluOpType
AX = mybir.AxisListType

N_CORES = 8
T = 2048
E = 1024
H = 16
D = 64
HID = 4096
NT = T // 128
NE = E // 128
NH = HID // 128
EPS = 1e-5

_DEBUG_TAPS = []
_PHASES = 99
_P5_SEQ = False
_ROPE = True

ENGINES = ("tensor", "vector", "scalar", "gpsimd", "sync")


class _Op:
    __slots__ = ("eng", "fn", "reads", "writes", "gidx", "eidx", "dma_sem",
                 "dma_thresh", "marked", "waits", "tag")

    def __init__(self, eng, fn, reads, writes, gidx, tag=""):
        self.eng = eng
        self.fn = fn
        self.reads = reads
        self.writes = writes
        self.gidx = gidx
        self.eidx = -1
        self.dma_sem = None
        self.dma_thresh = 0
        self.marked = -1
        self.waits = []
        self.tag = tag


class Builder:
    def __init__(self, nc):
        self.nc = nc
        self.ops = []
        self.eops = {e: [] for e in ENGINES}
        self.res = {}
        self.prog_sems = {e: nc.alloc_semaphore(name=f"prog_{e}") for e in ENGINES}
        self.dma_pools = {}
        self.dma_state = {}
        for q in ("gpsimd", "sync"):
            self.dma_pools[q] = [nc.alloc_semaphore(name=f"dma_{q}_{i}") for i in range(8)]
            self.dma_state[q] = {"i": 0, "tot": [0] * 8}
        self._deps_of = {}

    def op(self, eng, fn, reads=(), writes=(), tag=""):
        o = _Op(eng, fn, list(reads), list(writes), len(self.ops), tag)
        self._track(o)
        return o

    def dma(self, queue, out_ap, in_ap, reads=(), writes=(), tag="", transpose=False):
        pool = self.dma_pools[queue]
        st = self.dma_state[queue]
        slot = st["i"] % 8
        st["i"] += 1
        sem = pool[slot]
        prior = st["tot"][slot]
        st["tot"][slot] = prior + 16
        thresh = st["tot"][slot]

        def fn(eng, _o=out_ap, _i=in_ap, _t=transpose):
            if _t:
                return eng.dma_start_transpose(_o, _i)
            return eng.dma_start(_o, _i)

        o = _Op(queue, fn, list(reads), list(writes), len(self.ops), tag)
        o.dma_sem = sem
        o.dma_thresh = thresh
        if prior > 0:
            o.waits.append((sem, prior))
        self._track(o)
        return o

    def _track(self, o):
        deps = set()
        for r in o.reads:
            ent = self.res.get(r)
            if ent is not None and ent[0] is not None:
                deps.add(ent[0])
        for w in o.writes:
            ent = self.res.get(w)
            if ent is not None:
                if ent[0] is not None:
                    deps.add(ent[0])
                deps.update(ent[1])
        deps = {d for d in deps if d.eng != o.eng or d.dma_sem is not None}
        self._deps_of[o] = deps
        self.ops.append(o)
        self.eops[o.eng].append(o)
        o.eidx = len(self.eops[o.eng]) - 1
        for r in o.reads:
            self.res.setdefault(r, [None, []])[1].append(o)
        for w in o.writes:
            self.res[w] = [o, []]

    def accessors(self, name):
        out = []
        for k, (w, rs) in self.res.items():
            if k == name or k.startswith(name):
                if w is not None:
                    out.append(w)
                out.extend(rs)
        return out

    def seed_pseudo(self, key, dep_ops):
        if dep_ops:
            self.res[key] = [None, list(dep_ops)]

    def finalize_and_emit(self):
        nc = self.nc
        for o in self.ops:
            for d in self._deps_of[o]:
                if d.dma_sem is None and d.marked < 0:
                    d.marked = 0
        for e in ENGINES:
            n = 0
            for o in self.eops[e]:
                if o.dma_sem is None and o.marked == 0:
                    n += 1
                    o.marked = n
        for o in self.ops:
            need = {}
            for d in self._deps_of[o]:
                if d.dma_sem is not None:
                    key = ("dma", id(d.dma_sem))
                    sem, val = d.dma_sem, d.dma_thresh
                else:
                    key = ("eng", d.eng)
                    sem, val = self.prog_sems[d.eng], d.marked
                    assert d.marked > 0
                if key not in need or need[key][1] < val:
                    need[key] = (sem, val)
            o.waits.extend(need.values())

        waited = {e: {} for e in ENGINES}

        def run_engine(ename):
            def body(eng):
                wd = waited[ename]
                for o in self.eops[ename]:
                    for sem, val in o.waits:
                        k = id(sem)
                        if wd.get(k, -1) >= val:
                            continue
                        eng.wait_ge(sem, val)
                        wd[k] = val
                    inst = o.fn(eng)
                    if o.dma_sem is not None:
                        inst.then_inc(o.dma_sem, 16)
                    elif o.marked > 0:
                        if inst is None:
                            eng.sem_inc(self.prog_sems[ename], 1)
                        else:
                            inst.then_inc(self.prog_sems[ename], 1)
            return body

        with nc.Block() as blk:
            blk.tensor(run_engine("tensor"))
            blk.vector(run_engine("vector"))
            blk.scalar(run_engine("scalar"))
            blk.gpsimd(run_engine("gpsimd"))
            blk.sync(run_engine("sync"))


class Arena:
    def __init__(self, nc, builder):
        self.nc = nc
        self.b = builder
        self.base = (nc.sbuf_base + 31) & ~31
        self.cur = 0
        self.limit = nc.sbuf_top - self.base
        self.tensors = {}
        self.free_list = []
        self.peak = 0

    def alloc(self, name, shape, dtype):
        bpp = int(np.prod(shape[1:])) * mybir.dt.size(dtype)
        bpp = (bpp + 31) & ~31
        off = None
        names = set()
        for i, (fo, fs, nm) in enumerate(self.free_list):
            if fs >= bpp:
                off = fo
                names = nm
                if fs == bpp:
                    self.free_list.pop(i)
                else:
                    self.free_list[i] = (fo + bpp, fs - bpp, set(nm))
                break
        if off is None:
            off = self.cur
            self.cur += bpp
            self.peak = max(self.peak, self.cur)
            assert self.cur <= self.limit, (
                f"sbuf overflow at {name}: {self.cur} > {self.limit}")
        h = self.nc.alloc_sbuf_tensor_at(name, list(shape), dtype,
                                         offset=self.base + off)
        self.tensors[name] = (h, off, bpp)
        okey = None
        if names:
            dep_ops = []
            for nm in names:
                dep_ops.extend(self.b.accessors(nm))
            if dep_ops:
                okey = f"__ov_{name}"
                self.b.seed_pseudo(okey, dep_ops)
        return h.ap(), okey

    def free(self, name, keys=None):
        h, off, bpp = self.tensors.pop(name)
        self.free_list.append((off, bpp, set(keys) if keys else {name}))
        self.free_list.sort(key=lambda x: x[0])
        merged = []
        for fo, fs, nm in self.free_list:
            if merged and merged[-1][0] + merged[-1][1] == fo:
                po, ps, pn = merged[-1]
                merged[-1] = (po, ps + fs, pn | nm)
            else:
                merged.append((fo, fs, set(nm)))
        if merged and merged[-1][0] + merged[-1][1] == self.cur:
            pass
        self.free_list = merged


def _wk(name, okey, *extra):
    out = [name] + list(extra)
    if okey is not None:
        out.append(okey)
    return out



def _host_prep(inputs):
    x = np.ascontiguousarray(inputs["L0_attn_x"])
    Wqkv = np.asarray(inputs["L0_attn_Wqkv"])
    bqkv = np.asarray(inputs["L0_attn_bqkv"])
    Wproj = np.ascontiguousarray(inputs["L0_attn_Wproj"])
    bproj = np.asarray(inputs["L0_attn_bproj"])

    perm = np.concatenate([np.arange(h * D, (h + 1) * D).reshape(D // 2, 2).T.ravel()
                           for h in range(H)])
    Wq = np.ascontiguousarray(Wqkv[:, 0:E][:, perm])
    Wk = np.ascontiguousarray(Wqkv[:, E:2 * E][:, perm])
    Wv = np.ascontiguousarray(Wqkv[:, 2 * E:3 * E])
    bq = np.ascontiguousarray(bqkv[0:E][perm].reshape(NE, 128).T)
    bk = np.ascontiguousarray(bqkv[E:2 * E][perm].reshape(NE, 128).T)
    bv_row = np.ascontiguousarray(bqkv[2 * E:3 * E].reshape(1, E))
    bproj_row = np.ascontiguousarray(bproj.reshape(1, E))

    pos = np.arange(T, dtype=np.float64)
    inv = 10000.0 ** (-np.arange(0, D, 2, dtype=np.float64) / D)
    fr = pos[:, None] * inv[None, :]
    cos = np.cos(fr).astype(np.float32).T
    sin = np.sin(fr).astype(np.float32).T
    cosrep = np.ascontiguousarray(np.tile(cos, (4, 1)))
    sinpm = np.ascontiguousarray(np.tile(np.concatenate([-sin, sin], 0), (2, 1)))

    km = np.arange(128)[:, None]
    qm = np.arange(128)[None, :]
    M = np.where(km > qm, -1e30, 0.0).astype(np.float32)
    maskT = np.ascontiguousarray(M.T)
    ident = np.eye(128, dtype=np.float32)
    perm32 = np.zeros((128, 128), np.float32)
    perm32[np.arange(128) ^ 32, np.arange(128)] = 1.0
    ones_col = np.ones((65, 128), np.float32)
    epsrep = np.full((128, 1), 1e-5, np.float32)
    onesf = np.ones((128, 128), np.float32)
    onesb = np.ones((128, 1), np.float32)
    selA = np.zeros((33, 256), np.float32)
    selA[0, :64] = 1.0
    selA[0, 192:] = 1.0
    selA[32] = selA[0]

    w1 = np.asarray(inputs["L0_ffn_w1"])
    w2 = np.asarray(inputs["L0_ffn_w2"])

    shared = {
        "wq": Wq, "wk": Wk, "wv": Wv, "bq": bq, "bk": bk,
        "bv_row": bv_row, "wproj": Wproj, "bproj_row": bproj_row,
        "cosrep": cosrep, "sinpm": sinpm, "maskt": maskT, "ident": ident, "perm32": perm32,
        "ones_col": ones_col, "selA": selA, "epsrep": epsrep, "onesf": onesf, "onesb": onesb,
        "w1t": np.ascontiguousarray(w1.T),
        "w2t": np.ascontiguousarray(w2.T),
        "b1_row": np.ascontiguousarray(np.asarray(inputs["L0_ffn_b1"]).reshape(1, HID)),
        "b2_row": np.ascontiguousarray(np.asarray(inputs["L0_ffn_b2"]).reshape(1, E)),
        "ln1w_rep": np.ascontiguousarray(
            np.broadcast_to(np.asarray(inputs["L0_ffn_ln1_w"]), (128, E))),
        "ln1b_rep": np.ascontiguousarray(
            np.broadcast_to(np.asarray(inputs["L0_ffn_ln1_b"]), (128, E))),
        "ln2w_rep": np.ascontiguousarray(
            np.broadcast_to(np.asarray(inputs["L0_ffn_ln2_w"]), (128, HID))),
        "ln2b_rep": np.ascontiguousarray(
            np.broadcast_to(np.asarray(inputs["L0_ffn_ln2_b"]), (128, HID))),
    }
    in_maps = []
    for c in range(N_CORES):
        m = dict(shared)
        m["xt"] = np.ascontiguousarray(x[c].T)
        m["xtok"] = np.ascontiguousarray(x[c])
        in_maps.append(m)
    return in_maps


def build_program():
    nc = bass.Bass("TRN2", target_bir_lowering=False,
                   detect_race_conditions=False)
    b = Builder(nc)
    ar = Arena(nc, b)

    def din(name, shape, dtype=F32):
        return nc.dram_tensor(name, list(shape), dtype, kind="ExternalInput").ap()

    xt_d = din("xt", (E, T))
    xtok_d = din("xtok", (T, E))
    wq_d, wk_d, wv_d = din("wq", (E, E)), din("wk", (E, E)), din("wv", (E, E))
    bq_d, bk_d = din("bq", (128, NE)), din("bk", (128, NE))
    bv_row_d = din("bv_row", (1, E))
    wproj_d = din("wproj", (E, E))
    bproj_row_d = din("bproj_row", (1, E))
    cos_d, sin_d = din("cosrep", (128, T)), din("sinpm", (128, T))
    maskt_d = din("maskt", (128, 128))
    ident_d = din("ident", (128, 128))
    perm32_d = din("perm32", (128, 128))
    ones_col_d = din("ones_col", (65, 128))
    eps_d = din("epsrep", (128, 1))
    onesf_d = din("onesf", (128, 128))
    onesb_d = din("onesb", (128, 1))
    selA_d = din("selA", (33, 256))
    w1t_d = din("w1t", (E, HID))
    w2t_d = din("w2t", (HID, E))
    b1_row_d = din("b1_row", (1, HID))
    b2_row_d = din("b2_row", (1, E))
    ln1w_d, ln1b_d = din("ln1w_rep", (128, E)), din("ln1b_rep", (128, E))
    ln2w_d, ln2b_d = din("ln2w_rep", (128, HID)), din("ln2b_rep", (128, HID))

    y_d = nc.dram_tensor("y", [T, E], F32, kind="ExternalOutput").ap()
    taps = {}
    for tname, shape, dt in _DEBUG_TAPS:
        taps[tname] = nc.dram_tensor(tname, list(shape), dt,
                                     kind="ExternalOutput").ap()

    def store_tap(name, src_ap, reads):
        if name in taps:
            b.dma("sync", taps[name], src_ap, reads=reads, tag=f"tap_{name}")

    PO = nc.alloc_psum_tensor("PO", [128, 2048], F32).ap()
    PB = [nc.alloc_psum_tensor(f"PB{i}", [128, 512], F32).ap() for i in range(4)]
    save_base = nc.psum_base
    nc.psum_base = save_base - 2
    PT = [nc.alloc_psum_tensor(f"PT{i}", [128, 128], BF16).ap() for i in range(2)]
    nc.psum_base = save_base

    xtb = [ar.alloc(f"xtb{i}", [128, T], BF16)[0] for i in range(NE)]
    wsb = {w: [ar.alloc(f"{w}sb{i}", [128, E], BF16)[0] for i in range(NE)]
           for w in ("wq", "wk", "wv")}
    qT = [ar.alloc(f"qT{i}", [128, T], BF16)[0] for i in range(NE)]
    kT = [ar.alloc(f"kT{i}", [128, T], BF16)[0] for i in range(NE)]
    vst = [ar.alloc(f"vst{t}", [128, H * (D + 1)], BF16)[0] for t in range(NT)]
    cosr = ar.alloc("cosr", [128, T], BF16)[0]
    sinr = ar.alloc("sinr", [128, T], BF16)[0]
    part = [ar.alloc(f"part{i}", [128, T], BF16)[0] for i in range(2)]
    bq_sb = ar.alloc("bq_sb", [128, NE], F32)[0]
    bk_sb = ar.alloc("bk_sb", [128, NE], F32)[0]
    bv_sb = ar.alloc("bv_sb", [1, E], BF16)[0]
    ones_sb = ar.alloc("ones_sb", [65, 128], BF16)[0]
    eps_sb = ar.alloc("eps_sb", [128, 1], F32)[0]
    onesf_sb = ar.alloc("onesf_sb", [128, 128], F32)[0]
    onesb_sb = ar.alloc("onesb_sb", [128, 1], BF16)[0]
    maskt_sb = ar.alloc("maskt_sb", [128, 128], BF16)[0]
    ident_sb = ar.alloc("ident_sb", [128, 128], BF16)[0]
    perm32_sb = ar.alloc("perm32_sb", [128, 128], BF16)[0]
    selA_sb = ar.alloc("selA_sb", [33, 256], BF16)[0]

    for i in range(NE):
        b.dma("gpsimd", xtb[i], xt_d[i * 128:(i + 1) * 128, :],
              writes=[f"xtb{i}"], tag=f"ld_xtb{i}")
    for wname, d in (("wq", wq_d), ("wk", wk_d), ("wv", wv_d)):
        for i in range(NE):
            b.dma("gpsimd", wsb[wname][i], d[i * 128:(i + 1) * 128, :],
                  writes=[f"{wname}sb{i}"], tag=f"ld_{wname}{i}")
    b.dma("gpsimd", cosr, cos_d, writes=["cosr"])
    b.dma("gpsimd", sinr, sin_d, writes=["sinr"])
    b.dma("gpsimd", bq_sb, bq_d, writes=["bq_sb"])
    b.dma("gpsimd", bk_sb, bk_d, writes=["bk_sb"])
    b.dma("gpsimd", bv_sb, bv_row_d, writes=["bv_sb"])
    b.dma("gpsimd", ones_sb, ones_col_d, writes=["ones_sb"])
    b.dma("gpsimd", eps_sb, eps_d, writes=["eps_sb"])
    b.dma("gpsimd", onesf_sb, onesf_d, writes=["onesf_sb"])
    b.dma("gpsimd", onesb_sb, onesb_d, writes=["onesb_sb"])
    b.dma("gpsimd", maskt_sb, maskt_d, writes=["maskt_sb"])
    b.dma("gpsimd", ident_sb, ident_d, writes=["ident_sb"])
    b.dma("gpsimd", perm32_sb, perm32_d, writes=["perm32_sb"])
    b.dma("gpsimd", selA_sb, selA_d, writes=["selA_sb"])

    for t in range(NT):
        vv = vst[t].rearrange("p (h u) -> p h u", u=D + 1)
        b.op("vector", lambda e, _a=vv: e.memset(_a[:, :, D:D + 1], 1.0),
             writes=[f"vst{t}:ones"], tag=f"ones_v{t}")

    def bcast128(val_ap, dst_ap, rkey, wkeys, tag):
        b.op("tensor",
             lambda e: e.matmul(PB[0][:, 0:1], onesf_sb[0:1, :], val_ap,
                                start=True, stop=True),
             reads=[rkey, "onesf_sb"], writes=["PB0"], tag=f"{tag}_mm")
        b.op("vector", lambda e: (e.tensor_copy(dst_ap, PB[0][:, 0:1]),
                                  e.drain())[-1],
             reads=["PB0"], writes=list(wkeys) + ["PB0"], tag=f"{tag}_ev")

    pb_i = [0]

    def next_pb():
        i = pb_i[0] % 4
        pb_i[0] += 1
        return PB[i], f"PB{i}"

    def qk_block(which, wt, bias_sb, outT, ob):
        for c in range(4):
            ps, pkey = next_pb()
            for kk in range(NE):
                b.op("tensor",
                     lambda e, _ps=ps, _l=wt[kk][:, ob * 128:(ob + 1) * 128],
                            _r=xtb[kk][:, c * 512:(c + 1) * 512],
                            _st=(kk == 0), _sp=(kk == NE - 1):
                     e.matmul(_ps, _l, _r, start=_st, stop=_sp),
                     reads=[f"w{which}sb{kk}", f"xtb{kk}"],
                     writes=[pkey] if (kk == 0 or kk == NE - 1) else [],
                     tag=f"mm_{which}{ob}c{c}k{kk}")
            b.op("scalar",
                 lambda e, _o=outT[ob][:, c * 512:(c + 1) * 512], _i=ps,
                        _b=bias_sb[:, ob:ob + 1]:
                 e.activation(_o, _i, AF.Identity, bias=_b),
                 reads=[pkey, f"b{which}_sb"],
                 writes=[f"{which}T{ob}:pre{c}"],
                 tag=f"evac_{which}{ob}c{c}")

    def rope_block(which, outT, ob, pi):
        pbuf = part[pi]
        pre = [f"{which}T{ob}:pre{c}" for c in range(4)]
        for g in range(4):
            ps, pkey = next_pb()
            b.op("tensor",
                 lambda e, _ps=ps, _r=outT[ob][:, g * 512:(g + 1) * 512]:
                 e.matmul(_ps, perm32_sb, _r, start=True, stop=True),
                 reads=pre + ["perm32_sb"], writes=[pkey],
                 tag=f"pshm_{which}{ob}g{g}")
            b.op("scalar",
                 lambda e, _o=pbuf[:, g * 512:(g + 1) * 512], _i=ps:
                 e.activation(_o, _i, AF.Identity),
                 reads=[pkey],
                 writes=[f"part{pi}:g{g}", f"part{pi}:m", pkey],
                 tag=f"pshe_{which}{ob}g{g}")
        pr = [f"part{pi}:g{g}" for g in range(4)]
        b.op("vector", lambda e, _p=pbuf, _s=sinr: e.tensor_mul(_p, _p, _s),
             reads=pr + ["sinr"], writes=[f"part{pi}:m"],
             tag=f"rope_s_{which}{ob}")
        b.op("vector", lambda e, _q=outT[ob], _c=cosr: e.tensor_mul(_q, _q, _c),
             reads=pre + ["cosr"] + pr, writes=[f"{which}T{ob}:cosd"],
             tag=f"rope_c_{which}{ob}")
        b.op("vector", lambda e, _q=outT[ob], _p=pbuf: e.tensor_add(_q, _q, _p),
             reads=[f"part{pi}:m", f"{which}T{ob}:cosd"],
             writes=[f"{which}T{ob}"], tag=f"rope_a_{which}{ob}")

    for ob in range(NE):
        qk_block("q", wsb["wq"], bq_sb, qT, ob)
        qk_block("k", wsb["wk"], bk_sb, kT, ob)
        if _ROPE:
            rope_block("q", qT, ob, 0)
            rope_block("k", kT, ob, 1)

    for t in range(NT):
        for c in range(2):
            ps, pkey = next_pb()
            for kk in range(NE):
                b.op("tensor",
                     lambda e, _ps=ps, _l=xtb[kk][:, t * 128:(t + 1) * 128],
                            _r=wsb["wv"][kk][:, c * 512:(c + 1) * 512],
                            _st=(kk == 0):
                     e.matmul(_ps, _l, _r, start=_st, stop=False),
                     reads=[f"wvsb{kk}", f"xtb{kk}"],
                     writes=[pkey] if kk == 0 else [],
                     tag=f"mm_v{t}c{c}k{kk}")
            b.op("tensor",
                 lambda e, _ps=ps, _r=bv_sb[:, c * 512:(c + 1) * 512]:
                 e.matmul(_ps, ones_sb[0:1, :], _r, start=False, stop=True),
                 reads=["ones_sb", "bv_sb"], writes=[pkey], tag=f"mm_vb{t}c{c}")
            vv = vst[t].rearrange("p (h u) -> p h u", u=D + 1)
            pv = ps.rearrange("p (h u) -> p h u", u=D)
            b.op("vector",
                 lambda e, _o=vv[:, c * 8:(c + 1) * 8, 0:D], _i=pv:
                 e.tensor_copy(_o, _i),
                 reads=[pkey], writes=[f"vst{t}:c{c}"], tag=f"evac_v{t}c{c}")

    if _PHASES >= 1:
        for ob in range(NE):
            qkeys = [f"qT{ob}"] if _ROPE else [f"qT{ob}:pre{c}" for c in range(4)]
            kkeys = [f"kT{ob}"] if _ROPE else [f"kT{ob}:pre{c}" for c in range(4)]
            store_tap(f"tap_qT{ob}", qT[ob], qkeys)
            store_tap(f"tap_kT{ob}", kT[ob], kkeys)
        for i in range(NE):
            store_tap(f"tap_xtb{i}", xtb[i], [f"xtb{i}"])
        for t in range(NT):
            store_tap(f"tap_v{t}", vst[t],
                      [f"vst{t}:c0", f"vst{t}:c1", f"vst{t}:ones"])

    qdone = lambda a: ([f"qT{a}"] if _ROPE else [f"qT{a}:pre{c}" for c in range(4)])
    kdone = lambda a: ([f"kT{a}"] if _ROPE else [f"kT{a}:pre{c}" for c in range(4)])

    def wquant(wid, srcs, dsts, win, win_key,
               q_sb, ok_q, marep, ok_ma, inv_ma, ok_inv, i32c,
               ok_win=None, ok_i32=None):
        steps = []
        npc = len(srcs)
        numel = float(npc * 128 * 1024)
        steps.append(lambda: b.op(
            "vector", lambda e: e.memset(q_sb[:, 0:1], 0.0),
            writes=_wk(f"wqa{wid}:acc", ok_q), tag=f"wq{wid}_z"))

        def p1step(i):
            wj = win[i % 2]
            wk_ = f"{win_key}{i % 2}"
            b.dma("gpsimd", wj, srcs[i],
                  reads=[f"wqa{wid}:rd{i - 2}"] if i > 1 else [],
                  writes=_wk(wk_, ok_win[i] if i < 2 else None),
                  tag=f"wq{wid}_ld{i}")
            b.op("vector", lambda e, _w=wj: (
                 e.reduce_sum(q_sb[:, 1:2], _w, AX.X, apply_absolute_value=True),
                 e.drain(),
                 e.tensor_tensor(q_sb[:, 0:1], q_sb[:, 0:1], q_sb[:, 1:2],
                                 ALU.add),
                 e.drain())[-1],
                 reads=[wk_, f"wqa{wid}:acc"],
                 writes=[f"wqa{wid}:acc", f"wqa{wid}:rd{i}"], tag=f"wq{wid}_s{i}")
        for i in range(npc):
            steps.append(lambda _i=i: p1step(_i))

        def mastep():
            b.op("tensor",
                 lambda e: e.matmul(PB[0][0:1, 0:1], q_sb[:, 0:1],
                                    onesf_sb[:, 0:1], start=True, stop=True),
                 reads=[f"wqa{wid}:acc", "onesf_sb"], writes=["PB0"],
                 tag=f"wq{wid}_tot")
            b.op("vector", lambda e: (
                 e.tensor_copy(q_sb[0:1, 1:2], PB[0][0:1, 0:1]),
                 e.drain(),
                 e.tensor_scalar(q_sb[0:1, 2:3], q_sb[0:1, 1:2], 1.0 / numel,
                                 EPS, ALU.mult, ALU.max),
                 e.drain(),
                 e.reciprocal(q_sb[0:1, 3:4], q_sb[0:1, 2:3]),
                 e.drain())[-1],
                 reads=["PB0"], writes=[f"wqa{wid}:ma", "PB0"],
                 tag=f"wq{wid}_ma")
            bcast128(q_sb[0:1, 2:3], marep, f"wqa{wid}:ma",
                     _wk(f"marep{wid}", ok_ma), f"wq{wid}_mar")
            bcast128(q_sb[0:1, 3:4], inv_ma, f"wqa{wid}:ma",
                     _wk(f"invma{wid}", ok_inv), f"wq{wid}_invr")
        steps.append(mastep)

        def p2step(i):
            wj = win[i % 2]
            wk_ = f"{win_key}{i % 2}"
            b.dma("gpsimd", wj, srcs[i],
                  reads=[f"wqb{wid}:done{i - 2}"] if i > 1
                        else [f"wqa{wid}:rd{npc - 2 + i}"],
                  writes=[wk_], tag=f"wq{wid}_ld2_{i}")
            dst_ap, dkey = dsts[i]
            for hh in range(2):
                b.op("vector",
                     lambda e, _w=wj[:, hh * 512:(hh + 1) * 512]:
                     e.tensor_scalar(i32c, _w, inv_ma[:, 0:1], None, ALU.mult),
                     reads=[wk_, f"invma{wid}", f"wqb{wid}:q"],
                     writes=_wk("i32s", ok_i32 if (i == 0 and hh == 0) else None),
                     tag=f"wq{wid}_cv{i}.{hh}")
                b.op("vector",
                     lambda e, _o=dst_ap[:, hh * 512:(hh + 1) * 512]:
                     e.tensor_scalar(_o, i32c, 1.0, -1.0, ALU.min, ALU.max),
                     reads=["i32s"],
                     writes=[dkey, f"wqb{wid}:q"] +
                            ([f"wqb{wid}:done{i}"] if hh == 1 else []),
                     tag=f"wq{wid}_cl{i}.{hh}")
        for i in range(npc):
            steps.append(lambda _i=i: p2step(_i))
        return steps




    wq1_steps = []
    wq2_steps = []
    if _PHASES >= 2:
        for i in range(NE):
            ar.free(f"xtb{i}")
        for w in ("wq", "wk", "wv"):
            for i in range(NE):
                ar.free(f"{w}sb{i}")
        ar.free("cosr")
        ar.free("sinr")
        ar.free("part0")
        ar.free("part1")

        sumsP, ok_sp = [], []
        for i in range(2):
            apx, ok = ar.alloc(f"sumsP{i}", [33, 1024], BF16)
            sumsP.append(apx)
            ok_sp.append(ok)
        recb = [ar.alloc(f"recb{i}", [128, 512], BF16)[0] for i in range(2)]
        Esl, ok_E = [], []
        for i in range(4):
            apx, ok = ar.alloc(f"Esl{i}", [128, 1024], BF16)
            Esl.append(apx)
            ok_E.append(ok)
        attnT, ok_at = [], []
        for a2 in range(NE):
            apx, ok = ar.alloc(f"attnT{a2}", [128, T], BF16)
            attnT.append(apx)
            ok_at.append(ok)
        negC, ok_nc = [], []
        for i in range(2):
            apx, ok = ar.alloc(f"negC{i}", [128, 1], F32)
            negC.append(apx)
            ok_nc.append(ok)
        sqb_sb, ok_sqb = ar.alloc("sqb_sb", [128, T], BF16)
        nrm_sb, ok_nrm = ar.alloc("nrm_sb", [1, 12], F32)

        def cprep(a):
            for which, src_t, dk, col0 in (("q", qT[a], qdone(a), 0),
                                           ("k", kT[a], kdone(a), 4)):
                b.op("vector", lambda e, _s=src_t: e.tensor_tensor(
                     sqb_sb, _s, _s, ALU.mult),
                     reads=dk + ["sqb:rd"],
                     writes=_wk("sqb", ok_sqb if (a == 0 and which == "q")
                                else None), tag=f"sq{which}{a}")
                for cch in range(4):
                    pbn = PB[cch % 2]
                    pkn = f"PB{cch % 2}"
                    b.op("tensor",
                         lambda e, _p=pbn[0:1, :],
                                _r=sqb_sb[:, cch * 512:(cch + 1) * 512]:
                         e.matmul(_p, onesb_sb, _r, start=True, stop=True),
                         reads=["sqb", "onesb_sb"], writes=[pkn],
                         tag=f"nq{which}{a}.{cch}")
                    b.op("vector",
                         lambda e, _o=nrm_sb[:, col0 + cch:col0 + cch + 1],
                                _p=pbn[0:1, :]:
                         e.reduce_max(_o, _p, AX.X),
                         reads=[pkn], writes=[f"nrm:{which}{cch}", pkn,
                                              "sqb:rd"],
                         tag=f"nr{which}{a}.{cch}")
            b.op("vector", lambda e: (
                 e.drain(),
                 e.reduce_max(nrm_sb[:, 8:9], nrm_sb[:, 0:4], AX.X),
                 e.reduce_max(nrm_sb[:, 9:10], nrm_sb[:, 4:8], AX.X),
                 e.drain(),
                 e.tensor_scalar(nrm_sb[:, 10:11], nrm_sb[:, 8:9],
                                 nrm_sb[:, 9:10], None, ALU.mult),
                 e.drain())[-1],
                 reads=[f"nrm:{w2}{cch}" for w2 in ("q", "k")
                        for cch in range(4)],
                 writes=["nrm:prod"], tag=f"cb{a}")
            b.op("scalar", lambda e: (
                 e.activation(nrm_sb[:, 11:12], nrm_sb[:, 10:11], AF.Ln),
                 e.activation(nrm_sb[:, 11:12], nrm_sb[:, 11:12], AF.Exp,
                              scale=0.5))[-1],
                 reads=["nrm:prod"], writes=["nrm:sqrt"], tag=f"cs{a}")
            b.op("vector", lambda e: (e.tensor_scalar(
                 nrm_sb[:, 10:11], nrm_sb[:, 11:12], -0.125, None, ALU.mult),
                 e.drain())[-1],
                 reads=["nrm:sqrt"], writes=["nrm:negc"], tag=f"cn{a}")
            bcast128(nrm_sb[:, 10:11], negC[a % 2], "nrm:negc",
                     _wk(f"negC{a % 2}", ok_nc[a % 2] if a < 2 else None),
                     f"crep{a}")

        cprep(0)
        for a in range(NE):
            for qh in range(2):
                kt_max = 8 * qh + 7
                sched = [("sc", 0)]
                for kt in range(1, kt_max + 1):
                    sched.append(("sc", kt))
                    sched.append(("pv", kt - 1))
                sched.append(("pv", kt_max))

                def chunk_list(kt, qh=qh):
                    out = []
                    for lc in range(2):
                        c = 2 * qh + lc
                        if 4 * c + 3 < kt:
                            continue
                        off = max(0, kt * 128 - c * 512)
                        out.append((lc, c, off, c == kt // 4))
                    return out

                for kind, kt in sched:
                    if kind == "pv" and a >= 1 and wq1_steps:
                        wq1_steps.pop(0)()
                    for h01 in range(2):
                        h = 2 * a + h01
                        slot = (2 * kt + h01) % 4
                        pbi = 2 * (kt % 2) + h01
                        pb = PB[pbi]
                        pkey = f"PB{pbi}"
                        for lc, c, off, diag in chunk_list(kt):
                            n = 512 - off
                            ek = f"Esl{slot}:c{lc}"
                            if kind == "sc":
                                if diag:
                                    b.op("tensor",
                                         lambda e, _p=pb[:, off:off + 128]:
                                         e.matmul(_p, maskt_sb, ident_sb,
                                                  start=True, stop=False),
                                         reads=["maskt_sb", "ident_sb"],
                                         writes=[pkey],
                                         tag=f"msk{a}.{qh}.{kt}.{h01}")
                                b.op("tensor",
                                     lambda e, _p=pb[:, off:512],
                                            _l=kT[a][64 * h01:64 * h01 + 64,
                                                     kt * 128:(kt + 1) * 128],
                                            _r=qT[a][64 * h01:64 * h01 + 64,
                                                     c * 512 + off:(c + 1) * 512],
                                            _st=not diag, _tp=(64 * h01, 0):
                                     e.matmul(_p, _l, _r, start=_st, stop=True,
                                              tile_position=_tp),
                                     reads=qdone(a) + kdone(a),
                                     writes=[pkey],
                                     tag=f"sc{a}.{qh}.{kt}.{h01}.{c}")
                                b.op("scalar",
                                     lambda e, _o=Esl[slot][:, lc * 512:lc * 512 + n],
                                            _i=pb[:, off:512], _b=negC[a % 2]:
                                     e.activation(_o, _i, AF.Exp, bias=_b,
                                                  scale=0.125),
                                     reads=[pkey, f"negC{a % 2}"],
                                     writes=[ek],
                                     tag=f"exp{a}.{qh}.{kt}.{h01}.{c}")
                            else:
                                st = (kt == 0)
                                sp = (kt == min(4 * c + 3, kt_max))
                                b.op("tensor",
                                     lambda e,
                                            _p=PO[0:65, h01 * 1024 + lc * 512 + off:
                                                  h01 * 1024 + (lc + 1) * 512],
                                            _l=vst[kt][:, h * 65:h * 65 + 65],
                                            _r=Esl[slot][:, lc * 512:lc * 512 + n],
                                            _st=st, _sp=sp:
                                     e.matmul(_p, _l, _r, start=_st, stop=_sp),
                                     reads=[ek, f"vst{kt}:c0", f"vst{kt}:c1",
                                            f"vst{kt}:ones"],
                                     writes=[f"PO:{h01}.{lc}"],
                                     tag=f"pv{a}.{qh}.{kt}.{h01}.{c}")
                for h01 in range(2):
                    h = 2 * a + h01
                    b.op("scalar",
                         lambda e, _o=attnT[a][64 * h01:64 * h01 + 64,
                                               qh * 1024:(qh + 1) * 1024],
                                _i=PO[0:64, h01 * 1024:(h01 + 1) * 1024]:
                         e.activation(_o, _i, AF.Identity),
                         reads=[f"PO:{h01}.0", f"PO:{h01}.1"],
                         writes=_wk(f"attnT{a}:u{h01}.{qh}",
                                    ok_at[a] if (h01 == 0 and qh == 0) else None),
                         tag=f"oev{a}.{qh}.{h01}")
                    b.op("scalar",
                         lambda e, _o=sumsP[h01][qh * 32:qh * 32 + 1, :],
                                _i=PO[64:65, h01 * 1024:(h01 + 1) * 1024]:
                         e.activation(_o, _i, AF.Identity),
                         reads=[f"PO:{h01}.0", f"PO:{h01}.1", f"sums{h01}:used"],
                         writes=_wk(f"sums:{h01}.{qh}",
                                    ok_sp[h01] if a == 0 and qh == 0 else None,
                                    f"PO:{h01}.0", f"PO:{h01}.1"),
                         tag=f"sev{a}.{qh}.{h01}")

            if a + 1 < NE:
                cprep(a + 1)
            for lc in range(4):
                qh2 = lc // 2
                col = (lc % 2) * 512
                po = PO[:, (lc % 2) * 512:(lc % 2) * 512 + 512]
                pkey = f"PO:0.{lc % 2}"
                for h01 in range(2):
                    sl = selA_sb[qh2 * 32:qh2 * 32 + 1,
                                 h01 * 128:(h01 + 1) * 128]
                    sr = sumsP[h01][qh2 * 32:qh2 * 32 + 1, col:col + 512]
                    b.op("tensor",
                         lambda e, _p=po, _l=sl, _r=sr, _st=(h01 == 0):
                         e.matmul(_p, _l, _r, start=_st, stop=(h01 == 1)),
                         reads=[f"sums:{h01}.{qh2}", "selA_sb"],
                         writes=[pkey], tag=f"rep{a}.{lc}.{h01}")
                def _recip(e, _o=recb[lc % 2], _p=po):
                    with nc.allow_low_precision(reason="softmax 1/sum bf16"):
                        e.reciprocal(_o, _p)
                    return e.drain()
                b.op("vector", _recip,
                     reads=[pkey, f"recb{lc % 2}:used"],
                     writes=[f"recb{lc % 2}", pkey], tag=f"rcp{a}.{lc}")
                b.op("vector",
                     lambda e, _o=attnT[a][:, lc * 512:(lc + 1) * 512],
                            _r=recb[lc % 2]:
                     e.tensor_tensor(_o, _o, _r, ALU.mult),
                     reads=[f"recb{lc % 2}"] + [f"attnT{a}:u{h01}.{qh}"
                                                for h01 in range(2)
                                                for qh in range(2)],
                     writes=[f"attnT{a}:n{lc}", f"recb{lc % 2}:used"] +
                            (["sums0:used", "sums1:used"] if lc == 3 else []),
                     tag=f"div{a}.{lc}")
            if a == 0:
                win1a, ok_w1a = ar.alloc("win1a", [128, 1024], F32)
                win1b, ok_w1b = ar.alloc("win1b", [128, 1024], F32)
                wq1aux, ok_q1 = ar.alloc("wq1aux", [128, 132], F32)
                ma1rep, ok_ma1 = ar.alloc("ma1rep", [128, 1], F32)
                invma1, ok_inv1 = ar.alloc("invma1", [128, 1], F32)
                i32w, ok_i32w = ar.alloc("i32w", [128, 512], I32)
                w1q, ok_w1q = [], []
                for i in range(NE):
                    apx, ok = ar.alloc(f"w1q{i}", [128, HID], FP8)
                    w1q.append(apx)
                    ok_w1q.append(ok)
                w1srcs, w1dsts = [], []
                for i in range(NE):
                    for cc in range(4):
                        w1srcs.append(w1t_d[i * 128:(i + 1) * 128,
                                            cc * 1024:(cc + 1) * 1024])
                        w1dsts.append((w1q[i][:, cc * 1024:(cc + 1) * 1024],
                                       f"w1q:{i}.{cc}"))
                wq1_steps.extend(wquant(
                    1, w1srcs, w1dsts, [win1a, win1b], "win1",
                    wq1aux, ok_q1, ma1rep, ok_ma1, invma1, ok_inv1, i32w,
                    ok_win=[ok_w1a, ok_w1b], ok_i32=ok_i32w))

        if _PHASES == 2:
            for a in range(NE):
                store_tap(f"tap_attnT{a}", attnT[a],
                          [f"attnT{a}:n{lc}" for lc in range(4)])
            store_tap("tap_sums0", sumsP[0], [f"sums:0.{qh}" for qh in range(2)])
            store_tap("tap_sums1", sumsP[1], [f"sums:1.{qh}" for qh in range(2)])
            store_tap("tap_negC0", negC[0], ["negC0"])
            store_tap("tap_negC1", negC[1], ["negC1"])
            store_tap("tap_nrm", nrm_sb, ["nrm:negc"])

    def atdone(fb):
        return [f"attnT{fb}:n{lc}" for lc in range(4)]

    if _PHASES >= 2:
        while wq1_steps:
            wq1_steps.pop(0)()

    if _PHASES >= 3:
        for a in range(NE):
            ar.free(f"qT{a}")
            ar.free(f"kT{a}")
        for t in range(NT):
            ar.free(f"vst{t}")
        for i in range(4):
            ar.free(f"Esl{i}")
        ar.free("sumsP0", keys=["sums:0", "sums0:"])
        ar.free("sumsP1", keys=["sums:1", "sums1:"])
        ar.free("recb0")
        ar.free("recb1")
        ar.free("bv_sb")
        ar.free("bq_sb")
        ar.free("bk_sb")
        ar.free("maskt_sb")
        ar.free("sqb_sb", keys=["sqb"])
        ar.free("nrm_sb", keys=["nrm"])
        ar.free("selA_sb")
        ar.free("onesb_sb")
        ar.free("negC0")
        ar.free("negC1")

        wproj_sb, ok_wp = [], []
        for i in range(NE):
            apx, ok = ar.alloc(f"wproj{i}", [128, E], BF16)
            wproj_sb.append(apx)
            ok_wp.append(ok)
        bproj_sb, ok_bp = ar.alloc("bproj_sb", [1, E], BF16)
        for i in range(NE):
            b.dma("gpsimd", wproj_sb[i], wproj_d[i * 128:(i + 1) * 128, :],
                  writes=_wk(f"wproj{i}", ok_wp[i]), tag=f"ld_wproj{i}")
        b.dma("gpsimd", bproj_sb, bproj_row_d, writes=_wk("bproj_sb", ok_bp))

        x2_t, ok_x2 = ar.alloc("x2", [128, NT * E], F32)
        x1b, ok_x1 = [], []
        for i in range(2):
            apx, ok = ar.alloc(f"x1b{i}", [128, E], F32)
            x1b.append(apx)
            ok_x1.append(ok)
        st6, ok_st6 = ar.alloc("st6", [128, 12], F32)
        wrkA, ok_ag2 = ar.alloc("wrkA", [128, 12], F32)
        lnsc, ok_lnsc = ar.alloc("lnsc", [128, 3], F32)

        xtok_r = xtok_d.rearrange("(a p) e -> p a e", p=128)
        for t in range(NT):
            b.dma("gpsimd", x1b[t % 2], xtok_r[:, t, :],
                  reads=[f"x1u:{t % 2}"],
                  writes=_wk(f"x1ld:{t % 2}", ok_x1[t % 2] if t < 2 else None),
                  tag=f"ld_x1_{t}")
            x1 = x1b[t % 2]
            for c in range(2):
                pb = PB[c]
                pkey = f"PB{c}"
                for fb in range(NE):
                    b.op("tensor",
                         lambda e, _p=pb, _l=attnT[fb][:, t * 128:(t + 1) * 128],
                                _r=wproj_sb[fb][:, c * 512:(c + 1) * 512],
                                _st=(fb == 0):
                         e.matmul(_p, _l, _r, start=_st, stop=False),
                         reads=[f"wproj{fb}"] + atdone(fb),
                         writes=[pkey] if fb == 0 else [],
                         tag=f"prj{t}.{c}.{fb}")
                b.op("tensor",
                     lambda e, _p=pb, _r=bproj_sb[:, c * 512:(c + 1) * 512]:
                     e.matmul(_p, ones_sb[0:1, :], _r, start=False, stop=True),
                     reads=["ones_sb", "bproj_sb"], writes=[pkey],
                     tag=f"prjb{t}.{c}")
                b.op("vector",
                     lambda e, _o=x1[:, c * 512:(c + 1) * 512], _p=pb:
                     e.tensor_tensor(_o, _o, _p, ALU.add),
                     reads=[pkey, f"x1ld:{t % 2}"],
                     writes=[f"x1:{t % 2}.{c}"],
                     tag=f"prjev{t}.{c}")
            _ln_group(b, "lnA", x1, st6, wrkA, lnsc, 0, 1,
                      [f"x1:{t % 2}.{c}" for c in range(2)], f"pln{t}", 2,
                      extra_stats_w=[k for k in (ok_st6, ok_ag2, ok_lnsc)
                                     if k] if t == 0 else (), eps_ap=eps_sb,
                      pcol=2)
            _ln_center(b, "lnA", x1, x2_t[:, t * E:(t + 1) * E], lnsc, 0, 2,
                       [f"x1:{t % 2}.{c}" for c in range(2)],
                       _wk(f"x2:{t}", ok_x2 if t == 0 else None,
                           f"x1u:{t % 2}"), f"pln{t}")
            if t == 0 and _PHASES >= 4:
                b.seed_pseudo("__ov_w2w0", b.accessors("win10"))
                b.seed_pseudo("__ov_w2w1", b.accessors("win11"))
                wq2aux, ok_q2 = ar.alloc("wq2aux", [128, 132], F32)
                ma2rep, ok_ma2 = ar.alloc("ma2rep", [128, 1], F32)
                invma2, ok_inv2 = ar.alloc("invma2", [128, 1], F32)
                w2q, ok_w2q = [], []
                for i in range(NH):
                    apx, ok = ar.alloc(f"w2q{i}", [128, E], FP8)
                    w2q.append(apx)
                    ok_w2q.append(ok)
                w2srcs = [w2t_d[i * 128:(i + 1) * 128, :] for i in range(NH)]
                w2dsts = [(w2q[i], f"w2q:{i}.0") for i in range(NH)]
                wq2_steps.extend(wquant(
                    2, w2srcs, w2dsts,
                    [win1a[:, 0:E], win1b[:, 0:E]], "win2",
                    wq2aux, ok_q2, ma2rep, ok_ma2, invma2, ok_inv2, i32w,
                    ok_win=["__ov_w2w0", "__ov_w2w1"], ok_i32=None))
            for _ in range(5 if t >= 1 else 0):
                if wq2_steps:
                    wq2_steps.pop(0)()

        while wq2_steps:
            wq2_steps.pop(0)()
        store_tap("tap_x2", x2_t, [f"x2:{t}" for t in range(NT)])
        store_tap("tap_ag2", wrkA, ["lnA"])
        store_tap("tap_lnsc", lnsc, ["lnA"])
        store_tap("tap_x1b", x1b[15 % 2], ["lnA", "x1:1.0", "x1:1.1"])

    if _PHASES >= 5:
        for i in range(NE):
            ar.free(f"wproj{i}")
        ar.free("win1a", keys=["win10", "win20"])
        ar.free("win1b", keys=["win11", "win21"])
        ar.free("wq1aux", keys=["wqa1", "wqb1"])
        ar.free("x1b0", keys=["x1:0", "x1ld:0", "x1u:0"])
        ar.free("x1b1", keys=["x1:1", "x1ld:1", "x1u:1"])
        for a in range(NE):
            ar.free(f"attnT{a}")
        ar.free("bproj_sb")
        ar.free("wq2aux", keys=["wqa2", "wqb2"])
        ar.free("st6", keys=["lnA"])
        ar.free("wrkA", keys=["lnA"])
        ar.free("lnsc", keys=["lnA"])
        ar.free("onesf_sb")

        hact_b, ok_hact = [], []
        for i in range(2):
            apx, ok = ar.alloc(f"hact{i}", [128, HID], F32)
            hact_b.append(apx)
            ok_hact.append(ok)
        ln2w_sb, ok_l2w = ar.alloc("ln2w_sb", [128, HID], BF16)
        ln2b_sb, ok_l2b = ar.alloc("ln2b_sb", [128, HID], BF16)
        qhT, ok_qhT = ar.alloc("qhT", [128, HID], BF16)
        y1b, ok_y1 = ar.alloc("y1b", [128, E], F32)
        t3, ok_t3 = y1b, None
        ln1w_sb, ok_l1w = ar.alloc("ln1w_sb", [128, E], BF16)
        ln1b_sb, ok_l1b = ar.alloc("ln1b_sb", [128, E], BF16)
        x2qT_b, ok_x2qT = [], []
        for i in range(2):
            apx, ok = ar.alloc(f"x2qT{i}", [128, E], BF16)
            x2qT_b.append(apx)
            ok_x2qT.append(ok)
        qbuf_b, ok_qbuf_b = [], []
        for i in range(2):
            apx, ok = ar.alloc(f"qbuf{i}", [128, 512], BF16)
            qbuf_b.append(apx)
            ok_qbuf_b.append(ok)
        b1_sb, ok_b1 = ar.alloc("b1_sb", [65, 2048], BF16)
        st6f, ok_st6f = ar.alloc("st6f", [128, 48], F32)
        wrkB, ok_ag2f = ar.alloc("wrkB", [128, 48], F32)
        lnscf, ok_lnscf = ar.alloc("lnscf", [128, 13], F32)
        i32a, ok_i32a = i32w, None
        st6_l2, st6_l1, st6_f = st6f, st6f[:, 0:12], st6f[:, 0:12]
        wrk_l2, wrk_l1, wrk_f = wrkB[:, 0:24], wrkB[:, 24:36], wrkB[:, 36:48]

        b.dma("gpsimd", ln1w_sb, ln1w_d, writes=_wk("ln1w_sb", ok_l1w))
        b.dma("gpsimd", ln1b_sb, ln1b_d, writes=_wk("ln1b_sb", ok_l1b))
        b.dma("gpsimd", ln2w_sb, ln2w_d, writes=_wk("ln2w_sb", ok_l2w))
        b.dma("gpsimd", ln2b_sb, ln2b_d, writes=_wk("ln2b_sb", ok_l2b))
        b.dma("gpsimd", b1_sb[0:1, :], b1_row_d[:, 0:2048],
              writes=_wk("b1_sb:raw", ok_b1))
        b.dma("gpsimd", b1_sb[32:33, :], b1_row_d[:, 2048:4096],
              writes=["b1_sb:raw2"])
        b.dma("gpsimd", b1_sb[64:65, 0:1024], b2_row_d[:, 0:1024],
              writes=["b2_sb:raw"])
        b.op("vector", lambda e: e.tensor_scalar(
                 b1_sb[0:33, :], b1_sb[0:33, :], invma1[0:33, 0:1],
                 None, ALU.mult),
             reads=["b1_sb:raw", "b1_sb:raw2", "invma1"], writes=["b1_sb"],
             tag="b1s")
        b.op("vector", lambda e: e.tensor_scalar(
                 b1_sb[64:65, 0:1024], b1_sb[64:65, 0:1024],
                 invma2[64:65, 0:1], None, ALU.mult),
             reads=["b2_sb:raw", "invma2"], writes=["b2_sb"],
             tag="b2s")

        b.seed_pseudo("PO:m0", b.accessors("PO"))
        b.seed_pseudo("PO:m1", b.accessors("PO"))
        b.seed_pseudo("PT0", b.accessors("PB2"))
        b.seed_pseudo("PT1", b.accessors("PB3"))
        _lnB_deps = [op for k in (ok_st6f, ok_ag2f, ok_lnscf) if k
                     for op in b.res.get(k, [None, []])[1]]
        for _g in ("lnL1", "lnL2", "lnF"):
            b.seed_pseudo(_g, _lnB_deps)

        okd = {"t3": ok_t3, "qbuf0": ok_qbuf_b[0], "qbuf1": ok_qbuf_b[1],
               "qhT": ok_qhT, "y1b": ok_y1}
        used = set()

        def fk2(name):
            ok = okd.get(name)
            if ok and name not in used:
                used.add(name)
                return ok
            return None

        def okbuf(oklist, i, nm):
            if oklist[i] and nm not in used:
                used.add(nm)
                return oklist[i]
            return None

        def quant_scales(t, src, src_keys, grp, c_dq, c_q, qkey, tag):
            b.op("vector", lambda e: (
                 e.reduce_max(lnscf[:, c_dq:c_dq + 1], src, AX.X,
                              apply_absolute_value=True),
                 e.drain(),
                 e.reciprocal(lnscf[:, c_q:c_q + 1], lnscf[:, c_dq:c_dq + 1]),
                 e.tensor_scalar(lnscf[:, c_dq:c_dq + 1],
                                 lnscf[:, c_dq:c_dq + 1], 1.0 / 127.0,
                                 None, ALU.mult),
                 e.tensor_scalar(lnscf[:, c_q:c_q + 1],
                                 lnscf[:, c_q:c_q + 1], 127.0, None,
                                 ALU.mult))[-1],
                 reads=list(src_keys) + [grp], writes=[qkey, grp], tag=tag)

        def quant_cvdq(src, ch, c_dq, c_q, qkey, src_keys, trtag, grp):
            b.op("scalar",
                 lambda e, _i=src[:, ch * 512:(ch + 1) * 512]:
                 e.activation(i32a, _i, AF.Identity,
                              scale=lnscf[:, c_q:c_q + 1]),
                 reads=list(src_keys) + [qkey, "i32s", grp], writes=["i32s"],
                 tag=f"{trtag}cv.{ch}")
            qb = qbuf_b[ch % 2]
            qbk = f"qbuf{ch % 2}"
            b.op("scalar",
                 lambda e, _o=qb: e.activation(_o, i32a, AF.Identity,
                                               scale=lnscf[:, c_dq:c_dq + 1]),
                 reads=["i32s", qkey, qbk, grp],
                 writes=_wk(qbk, fk2(qbk)), tag=f"{trtag}dq.{ch}")

        def quant_trs(dst, dkey, ok_dst, ch, trtag):
            qb = qbuf_b[ch % 2]
            qbk = f"qbuf{ch % 2}"
            for b4 in range(4):
                bb = ch * 4 + b4
                pt = PT[bb % 2]
                ptk = f"PT{bb % 2}"
                b.op("tensor",
                     lambda e, _p=pt, _i=qb[:, b4 * 128:(b4 + 1) * 128]:
                     e.transpose(_p, _i, ident_sb),
                     reads=[qbk, "ident_sb", ptk + ":rd"],
                     writes=[ptk], tag=f"{trtag}_{bb}")
                b.op("scalar",
                     lambda e, _o=dst[:, bb * 128:(bb + 1) * 128], _p=pt:
                     e.activation(_o, _p, AF.Identity),
                     reads=[ptk, dkey + ":mm"],
                     writes=_wk(dkey, ok_dst, ptk + ":rd"),
                     tag=f"{trtag}e.{bb}")

        def quant_tr(t, src, src_keys, grp, c_dq, c_q, qkey, dst, dkey,
                     ok_dst, nch, trtag, inter=None):
            quant_cvdq(src, 0, c_dq, c_q, qkey, src_keys, trtag, grp)
            for ch in range(nch):
                quant_trs(dst, dkey, ok_dst if ch == 0 else None, ch, trtag)
                if ch + 1 < nch:
                    quant_cvdq(src, ch + 1, c_dq, c_q, qkey, src_keys, trtag,
                               grp)
                if inter is not None:
                    inter(ch)

        def ln1_q1_tr1(t):
            x2sl = x2_t[:, t * E:(t + 1) * E]
            x2qTt = x2qT_b[t % 2]
            _ln_group(b, "lnL1", x2sl, st6_l1, wrk_l1, lnscf, 0, 1,
                      [f"x2:{t}"], f"l1n{t}", 2, eps_ap=eps_sb, pcol=10)
            _ln_center(b, "lnL1", x2sl, t3, lnscf, 0, 10,
                       [f"x2:{t}", "y1:rd"],
                       _wk("t3", fk2("y1b"), "y1:0", "y1:1"), f"l1n{t}")
            b.op("vector", lambda e: e.tensor_tensor(t3, t3, ln1w_sb, ALU.mult),
                 reads=["t3", "ln1w_sb"], writes=["t3"], tag=f"l1w{t}")
            b.op("vector", lambda e: e.tensor_tensor(t3, t3, ln1b_sb, ALU.add),
                 reads=["t3", "ln1b_sb"], writes=["t3"], tag=f"l1b{t}")
            quant_scales(t, t3, ["t3"], "lnL1", 2, 3, f"q1s{t}", f"l1qs{t}")
            quant_cvdq(t3, 0, 2, 3, f"q1s{t}", ["t3"], f"tr1_{t}", "lnL1")
            quant_cvdq(t3, 1, 2, 3, f"q1s{t}", ["t3"], f"tr1_{t}", "lnL1")

        def ln1_tr(t):
            x2qTt = x2qT_b[t % 2]
            ok = okbuf(ok_x2qT, t % 2, f"x2qT{t % 2}")
            quant_trs(x2qTt, f"x2qT{t % 2}", ok, 0, f"tr1_{t}")
            quant_trs(x2qTt, f"x2qT{t % 2}", None, 1, f"tr1_{t}")

        def mm1_mm(t, n):
            x2qTt = x2qT_b[t % 2]
            pb = PB[n % 2]
            pkey = f"PB{n % 2}"
            for kb in range(NE):
                b.op("tensor",
                     lambda e, _p=pb, _l=x2qTt[:, kb * 128:(kb + 1) * 128],
                            _r=w1q[kb][:, n * 512:(n + 1) * 512],
                            _st=(kb == 0):
                     e.matmul(_p, _l, _r, start=_st, stop=False),
                     reads=[f"x2qT{t % 2}"] +
                           [f"w1q:{kb}.{cc}" for cc in range(4)],
                     writes=[pkey] if kb == 0 else [],
                     tag=f"mm1_{t}.{n}.{kb}")
            b1r = (b1_sb[0:1, n * 512:(n + 1) * 512] if n < 4 else
                   b1_sb[32:33, (n - 4) * 512:(n - 3) * 512])
            onesr = ones_sb[0:1, :] if n < 4 else ones_sb[32:33, :]
            b.op("tensor",
                 lambda e, _p=pb, _r=b1r, _l=onesr:
                 e.matmul(_p, _l, _r, start=False, stop=True),
                 reads=["ones_sb", "b1_sb"], writes=[pkey],
                 tag=f"mm1b_{t}.{n}")

        def mm1_gel(t, n):
            hact = hact_b[t % 2]
            pb = PB[n % 2]
            pkey = f"PB{n % 2}"
            b.op("scalar",
                 lambda e, _o=hact[:, n * 512:(n + 1) * 512], _p=pb:
                 e.activation(_o, _p, AF.Gelu, scale=ma1rep[:, 0:1]),
                 reads=[pkey, "marep1", f"hact{t % 2}:rd"],
                 writes=_wk(f"hact{t % 2}:{n}",
                            okbuf(ok_hact, t % 2, f"hact{t % 2}"),
                            f"hact{t % 2}:n"),
                 tag=f"gel{t}.{n}")

        def mm1_block(t, n):
            mm1_mm(t, n)
            mm1_gel(t, n)

        def mm1_gelu(t):
            for n in range(8):
                mm1_block(t, n)

        def ln2_part1(t):
            hact = hact_b[t % 2]
            hreads = [f"hact{t % 2}:{n}" for n in range(8)]
            _ln_group(b, "lnL2", hact, st6_l2, wrk_l2, lnscf, 6, 7,
                      hreads, f"l2n{t}", 8, eps_ap=eps_sb, pcol=12)

        def p2_head(t):
            hact = hact_b[t % 2]
            hreads = [f"hact{t % 2}:{n}" for n in range(8)]
            _ln_center(b, "lnL2", hact, hact, lnscf, 6, 12, hreads,
                       [f"hact{t % 2}:n"], f"l2n{t}")
            b.op("vector", lambda e, _h=hact:
                 e.tensor_tensor(_h, _h, ln2w_sb, ALU.mult),
                 reads=[f"hact{t % 2}:n", "ln2w_sb"], writes=[f"hact{t % 2}:n"],
                 tag=f"l2w{t}")
            b.op("vector", lambda e, _h=hact:
                 e.tensor_tensor(_h, _h, ln2b_sb, ALU.add),
                 reads=[f"hact{t % 2}:n", "ln2b_sb"],
                 writes=[f"hact{t % 2}:n", f"hact{t % 2}:rd"],
                 tag=f"l2b{t}")
            quant_scales(t, hact, [f"hact{t % 2}:n"], "lnL2", 8, 9,
                         f"q2s{t}", f"l2qs{t}")
            quant_cvdq(hact, 0, 8, 9, f"q2s{t}", [f"hact{t % 2}:n"],
                       f"tr2_{t}", "lnL2")
            quant_cvdq(hact, 1, 8, 9, f"q2s{t}", [f"hact{t % 2}:n"],
                       f"tr2_{t}", "lnL2")

        def p2_chunks(t):
            hact = hact_b[t % 2]
            ok = fk2("qhT")
            for ch in range(8):
                quant_trs(qhT, "qhT", ok if ch == 0 else None, ch, f"tr2_{t}")
                if ch + 2 < 8:
                    quant_cvdq(hact, ch + 2, 8, 9, f"q2s{t}",
                               [f"hact{t % 2}:n"], f"tr2_{t}", "lnL2")
                if t + 2 < NT:
                    mm1_block(t + 2, ch)

        def mm2_out(t):
            x2sl = x2_t[:, t * E:(t + 1) * E]
            for kb in range(NH):
                for c in range(2):
                    po = PO[:, c * 512:(c + 1) * 512]
                    pkey = f"PO:m{c}"
                    b.op("tensor",
                         lambda e, _p=po, _l=qhT[:, kb * 128:(kb + 1) * 128],
                                _r=w2q[kb][:, c * 512:(c + 1) * 512],
                                _st=(kb == 0):
                         e.matmul(_p, _l, _r, start=_st, stop=False),
                         reads=["qhT", f"w2q:{kb}.0"],
                         writes=[pkey] if kb == 0 else [],
                         tag=f"mm2_{t}.{c}.{kb}")
            for c in range(2):
                po = PO[:, c * 512:(c + 1) * 512]
                pkey = f"PO:m{c}"
                b2r = b1_sb[64:65, c * 512:(c + 1) * 512]
                onesr2 = ones_sb[64:65, :]
                b.op("tensor",
                     lambda e, _p=po, _r=b2r, _l=onesr2:
                     e.matmul(_p, _l, _r, start=False, stop=True),
                     reads=["ones_sb", "b2_sb"], writes=[pkey],
                     tag=f"mm2b_{t}.{c}")
                b.op("vector",
                     lambda e, _o=y1b[:, c * 512:(c + 1) * 512], _p=po,
                            _x=x2sl[:, c * 512:(c + 1) * 512]:
                     e.scalar_tensor_tensor(_o, _p, ma2rep[:, 0:1], _x,
                                            ALU.mult, ALU.add),
                     reads=[pkey, "marep2", f"x2:{t}", "y1:rd"],
                     writes=[f"y1:{c}", "t3"],
                     tag=f"m2ev{t}.{c}")
            _ln_group(b, "lnF", y1b, st6_f, wrk_f, lnscf, 4, 5,
                      [f"y1:{c}" for c in range(2)], f"fln{t}", 2,
                      eps_ap=eps_sb, pcol=11)
            _ln_center(b, "lnF", y1b, y1b, lnscf, 4, 11,
                       [f"y1:{c}" for c in range(2)],
                       _wk(f"ob:{t}", None, "y1:rd"), f"fln{t}")
            b.dma("sync", y_d[t * 128:(t + 1) * 128, :], y1b,
                  reads=[f"ob:{t}"], writes=["y1:rd"], tag=f"st_y{t}")

        if _P5_SEQ:
            for t in range(NT):
                ln1_q1_tr1(t)
                ln1_tr(t)
                mm1_gelu(t)
                ln2_part1(t)
                p2_head(t)
                p2_chunks(t)
                mm2_out(t)
        else:
            ln1_q1_tr1(0)
            ln1_tr(0)
            mm1_gelu(0)
            ln1_q1_tr1(1)
            ln1_tr(1)
            ln2_part1(0)
            mm1_gelu(1)
            ln1_q1_tr1(2)
            ln1_tr(2)
            p2_head(0)
            for t in range(NT):
                p2_chunks(t)
                if t + 3 < NT:
                    ln1_q1_tr1(t + 3)
                    ln1_tr(t + 3)
                if t + 1 < NT:
                    ln2_part1(t + 1)
                    p2_head(t + 1)
                mm2_out(t)

        store_tap("tap_qhT0", qhT, ["qhT"])

    def fin(eng):
        for q in ("sync", "gpsimd"):
            st = b.dma_state[q]
            for i, tot in enumerate(st["tot"]):
                if tot > 0:
                    eng.wait_ge(b.dma_pools[q][i], tot)
        return None

    b.op("sync", fin, reads=[], writes=[], tag="final_wait")

    b.finalize_and_emit()
    return nc


def _ln_group(b, grp, src, st6, wrk, lnsc, rcol, mcol, reads, tag, nchunk,
              extra_stats_w=(), eps_ap=None, pcol=None):
    g = 2 * nchunk
    W = 512 * nchunk

    def fn(e):
        for i in range(nchunk):
            e.bn_stats(st6[:, 6 * i:6 * i + 6], src[:, 512 * i:512 * (i + 1)])
        e.drain()
        tr = st6[:, 0:6 * nchunk].rearrange("p (a c) -> p a c", c=3)
        e.reduce_sum(wrk[:, 0:1], tr[:, :, 1], AX.X)
        e.tensor_tensor(wrk[:, 8:8 + g], tr[:, :, 1], tr[:, :, 1], ALU.mult)
        e.reduce_sum(wrk[:, 2:3], tr[:, :, 2], AX.X)
        e.reduce_sum(wrk[:, 1:2], wrk[:, 8:8 + g], AX.X)
        e.tensor_scalar(wrk[:, 3:4], wrk[:, 0:1], wrk[:, 0:1],
                        1.0 / (g * g), ALU.mult, ALU.mult)
        e.scalar_tensor_tensor(wrk[:, 4:5], wrk[:, 1:2], 256.0, wrk[:, 2:3],
                               ALU.mult, ALU.add)
        e.tensor_scalar(lnsc[:, mcol:mcol + 1], wrk[:, 0:1], -1.0 / g, None,
                        ALU.mult)
        return e.scalar_tensor_tensor(wrk[:, 5:6], wrk[:, 4:5], 1.0 / W,
                                      wrk[:, 3:4], ALU.mult, ALU.subtract)
    b.op("vector", fn, reads=list(reads) + [grp],
         writes=[grp, f"{tag}:st", f"{tag}:mu"] + list(extra_stats_w),
         tag=f"{tag}_st")

    def fsc(e):
        e.activation(lnsc[:, rcol:rcol + 1], wrk[:, 5:6], AF.Ln, bias=eps_ap)
        r = e.activation(lnsc[:, rcol:rcol + 1], lnsc[:, rcol:rcol + 1],
                         AF.Exp, scale=-0.5)
        if pcol is not None:
            e.activation(lnsc[:, pcol:pcol + 1], lnsc[:, mcol:mcol + 1],
                         AF.Identity, scale=lnsc[:, rcol:rcol + 1])
            e.activation(wrk[:, 6:7], lnsc[:, mcol:mcol + 1], AF.Identity)
            r = e.activation(wrk[:, 7:8], lnsc[:, mcol:mcol + 1], AF.Identity)
        return r
    b.op("scalar", fsc,
         reads=[f"{tag}:st", f"{tag}:mu", grp, "eps_sb"],
         writes=[grp, f"{tag}:sc"], tag=f"{tag}_sc")


def _ln_center(b, grp, src, out_sl, lnsc, rcol, pcol, reads, wkeys, tag):
    b.op("scalar", lambda e:
         e.activation(out_sl, src, AF.Identity,
                      scale=lnsc[:, rcol:rcol + 1],
                      bias=lnsc[:, pcol:pcol + 1]),
         reads=[f"{tag}:sc", grp] + list(reads),
         writes=[grp] + list(wkeys), tag=f"{tag}_ap")


def kernel(**inputs):
    in_maps = _host_prep(inputs)
    nc = build_program()
    res = run_bass_kernel_spmd(nc, in_maps, core_ids=list(range(N_CORES)))
    outs = [r["y"] for r in res.results]
    out = np.stack(outs, 0).astype(np.float32)
    if _DEBUG_TAPS:
        kernel._tap_results = res.results
    return out


# version 19
# speedup vs baseline: 1.0391x; 1.0093x over previous
import numpy as np
import ml_dtypes
import concourse.bass as bass
import concourse.mybir as mybir
from concourse.bass_utils import run_bass_kernel_spmd

F32 = mybir.dt.float32
BF16 = mybir.dt.bfloat16
FP8 = mybir.dt.float8e4
I32 = mybir.dt.int32
AF = mybir.ActivationFunctionType
ALU = mybir.AluOpType
AX = mybir.AxisListType

N_CORES = 8
T = 2048
E = 1024
H = 16
D = 64
HID = 4096
NT = T // 128
NE = E // 128
NH = HID // 128
EPS = 1e-5

_DEBUG_TAPS = []
_PHASES = 99
_P5_SEQ = False
_ROPE = True

ENGINES = ("tensor", "vector", "scalar", "gpsimd", "sync")


class _Op:
    __slots__ = ("eng", "fn", "reads", "writes", "gidx", "eidx", "dma_sem",
                 "dma_thresh", "marked", "waits", "tag")

    def __init__(self, eng, fn, reads, writes, gidx, tag=""):
        self.eng = eng
        self.fn = fn
        self.reads = reads
        self.writes = writes
        self.gidx = gidx
        self.eidx = -1
        self.dma_sem = None
        self.dma_thresh = 0
        self.marked = -1
        self.waits = []
        self.tag = tag


class Builder:
    def __init__(self, nc):
        self.nc = nc
        self.ops = []
        self.eops = {e: [] for e in ENGINES}
        self.res = {}
        self.prog_sems = {e: nc.alloc_semaphore(name=f"prog_{e}") for e in ENGINES}
        self.dma_pools = {}
        self.dma_state = {}
        for q in ("gpsimd", "sync"):
            self.dma_pools[q] = [nc.alloc_semaphore(name=f"dma_{q}_{i}") for i in range(8)]
            self.dma_state[q] = {"i": 0, "tot": [0] * 8}
        self._deps_of = {}

    def op(self, eng, fn, reads=(), writes=(), tag=""):
        o = _Op(eng, fn, list(reads), list(writes), len(self.ops), tag)
        self._track(o)
        return o

    def dma(self, queue, out_ap, in_ap, reads=(), writes=(), tag="", transpose=False):
        pool = self.dma_pools[queue]
        st = self.dma_state[queue]
        slot = st["i"] % 8
        st["i"] += 1
        sem = pool[slot]
        prior = st["tot"][slot]
        st["tot"][slot] = prior + 16
        thresh = st["tot"][slot]

        def fn(eng, _o=out_ap, _i=in_ap, _t=transpose):
            if _t:
                return eng.dma_start_transpose(_o, _i)
            return eng.dma_start(_o, _i)

        o = _Op(queue, fn, list(reads), list(writes), len(self.ops), tag)
        o.dma_sem = sem
        o.dma_thresh = thresh
        if prior > 0:
            o.waits.append((sem, prior))
        self._track(o)
        return o

    def _track(self, o):
        deps = set()
        for r in o.reads:
            ent = self.res.get(r)
            if ent is not None and ent[0] is not None:
                deps.add(ent[0])
        for w in o.writes:
            ent = self.res.get(w)
            if ent is not None:
                if ent[0] is not None:
                    deps.add(ent[0])
                deps.update(ent[1])
        deps = {d for d in deps if d.eng != o.eng or d.dma_sem is not None}
        self._deps_of[o] = deps
        self.ops.append(o)
        self.eops[o.eng].append(o)
        o.eidx = len(self.eops[o.eng]) - 1
        for r in o.reads:
            self.res.setdefault(r, [None, []])[1].append(o)
        for w in o.writes:
            self.res[w] = [o, []]

    def accessors(self, name):
        out = []
        for k, (w, rs) in self.res.items():
            if k == name or k.startswith(name):
                if w is not None:
                    out.append(w)
                out.extend(rs)
        return out

    def seed_pseudo(self, key, dep_ops):
        if dep_ops:
            self.res[key] = [None, list(dep_ops)]

    def finalize_and_emit(self):
        nc = self.nc
        for o in self.ops:
            for d in self._deps_of[o]:
                if d.dma_sem is None and d.marked < 0:
                    d.marked = 0
        for e in ENGINES:
            n = 0
            for o in self.eops[e]:
                if o.dma_sem is None and o.marked == 0:
                    n += 1
                    o.marked = n
        for o in self.ops:
            need = {}
            for d in self._deps_of[o]:
                if d.dma_sem is not None:
                    key = ("dma", id(d.dma_sem))
                    sem, val = d.dma_sem, d.dma_thresh
                else:
                    key = ("eng", d.eng)
                    sem, val = self.prog_sems[d.eng], d.marked
                    assert d.marked > 0
                if key not in need or need[key][1] < val:
                    need[key] = (sem, val)
            o.waits.extend(need.values())

        waited = {e: {} for e in ENGINES}

        def run_engine(ename):
            def body(eng):
                wd = waited[ename]
                for o in self.eops[ename]:
                    for sem, val in o.waits:
                        k = id(sem)
                        if wd.get(k, -1) >= val:
                            continue
                        eng.wait_ge(sem, val)
                        wd[k] = val
                    inst = o.fn(eng)
                    if o.dma_sem is not None:
                        inst.then_inc(o.dma_sem, 16)
                    elif o.marked > 0:
                        if inst is None:
                            eng.sem_inc(self.prog_sems[ename], 1)
                        else:
                            inst.then_inc(self.prog_sems[ename], 1)
            return body

        with nc.Block() as blk:
            blk.tensor(run_engine("tensor"))
            blk.vector(run_engine("vector"))
            blk.scalar(run_engine("scalar"))
            blk.gpsimd(run_engine("gpsimd"))
            blk.sync(run_engine("sync"))


class Arena:
    def __init__(self, nc, builder):
        self.nc = nc
        self.b = builder
        self.base = (nc.sbuf_base + 31) & ~31
        self.cur = 0
        self.limit = nc.sbuf_top - self.base
        self.tensors = {}
        self.free_list = []
        self.peak = 0

    def alloc(self, name, shape, dtype):
        bpp = int(np.prod(shape[1:])) * mybir.dt.size(dtype)
        bpp = (bpp + 31) & ~31
        off = None
        names = set()
        for i, (fo, fs, nm) in enumerate(self.free_list):
            if fs >= bpp:
                off = fo
                names = nm
                if fs == bpp:
                    self.free_list.pop(i)
                else:
                    self.free_list[i] = (fo + bpp, fs - bpp, set(nm))
                break
        if off is None:
            off = self.cur
            self.cur += bpp
            self.peak = max(self.peak, self.cur)
            assert self.cur <= self.limit, (
                f"sbuf overflow at {name}: {self.cur} > {self.limit}")
        h = self.nc.alloc_sbuf_tensor_at(name, list(shape), dtype,
                                         offset=self.base + off)
        self.tensors[name] = (h, off, bpp)
        okey = None
        if names:
            dep_ops = []
            for nm in names:
                dep_ops.extend(self.b.accessors(nm))
            if dep_ops:
                okey = f"__ov_{name}"
                self.b.seed_pseudo(okey, dep_ops)
        return h.ap(), okey

    def free(self, name, keys=None):
        h, off, bpp = self.tensors.pop(name)
        self.free_list.append((off, bpp, set(keys) if keys else {name}))
        self.free_list.sort(key=lambda x: x[0])
        merged = []
        for fo, fs, nm in self.free_list:
            if merged and merged[-1][0] + merged[-1][1] == fo:
                po, ps, pn = merged[-1]
                merged[-1] = (po, ps + fs, pn | nm)
            else:
                merged.append((fo, fs, set(nm)))
        if merged and merged[-1][0] + merged[-1][1] == self.cur:
            pass
        self.free_list = merged


def _wk(name, okey, *extra):
    out = [name] + list(extra)
    if okey is not None:
        out.append(okey)
    return out



def _host_prep(inputs):
    x = np.ascontiguousarray(inputs["L0_attn_x"])
    Wqkv = np.asarray(inputs["L0_attn_Wqkv"])
    bqkv = np.asarray(inputs["L0_attn_bqkv"])
    Wproj = np.ascontiguousarray(inputs["L0_attn_Wproj"])
    bproj = np.asarray(inputs["L0_attn_bproj"])

    perm = np.concatenate([np.arange(h * D, (h + 1) * D).reshape(D // 2, 2).T.ravel()
                           for h in range(H)])
    Wq = np.ascontiguousarray(Wqkv[:, 0:E][:, perm])
    Wk = np.ascontiguousarray(Wqkv[:, E:2 * E][:, perm])
    Wv = np.ascontiguousarray(Wqkv[:, 2 * E:3 * E])
    bq = np.ascontiguousarray(bqkv[0:E][perm].reshape(NE, 128).T)
    bk = np.ascontiguousarray(bqkv[E:2 * E][perm].reshape(NE, 128).T)
    bv_row = np.ascontiguousarray(bqkv[2 * E:3 * E].reshape(1, E))
    bproj_row = np.ascontiguousarray(bproj.reshape(1, E))

    pos = np.arange(T, dtype=np.float64)
    inv = 10000.0 ** (-np.arange(0, D, 2, dtype=np.float64) / D)
    fr = pos[:, None] * inv[None, :]
    cos = np.cos(fr).astype(np.float32).T
    sin = np.sin(fr).astype(np.float32).T
    cosrep = np.ascontiguousarray(np.tile(cos, (4, 1)))
    sinpm = np.ascontiguousarray(np.tile(np.concatenate([-sin, sin], 0), (2, 1)))

    km = np.arange(128)[:, None]
    qm = np.arange(128)[None, :]
    M = np.where(km > qm, -1e30, 0.0).astype(np.float32)
    maskT = np.ascontiguousarray(M.T)
    ident = np.eye(128, dtype=np.float32)
    perm32 = np.zeros((128, 128), np.float32)
    perm32[np.arange(128) ^ 32, np.arange(128)] = 1.0
    ones_col = np.ones((65, 128), np.float32)
    epsrep = np.full((128, 1), 1e-5, np.float32)
    onesf = np.ones((128, 128), np.float32)
    onesb = np.ones((128, 1), np.float32)
    selA = np.zeros((33, 256), np.float32)
    selA[0, :64] = 1.0
    selA[0, 192:] = 1.0
    selA[32] = selA[0]

    w1 = np.asarray(inputs["L0_ffn_w1"])
    w2 = np.asarray(inputs["L0_ffn_w2"])

    shared = {
        "wq": Wq, "wk": Wk, "wv": Wv, "bq": bq, "bk": bk,
        "bv_row": bv_row, "wproj": Wproj, "bproj_row": bproj_row,
        "cosrep": cosrep, "sinpm": sinpm, "maskt": maskT, "ident": ident, "perm32": perm32,
        "ones_col": ones_col, "selA": selA, "epsrep": epsrep, "onesf": onesf, "onesb": onesb,
        "w1t": np.ascontiguousarray(w1.T),
        "w2t": np.ascontiguousarray(w2.T),
        "b1_row": np.ascontiguousarray(np.asarray(inputs["L0_ffn_b1"]).reshape(1, HID)),
        "b2_row": np.ascontiguousarray(np.asarray(inputs["L0_ffn_b2"]).reshape(1, E)),
        "ln1w_rep": np.ascontiguousarray(
            np.broadcast_to(np.asarray(inputs["L0_ffn_ln1_w"]), (128, E))),
        "ln1b_rep": np.ascontiguousarray(
            np.broadcast_to(np.asarray(inputs["L0_ffn_ln1_b"]), (128, E))),
        "ln2w_rep": np.ascontiguousarray(
            np.broadcast_to(np.asarray(inputs["L0_ffn_ln2_w"]), (128, HID))),
        "ln2b_rep": np.ascontiguousarray(
            np.broadcast_to(np.asarray(inputs["L0_ffn_ln2_b"]), (128, HID))),
    }
    in_maps = []
    for c in range(N_CORES):
        m = dict(shared)
        m["xt"] = np.ascontiguousarray(x[c].T)
        m["xtok"] = np.ascontiguousarray(x[c])
        in_maps.append(m)
    return in_maps


def build_program():
    nc = bass.Bass("TRN2", target_bir_lowering=False,
                   detect_race_conditions=False)
    b = Builder(nc)
    ar = Arena(nc, b)

    def din(name, shape, dtype=F32):
        return nc.dram_tensor(name, list(shape), dtype, kind="ExternalInput").ap()

    xt_d = din("xt", (E, T))
    xtok_d = din("xtok", (T, E))
    wq_d, wk_d, wv_d = din("wq", (E, E)), din("wk", (E, E)), din("wv", (E, E))
    bq_d, bk_d = din("bq", (128, NE)), din("bk", (128, NE))
    bv_row_d = din("bv_row", (1, E))
    wproj_d = din("wproj", (E, E))
    bproj_row_d = din("bproj_row", (1, E))
    cos_d, sin_d = din("cosrep", (128, T)), din("sinpm", (128, T))
    maskt_d = din("maskt", (128, 128))
    ident_d = din("ident", (128, 128))
    perm32_d = din("perm32", (128, 128))
    ones_col_d = din("ones_col", (65, 128))
    eps_d = din("epsrep", (128, 1))
    onesf_d = din("onesf", (128, 128))
    onesb_d = din("onesb", (128, 1))
    selA_d = din("selA", (33, 256))
    w1t_d = din("w1t", (E, HID))
    w2t_d = din("w2t", (HID, E))
    b1_row_d = din("b1_row", (1, HID))
    b2_row_d = din("b2_row", (1, E))
    ln1w_d, ln1b_d = din("ln1w_rep", (128, E)), din("ln1b_rep", (128, E))
    ln2w_d, ln2b_d = din("ln2w_rep", (128, HID)), din("ln2b_rep", (128, HID))

    y_d = nc.dram_tensor("y", [T, E], F32, kind="ExternalOutput").ap()
    taps = {}
    for tname, shape, dt in _DEBUG_TAPS:
        taps[tname] = nc.dram_tensor(tname, list(shape), dt,
                                     kind="ExternalOutput").ap()

    def store_tap(name, src_ap, reads):
        if name in taps:
            b.dma("sync", taps[name], src_ap, reads=reads, tag=f"tap_{name}")

    PO = nc.alloc_psum_tensor("PO", [128, 2048], F32).ap()
    PB = [nc.alloc_psum_tensor(f"PB{i}", [128, 512], F32).ap() for i in range(4)]
    save_base = nc.psum_base
    nc.psum_base = save_base - 2
    PT = [nc.alloc_psum_tensor(f"PT{i}", [128, 128], BF16).ap() for i in range(2)]
    nc.psum_base = save_base

    xtb = [ar.alloc(f"xtb{i}", [128, T], BF16)[0] for i in range(NE)]
    wsb = {w: [ar.alloc(f"{w}sb{i}", [128, E], BF16)[0] for i in range(NE)]
           for w in ("wq", "wk", "wv")}
    qT = [ar.alloc(f"qT{i}", [128, T], BF16)[0] for i in range(NE)]
    kT = [ar.alloc(f"kT{i}", [128, T], BF16)[0] for i in range(NE)]
    vst = [ar.alloc(f"vst{t}", [128, H * (D + 1)], BF16)[0] for t in range(NT)]
    cosr = ar.alloc("cosr", [128, T], BF16)[0]
    sinr = ar.alloc("sinr", [128, T], BF16)[0]
    part = [ar.alloc(f"part{i}", [128, T], BF16)[0] for i in range(2)]
    bq_sb = ar.alloc("bq_sb", [128, NE], F32)[0]
    bk_sb = ar.alloc("bk_sb", [128, NE], F32)[0]
    bv_sb = ar.alloc("bv_sb", [1, E], BF16)[0]
    ones_sb = ar.alloc("ones_sb", [65, 128], BF16)[0]
    eps_sb = ar.alloc("eps_sb", [128, 1], F32)[0]
    onesf_sb = ar.alloc("onesf_sb", [128, 128], F32)[0]
    onesb_sb = ar.alloc("onesb_sb", [128, 1], BF16)[0]
    maskt_sb = ar.alloc("maskt_sb", [128, 128], BF16)[0]
    ident_sb = ar.alloc("ident_sb", [128, 128], BF16)[0]
    perm32_sb = ar.alloc("perm32_sb", [128, 128], BF16)[0]
    selA_sb = ar.alloc("selA_sb", [33, 256], BF16)[0]

    for i in range(NE):
        b.dma("gpsimd", xtb[i], xt_d[i * 128:(i + 1) * 128, :],
              writes=[f"xtb{i}"], tag=f"ld_xtb{i}")
    for wname, d in (("wq", wq_d), ("wk", wk_d), ("wv", wv_d)):
        for i in range(NE):
            b.dma("gpsimd", wsb[wname][i], d[i * 128:(i + 1) * 128, :],
                  writes=[f"{wname}sb{i}"], tag=f"ld_{wname}{i}")
    b.dma("gpsimd", cosr, cos_d, writes=["cosr"])
    b.dma("gpsimd", sinr, sin_d, writes=["sinr"])
    b.dma("gpsimd", bq_sb, bq_d, writes=["bq_sb"])
    b.dma("gpsimd", bk_sb, bk_d, writes=["bk_sb"])
    b.dma("gpsimd", bv_sb, bv_row_d, writes=["bv_sb"])
    b.dma("gpsimd", ones_sb, ones_col_d, writes=["ones_sb"])
    b.dma("gpsimd", eps_sb, eps_d, writes=["eps_sb"])
    b.dma("gpsimd", onesf_sb, onesf_d, writes=["onesf_sb"])
    b.dma("gpsimd", onesb_sb, onesb_d, writes=["onesb_sb"])
    b.dma("gpsimd", maskt_sb, maskt_d, writes=["maskt_sb"])
    b.dma("gpsimd", ident_sb, ident_d, writes=["ident_sb"])
    b.dma("gpsimd", perm32_sb, perm32_d, writes=["perm32_sb"])
    b.dma("gpsimd", selA_sb, selA_d, writes=["selA_sb"])

    for t in range(NT):
        vv = vst[t].rearrange("p (h u) -> p h u", u=D + 1)
        b.op("vector", lambda e, _a=vv: e.memset(_a[:, :, D:D + 1], 1.0),
             writes=[f"vst{t}:ones"], tag=f"ones_v{t}")

    def bcast128(val_ap, dst_ap, rkey, wkeys, tag):
        b.op("tensor",
             lambda e: e.matmul(PB[0][:, 0:1], onesf_sb[0:1, :], val_ap,
                                start=True, stop=True),
             reads=[rkey, "onesf_sb"], writes=["PB0"], tag=f"{tag}_mm")
        b.op("vector", lambda e: (e.tensor_copy(dst_ap, PB[0][:, 0:1]),
                                  e.drain())[-1],
             reads=["PB0"], writes=list(wkeys) + ["PB0"], tag=f"{tag}_ev")

    pb_i = [0]

    def next_pb():
        i = pb_i[0] % 4
        pb_i[0] += 1
        return PB[i], f"PB{i}"

    def qk_block(which, wt, bias_sb, outT, ob):
        for c in range(4):
            ps, pkey = next_pb()
            for kk in range(NE):
                b.op("tensor",
                     lambda e, _ps=ps, _l=wt[kk][:, ob * 128:(ob + 1) * 128],
                            _r=xtb[kk][:, c * 512:(c + 1) * 512],
                            _st=(kk == 0), _sp=(kk == NE - 1):
                     e.matmul(_ps, _l, _r, start=_st, stop=_sp),
                     reads=[f"w{which}sb{kk}", f"xtb{kk}"],
                     writes=[pkey] if (kk == 0 or kk == NE - 1) else [],
                     tag=f"mm_{which}{ob}c{c}k{kk}")
            b.op("scalar",
                 lambda e, _o=outT[ob][:, c * 512:(c + 1) * 512], _i=ps,
                        _b=bias_sb[:, ob:ob + 1]:
                 e.activation(_o, _i, AF.Identity, bias=_b),
                 reads=[pkey, f"b{which}_sb"],
                 writes=[f"{which}T{ob}:pre{c}"],
                 tag=f"evac_{which}{ob}c{c}")

    def rope_block(which, outT, ob, pi):
        pbuf = part[pi]
        pre = [f"{which}T{ob}:pre{c}" for c in range(4)]
        for g in range(4):
            ps, pkey = next_pb()
            b.op("tensor",
                 lambda e, _ps=ps, _r=outT[ob][:, g * 512:(g + 1) * 512]:
                 e.matmul(_ps, perm32_sb, _r, start=True, stop=True),
                 reads=pre + ["perm32_sb"], writes=[pkey],
                 tag=f"pshm_{which}{ob}g{g}")
            b.op("scalar",
                 lambda e, _o=pbuf[:, g * 512:(g + 1) * 512], _i=ps:
                 e.activation(_o, _i, AF.Identity),
                 reads=[pkey],
                 writes=[f"part{pi}:g{g}", f"part{pi}:m", pkey],
                 tag=f"pshe_{which}{ob}g{g}")
        pr = [f"part{pi}:g{g}" for g in range(4)]
        b.op("vector", lambda e, _p=pbuf, _s=sinr: e.tensor_mul(_p, _p, _s),
             reads=pr + ["sinr"], writes=[f"part{pi}:m"],
             tag=f"rope_s_{which}{ob}")
        b.op("vector", lambda e, _q=outT[ob], _c=cosr: e.tensor_mul(_q, _q, _c),
             reads=pre + ["cosr"] + pr, writes=[f"{which}T{ob}:cosd"],
             tag=f"rope_c_{which}{ob}")
        b.op("vector", lambda e, _q=outT[ob], _p=pbuf: e.tensor_add(_q, _q, _p),
             reads=[f"part{pi}:m", f"{which}T{ob}:cosd"],
             writes=[f"{which}T{ob}"], tag=f"rope_a_{which}{ob}")

    for ob in range(NE):
        qk_block("q", wsb["wq"], bq_sb, qT, ob)
        qk_block("k", wsb["wk"], bk_sb, kT, ob)
        if _ROPE:
            rope_block("q", qT, ob, 0)
            rope_block("k", kT, ob, 1)

    for t in range(NT):
        for c in range(2):
            ps, pkey = next_pb()
            for kk in range(NE):
                b.op("tensor",
                     lambda e, _ps=ps, _l=xtb[kk][:, t * 128:(t + 1) * 128],
                            _r=wsb["wv"][kk][:, c * 512:(c + 1) * 512],
                            _st=(kk == 0):
                     e.matmul(_ps, _l, _r, start=_st, stop=False),
                     reads=[f"wvsb{kk}", f"xtb{kk}"],
                     writes=[pkey] if kk == 0 else [],
                     tag=f"mm_v{t}c{c}k{kk}")
            b.op("tensor",
                 lambda e, _ps=ps, _r=bv_sb[:, c * 512:(c + 1) * 512]:
                 e.matmul(_ps, ones_sb[0:1, :], _r, start=False, stop=True),
                 reads=["ones_sb", "bv_sb"], writes=[pkey], tag=f"mm_vb{t}c{c}")
            vv = vst[t].rearrange("p (h u) -> p h u", u=D + 1)
            pv = ps.rearrange("p (h u) -> p h u", u=D)
            b.op("vector",
                 lambda e, _o=vv[:, c * 8:(c + 1) * 8, 0:D], _i=pv:
                 e.tensor_copy(_o, _i),
                 reads=[pkey], writes=[f"vst{t}:c{c}"], tag=f"evac_v{t}c{c}")

    if _PHASES >= 1:
        for ob in range(NE):
            qkeys = [f"qT{ob}"] if _ROPE else [f"qT{ob}:pre{c}" for c in range(4)]
            kkeys = [f"kT{ob}"] if _ROPE else [f"kT{ob}:pre{c}" for c in range(4)]
            store_tap(f"tap_qT{ob}", qT[ob], qkeys)
            store_tap(f"tap_kT{ob}", kT[ob], kkeys)
        for i in range(NE):
            store_tap(f"tap_xtb{i}", xtb[i], [f"xtb{i}"])
        for t in range(NT):
            store_tap(f"tap_v{t}", vst[t],
                      [f"vst{t}:c0", f"vst{t}:c1", f"vst{t}:ones"])

    qdone = lambda a: ([f"qT{a}"] if _ROPE else [f"qT{a}:pre{c}" for c in range(4)])
    kdone = lambda a: ([f"kT{a}"] if _ROPE else [f"kT{a}:pre{c}" for c in range(4)])

    def wquant(wid, srcs, dsts, win, win_key,
               q_sb, ok_q, marep, ok_ma, inv_ma, ok_inv, i32c,
               ok_win=None, ok_i32=None):
        steps = []
        npc = len(srcs)
        numel = float(npc * 128 * 1024)
        steps.append(lambda: b.op(
            "vector", lambda e: e.memset(q_sb[:, 0:1], 0.0),
            writes=_wk(f"wqa{wid}:acc", ok_q), tag=f"wq{wid}_z"))

        def p1step(i):
            wj = win[i % 2]
            wk_ = f"{win_key}{i % 2}"
            b.dma("gpsimd", wj, srcs[i],
                  reads=[f"wqa{wid}:rd{i - 2}"] if i > 1 else [],
                  writes=_wk(wk_, ok_win[i] if i < 2 else None),
                  tag=f"wq{wid}_ld{i}")
            b.op("vector", lambda e, _w=wj: (
                 e.reduce_sum(q_sb[:, 1:2], _w, AX.X, apply_absolute_value=True),
                 e.drain(),
                 e.tensor_tensor(q_sb[:, 0:1], q_sb[:, 0:1], q_sb[:, 1:2],
                                 ALU.add),
                 e.drain())[-1],
                 reads=[wk_, f"wqa{wid}:acc"],
                 writes=[f"wqa{wid}:acc", f"wqa{wid}:rd{i}"], tag=f"wq{wid}_s{i}")
        for i in range(npc):
            steps.append(lambda _i=i: p1step(_i))

        def mastep():
            b.op("tensor",
                 lambda e: e.matmul(PB[0][0:1, 0:1], q_sb[:, 0:1],
                                    onesf_sb[:, 0:1], start=True, stop=True),
                 reads=[f"wqa{wid}:acc", "onesf_sb"], writes=["PB0"],
                 tag=f"wq{wid}_tot")
            b.op("vector", lambda e: (
                 e.tensor_copy(q_sb[0:1, 1:2], PB[0][0:1, 0:1]),
                 e.drain(),
                 e.tensor_scalar(q_sb[0:1, 2:3], q_sb[0:1, 1:2], 1.0 / numel,
                                 EPS, ALU.mult, ALU.max),
                 e.drain(),
                 e.reciprocal(q_sb[0:1, 3:4], q_sb[0:1, 2:3]),
                 e.drain())[-1],
                 reads=["PB0"], writes=[f"wqa{wid}:ma", "PB0"],
                 tag=f"wq{wid}_ma")
            bcast128(q_sb[0:1, 2:3], marep, f"wqa{wid}:ma",
                     _wk(f"marep{wid}", ok_ma), f"wq{wid}_mar")
            bcast128(q_sb[0:1, 3:4], inv_ma, f"wqa{wid}:ma",
                     _wk(f"invma{wid}", ok_inv), f"wq{wid}_invr")
        steps.append(mastep)

        def p2step(i):
            wj = win[i % 2]
            wk_ = f"{win_key}{i % 2}"
            b.dma("gpsimd", wj, srcs[i],
                  reads=[f"wqb{wid}:done{i - 2}"] if i > 1
                        else [f"wqa{wid}:rd{npc - 2 + i}"],
                  writes=[wk_], tag=f"wq{wid}_ld2_{i}")
            dst_ap, dkey = dsts[i]
            for hh in range(2):
                b.op("vector",
                     lambda e, _w=wj[:, hh * 512:(hh + 1) * 512]:
                     e.tensor_scalar(i32c, _w, inv_ma[:, 0:1], None, ALU.mult),
                     reads=[wk_, f"invma{wid}", f"wqb{wid}:q"],
                     writes=_wk("i32s", ok_i32 if (i == 0 and hh == 0) else None),
                     tag=f"wq{wid}_cv{i}.{hh}")
                b.op("vector",
                     lambda e, _o=dst_ap[:, hh * 512:(hh + 1) * 512]:
                     e.tensor_scalar(_o, i32c, 1.0, -1.0, ALU.min, ALU.max),
                     reads=["i32s"],
                     writes=[dkey, f"wqb{wid}:q"] +
                            ([f"wqb{wid}:done{i}"] if hh == 1 else []),
                     tag=f"wq{wid}_cl{i}.{hh}")
        for i in range(npc):
            steps.append(lambda _i=i: p2step(_i))
        return steps




    wq1_steps = []
    wq2_steps = []
    if _PHASES >= 2:
        for i in range(NE):
            ar.free(f"xtb{i}")
        for w in ("wq", "wk", "wv"):
            for i in range(NE):
                ar.free(f"{w}sb{i}")
        ar.free("cosr")
        ar.free("sinr")
        ar.free("part0")
        ar.free("part1")

        sumsP, ok_sp = [], []
        for i in range(2):
            apx, ok = ar.alloc(f"sumsP{i}", [33, 1024], BF16)
            sumsP.append(apx)
            ok_sp.append(ok)
        recb = [ar.alloc(f"recb{i}", [128, 512], BF16)[0] for i in range(2)]
        Esl, ok_E = [], []
        for i in range(4):
            apx, ok = ar.alloc(f"Esl{i}", [128, 1024], BF16)
            Esl.append(apx)
            ok_E.append(ok)
        attnT, ok_at = [], []
        for a2 in range(NE):
            apx, ok = ar.alloc(f"attnT{a2}", [128, T], BF16)
            attnT.append(apx)
            ok_at.append(ok)
        negC, ok_nc = [], []
        for i in range(2):
            apx, ok = ar.alloc(f"negC{i}", [128, 1], F32)
            negC.append(apx)
            ok_nc.append(ok)
        sqb_sb, ok_sqb = ar.alloc("sqb_sb", [128, T], BF16)
        nrm_sb, ok_nrm = ar.alloc("nrm_sb", [1, 12], F32)

        def cprep(a):
            for which, src_t, dk, col0 in (("q", qT[a], qdone(a), 0),
                                           ("k", kT[a], kdone(a), 4)):
                b.op("vector", lambda e, _s=src_t: e.tensor_tensor(
                     sqb_sb, _s, _s, ALU.mult),
                     reads=dk + ["sqb:rd"],
                     writes=_wk("sqb", ok_sqb if (a == 0 and which == "q")
                                else None), tag=f"sq{which}{a}")
                for cch in range(4):
                    pbn = PB[cch % 2]
                    pkn = f"PB{cch % 2}"
                    b.op("tensor",
                         lambda e, _p=pbn[0:1, :],
                                _r=sqb_sb[:, cch * 512:(cch + 1) * 512]:
                         e.matmul(_p, onesb_sb, _r, start=True, stop=True),
                         reads=["sqb", "onesb_sb"], writes=[pkn],
                         tag=f"nq{which}{a}.{cch}")
                    b.op("vector",
                         lambda e, _o=nrm_sb[:, col0 + cch:col0 + cch + 1],
                                _p=pbn[0:1, :]:
                         e.reduce_max(_o, _p, AX.X),
                         reads=[pkn], writes=[f"nrm:{which}{cch}", pkn,
                                              "sqb:rd"],
                         tag=f"nr{which}{a}.{cch}")
            b.op("vector", lambda e: (
                 e.drain(),
                 e.reduce_max(nrm_sb[:, 8:9], nrm_sb[:, 0:4], AX.X),
                 e.reduce_max(nrm_sb[:, 9:10], nrm_sb[:, 4:8], AX.X),
                 e.drain(),
                 e.tensor_scalar(nrm_sb[:, 10:11], nrm_sb[:, 8:9],
                                 nrm_sb[:, 9:10], None, ALU.mult),
                 e.drain())[-1],
                 reads=[f"nrm:{w2}{cch}" for w2 in ("q", "k")
                        for cch in range(4)],
                 writes=["nrm:prod"], tag=f"cb{a}")
            b.op("scalar", lambda e: (
                 e.activation(nrm_sb[:, 11:12], nrm_sb[:, 10:11], AF.Ln),
                 e.activation(nrm_sb[:, 11:12], nrm_sb[:, 11:12], AF.Exp,
                              scale=0.5))[-1],
                 reads=["nrm:prod"], writes=["nrm:sqrt"], tag=f"cs{a}")
            b.op("vector", lambda e: (e.tensor_scalar(
                 nrm_sb[:, 10:11], nrm_sb[:, 11:12], -0.125, None, ALU.mult),
                 e.drain())[-1],
                 reads=["nrm:sqrt"], writes=["nrm:negc"], tag=f"cn{a}")
            bcast128(nrm_sb[:, 10:11], negC[a % 2], "nrm:negc",
                     _wk(f"negC{a % 2}", ok_nc[a % 2] if a < 2 else None),
                     f"crep{a}")

        cprep(0)
        for a in range(NE):
            for qh in range(2):
                kt_max = 8 * qh + 7
                sched = [("sc", 0)]
                for kt in range(1, kt_max + 1):
                    sched.append(("sc", kt))
                    sched.append(("pv", kt - 1))
                sched.append(("pv", kt_max))

                def chunk_list(kt, qh=qh):
                    out = []
                    for lc in range(2):
                        c = 2 * qh + lc
                        if 4 * c + 3 < kt:
                            continue
                        off = max(0, kt * 128 - c * 512)
                        out.append((lc, c, off, c == kt // 4))
                    return out

                for kind, kt in sched:
                    if kind == "pv" and a >= 1 and wq1_steps:
                        wq1_steps.pop(0)()
                    for h01 in range(2):
                        h = 2 * a + h01
                        slot = (2 * kt + h01) % 4
                        pbi = 2 * (kt % 2) + h01
                        pb = PB[pbi]
                        pkey = f"PB{pbi}"
                        for lc, c, off, diag in chunk_list(kt):
                            n = 512 - off
                            ek = f"Esl{slot}:c{lc}"
                            if kind == "sc":
                                if diag:
                                    b.op("tensor",
                                         lambda e, _p=pb[:, off:off + 128]:
                                         e.matmul(_p, maskt_sb, ident_sb,
                                                  start=True, stop=False),
                                         reads=["maskt_sb", "ident_sb"],
                                         writes=[pkey],
                                         tag=f"msk{a}.{qh}.{kt}.{h01}")
                                b.op("tensor",
                                     lambda e, _p=pb[:, off:512],
                                            _l=kT[a][64 * h01:64 * h01 + 64,
                                                     kt * 128:(kt + 1) * 128],
                                            _r=qT[a][64 * h01:64 * h01 + 64,
                                                     c * 512 + off:(c + 1) * 512],
                                            _st=not diag, _tp=(64 * h01, 0):
                                     e.matmul(_p, _l, _r, start=_st, stop=True,
                                              tile_position=_tp),
                                     reads=qdone(a) + kdone(a),
                                     writes=[pkey],
                                     tag=f"sc{a}.{qh}.{kt}.{h01}.{c}")
                                b.op("scalar",
                                     lambda e, _o=Esl[slot][:, lc * 512:lc * 512 + n],
                                            _i=pb[:, off:512], _b=negC[a % 2]:
                                     e.activation(_o, _i, AF.Exp, bias=_b,
                                                  scale=0.125),
                                     reads=[pkey, f"negC{a % 2}"],
                                     writes=[ek],
                                     tag=f"exp{a}.{qh}.{kt}.{h01}.{c}")
                            else:
                                st = (kt == 0)
                                sp = (kt == min(4 * c + 3, kt_max))
                                b.op("tensor",
                                     lambda e,
                                            _p=PO[0:65, h01 * 1024 + lc * 512 + off:
                                                  h01 * 1024 + (lc + 1) * 512],
                                            _l=vst[kt][:, h * 65:h * 65 + 65],
                                            _r=Esl[slot][:, lc * 512:lc * 512 + n],
                                            _st=st, _sp=sp:
                                     e.matmul(_p, _l, _r, start=_st, stop=_sp),
                                     reads=[ek, f"vst{kt}:c0", f"vst{kt}:c1",
                                            f"vst{kt}:ones"],
                                     writes=[f"PO:{h01}.{lc}"],
                                     tag=f"pv{a}.{qh}.{kt}.{h01}.{c}")
                for h01 in range(2):
                    h = 2 * a + h01
                    b.op("scalar",
                         lambda e, _o=attnT[a][64 * h01:64 * h01 + 64,
                                               qh * 1024:(qh + 1) * 1024],
                                _i=PO[0:64, h01 * 1024:(h01 + 1) * 1024]:
                         e.activation(_o, _i, AF.Identity),
                         reads=[f"PO:{h01}.0", f"PO:{h01}.1"],
                         writes=_wk(f"attnT{a}:u{h01}.{qh}",
                                    ok_at[a] if (h01 == 0 and qh == 0) else None),
                         tag=f"oev{a}.{qh}.{h01}")
                    b.op("scalar",
                         lambda e, _o=sumsP[h01][qh * 32:qh * 32 + 1, :],
                                _i=PO[64:65, h01 * 1024:(h01 + 1) * 1024]:
                         e.activation(_o, _i, AF.Identity),
                         reads=[f"PO:{h01}.0", f"PO:{h01}.1", f"sums{h01}:used"],
                         writes=_wk(f"sums:{h01}.{qh}",
                                    ok_sp[h01] if a == 0 and qh == 0 else None,
                                    f"PO:{h01}.0", f"PO:{h01}.1"),
                         tag=f"sev{a}.{qh}.{h01}")

            if a + 1 < NE:
                cprep(a + 1)
            for lc in range(4):
                qh2 = lc // 2
                col = (lc % 2) * 512
                po = PO[:, (lc % 2) * 512:(lc % 2) * 512 + 512]
                pkey = f"PO:0.{lc % 2}"
                for h01 in range(2):
                    sl = selA_sb[qh2 * 32:qh2 * 32 + 1,
                                 h01 * 128:(h01 + 1) * 128]
                    sr = sumsP[h01][qh2 * 32:qh2 * 32 + 1, col:col + 512]
                    b.op("tensor",
                         lambda e, _p=po, _l=sl, _r=sr, _st=(h01 == 0):
                         e.matmul(_p, _l, _r, start=_st, stop=(h01 == 1)),
                         reads=[f"sums:{h01}.{qh2}", "selA_sb"],
                         writes=[pkey], tag=f"rep{a}.{lc}.{h01}")
                def _recip(e, _o=recb[lc % 2], _p=po):
                    with nc.allow_low_precision(reason="softmax 1/sum bf16"):
                        e.reciprocal(_o, _p)
                    return e.drain()
                b.op("vector", _recip,
                     reads=[pkey, f"recb{lc % 2}:used"],
                     writes=[f"recb{lc % 2}", pkey], tag=f"rcp{a}.{lc}")
                b.op("vector",
                     lambda e, _o=attnT[a][:, lc * 512:(lc + 1) * 512],
                            _r=recb[lc % 2]:
                     e.tensor_tensor(_o, _o, _r, ALU.mult),
                     reads=[f"recb{lc % 2}"] + [f"attnT{a}:u{h01}.{qh}"
                                                for h01 in range(2)
                                                for qh in range(2)],
                     writes=[f"attnT{a}:n{lc}", f"recb{lc % 2}:used"] +
                            (["sums0:used", "sums1:used"] if lc == 3 else []),
                     tag=f"div{a}.{lc}")
            if a == 0:
                win1a, ok_w1a = ar.alloc("win1a", [128, 1024], F32)
                win1b, ok_w1b = ar.alloc("win1b", [128, 1024], F32)
                wq1aux, ok_q1 = ar.alloc("wq1aux", [128, 132], F32)
                ma1rep, ok_ma1 = ar.alloc("ma1rep", [128, 1], F32)
                invma1, ok_inv1 = ar.alloc("invma1", [128, 1], F32)
                i32w, ok_i32w = ar.alloc("i32w", [128, 512], I32)
                w1q, ok_w1q = [], []
                for i in range(NE):
                    apx, ok = ar.alloc(f"w1q{i}", [128, HID], FP8)
                    w1q.append(apx)
                    ok_w1q.append(ok)
                w1srcs, w1dsts = [], []
                for i in range(NE):
                    for cc in range(4):
                        w1srcs.append(w1t_d[i * 128:(i + 1) * 128,
                                            cc * 1024:(cc + 1) * 1024])
                        w1dsts.append((w1q[i][:, cc * 1024:(cc + 1) * 1024],
                                       f"w1q:{i}.{cc}"))
                wq1_steps.extend(wquant(
                    1, w1srcs, w1dsts, [win1a, win1b], "win1",
                    wq1aux, ok_q1, ma1rep, ok_ma1, invma1, ok_inv1, i32w,
                    ok_win=[ok_w1a, ok_w1b], ok_i32=ok_i32w))

        if _PHASES == 2:
            for a in range(NE):
                store_tap(f"tap_attnT{a}", attnT[a],
                          [f"attnT{a}:n{lc}" for lc in range(4)])
            store_tap("tap_sums0", sumsP[0], [f"sums:0.{qh}" for qh in range(2)])
            store_tap("tap_sums1", sumsP[1], [f"sums:1.{qh}" for qh in range(2)])
            store_tap("tap_negC0", negC[0], ["negC0"])
            store_tap("tap_negC1", negC[1], ["negC1"])
            store_tap("tap_nrm", nrm_sb, ["nrm:negc"])

    def atdone(fb):
        return [f"attnT{fb}:n{lc}" for lc in range(4)]

    if _PHASES >= 2:
        while wq1_steps:
            wq1_steps.pop(0)()

    if _PHASES >= 3:
        for a in range(NE):
            ar.free(f"qT{a}")
            ar.free(f"kT{a}")
        for t in range(NT):
            ar.free(f"vst{t}")
        for i in range(4):
            ar.free(f"Esl{i}")
        ar.free("sumsP0", keys=["sums:0", "sums0:"])
        ar.free("sumsP1", keys=["sums:1", "sums1:"])
        ar.free("recb0")
        ar.free("recb1")
        ar.free("bv_sb")
        ar.free("bq_sb")
        ar.free("bk_sb")
        ar.free("maskt_sb")
        ar.free("sqb_sb", keys=["sqb"])
        ar.free("nrm_sb", keys=["nrm"])
        ar.free("selA_sb")
        ar.free("onesb_sb")
        ar.free("negC0")
        ar.free("negC1")

        wproj_sb, ok_wp = [], []
        for i in range(NE):
            apx, ok = ar.alloc(f"wproj{i}", [128, E], BF16)
            wproj_sb.append(apx)
            ok_wp.append(ok)
        bproj_sb, ok_bp = ar.alloc("bproj_sb", [1, E], BF16)
        for i in range(NE):
            b.dma("gpsimd", wproj_sb[i], wproj_d[i * 128:(i + 1) * 128, :],
                  writes=_wk(f"wproj{i}", ok_wp[i]), tag=f"ld_wproj{i}")
        b.dma("gpsimd", bproj_sb, bproj_row_d, writes=_wk("bproj_sb", ok_bp))

        x2_t, ok_x2 = ar.alloc("x2", [128, NT * E], F32)
        x1b, ok_x1 = [], []
        for i in range(2):
            apx, ok = ar.alloc(f"x1b{i}", [128, E], F32)
            x1b.append(apx)
            ok_x1.append(ok)
        st6, ok_st6 = ar.alloc("st6", [128, 12], F32)
        wrkA, ok_ag2 = ar.alloc("wrkA", [128, 12], F32)
        lnsc, ok_lnsc = ar.alloc("lnsc", [128, 3], F32)

        xtok_r = xtok_d.rearrange("(a p) e -> p a e", p=128)
        for t in range(NT):
            b.dma("gpsimd", x1b[t % 2], xtok_r[:, t, :],
                  reads=[f"x1u:{t % 2}"],
                  writes=_wk(f"x1ld:{t % 2}", ok_x1[t % 2] if t < 2 else None),
                  tag=f"ld_x1_{t}")
            x1 = x1b[t % 2]
            for c in range(2):
                pb = PB[c]
                pkey = f"PB{c}"
                for fb in range(NE):
                    b.op("tensor",
                         lambda e, _p=pb, _l=attnT[fb][:, t * 128:(t + 1) * 128],
                                _r=wproj_sb[fb][:, c * 512:(c + 1) * 512],
                                _st=(fb == 0):
                         e.matmul(_p, _l, _r, start=_st, stop=False),
                         reads=[f"wproj{fb}"] + atdone(fb),
                         writes=[pkey] if fb == 0 else [],
                         tag=f"prj{t}.{c}.{fb}")
                b.op("tensor",
                     lambda e, _p=pb, _r=bproj_sb[:, c * 512:(c + 1) * 512]:
                     e.matmul(_p, ones_sb[0:1, :], _r, start=False, stop=True),
                     reads=["ones_sb", "bproj_sb"], writes=[pkey],
                     tag=f"prjb{t}.{c}")
                b.op("vector",
                     lambda e, _o=x1[:, c * 512:(c + 1) * 512], _p=pb:
                     e.tensor_tensor(_o, _o, _p, ALU.add),
                     reads=[pkey, f"x1ld:{t % 2}"],
                     writes=[f"x1:{t % 2}.{c}"],
                     tag=f"prjev{t}.{c}")
            _ln_group(b, "lnA", x1, st6, wrkA, lnsc, 0, 1,
                      [f"x1:{t % 2}.{c}" for c in range(2)], f"pln{t}", 2,
                      extra_stats_w=[k for k in (ok_st6, ok_ag2, ok_lnsc)
                                     if k] if t == 0 else (), eps_ap=eps_sb,
                      pcol=2)
            _ln_center(b, "lnA", x1, x2_t[:, t * E:(t + 1) * E], lnsc, 0, 2,
                       [f"x1:{t % 2}.{c}" for c in range(2)],
                       _wk(f"x2:{t}", ok_x2 if t == 0 else None,
                           f"x1u:{t % 2}"), f"pln{t}")
            if t == 0 and _PHASES >= 4:
                b.seed_pseudo("__ov_w2w0", b.accessors("win10"))
                b.seed_pseudo("__ov_w2w1", b.accessors("win11"))
                wq2aux, ok_q2 = ar.alloc("wq2aux", [128, 132], F32)
                ma2rep, ok_ma2 = ar.alloc("ma2rep", [128, 1], F32)
                invma2, ok_inv2 = ar.alloc("invma2", [128, 1], F32)
                w2q, ok_w2q = [], []
                for i in range(NH):
                    apx, ok = ar.alloc(f"w2q{i}", [128, E], FP8)
                    w2q.append(apx)
                    ok_w2q.append(ok)
                w2srcs = [w2t_d[i * 128:(i + 1) * 128, :] for i in range(NH)]
                w2dsts = [(w2q[i], f"w2q:{i}.0") for i in range(NH)]
                wq2_steps.extend(wquant(
                    2, w2srcs, w2dsts,
                    [win1a[:, 0:E], win1b[:, 0:E]], "win2",
                    wq2aux, ok_q2, ma2rep, ok_ma2, invma2, ok_inv2, i32w,
                    ok_win=["__ov_w2w0", "__ov_w2w1"], ok_i32=None))
            for _ in range(5 if t >= 1 else 0):
                if wq2_steps:
                    wq2_steps.pop(0)()

        while wq2_steps:
            wq2_steps.pop(0)()
        store_tap("tap_x2", x2_t, [f"x2:{t}" for t in range(NT)])
        store_tap("tap_ag2", wrkA, ["lnA"])
        store_tap("tap_lnsc", lnsc, ["lnA"])
        store_tap("tap_x1b", x1b[15 % 2], ["lnA", "x1:1.0", "x1:1.1"])

    if _PHASES >= 5:
        for i in range(NE):
            ar.free(f"wproj{i}")
        ar.free("win1a", keys=["win10", "win20"])
        ar.free("win1b", keys=["win11", "win21"])
        ar.free("wq1aux", keys=["wqa1", "wqb1"])
        ar.free("x1b0", keys=["x1:0", "x1ld:0", "x1u:0"])
        ar.free("x1b1", keys=["x1:1", "x1ld:1", "x1u:1"])
        for a in range(NE):
            ar.free(f"attnT{a}")
        ar.free("bproj_sb")
        ar.free("wq2aux", keys=["wqa2", "wqb2"])
        ar.free("st6", keys=["lnA"])
        ar.free("wrkA", keys=["lnA"])
        ar.free("lnsc", keys=["lnA"])
        ar.free("onesf_sb")

        hact_b, ok_hact = [], []
        for i in range(2):
            apx, ok = ar.alloc(f"hact{i}", [128, HID], F32)
            hact_b.append(apx)
            ok_hact.append(ok)
        ln2w_sb, ok_l2w = ar.alloc("ln2w_sb", [128, HID], BF16)
        ln2b_sb, ok_l2b = ar.alloc("ln2b_sb", [128, HID], BF16)
        qhT, ok_qhT = ar.alloc("qhT", [128, HID], BF16)
        y1b, ok_y1 = ar.alloc("y1b", [128, E], F32)
        t3, ok_t3 = y1b, None
        ln1w_sb, ok_l1w = ar.alloc("ln1w_sb", [128, E], BF16)
        ln1b_sb, ok_l1b = ar.alloc("ln1b_sb", [128, E], BF16)
        x2qT_b, ok_x2qT = [], []
        for i in range(2):
            apx, ok = ar.alloc(f"x2qT{i}", [128, E], BF16)
            x2qT_b.append(apx)
            ok_x2qT.append(ok)
        qbuf_b, ok_qbuf_b = [], []
        for i in range(2):
            apx, ok = ar.alloc(f"qbuf{i}", [128, 512], BF16)
            qbuf_b.append(apx)
            ok_qbuf_b.append(ok)
        b1_sb, ok_b1 = ar.alloc("b1_sb", [65, 2048], BF16)
        st6f, ok_st6f = ar.alloc("st6f", [128, 48], F32)
        wrkB, ok_ag2f = ar.alloc("wrkB", [128, 48], F32)
        lnscf, ok_lnscf = ar.alloc("lnscf", [128, 13], F32)
        i32a, ok_i32a = i32w, None
        st6_l2, st6_l1, st6_f = st6f, st6f[:, 0:12], st6f[:, 0:12]
        wrk_l2, wrk_l1, wrk_f = wrkB[:, 0:24], wrkB[:, 24:36], wrkB[:, 36:48]

        b.dma("gpsimd", ln1w_sb, ln1w_d, writes=_wk("ln1w_sb", ok_l1w))
        b.dma("gpsimd", ln1b_sb, ln1b_d, writes=_wk("ln1b_sb", ok_l1b))
        b.dma("gpsimd", ln2w_sb, ln2w_d, writes=_wk("ln2w_sb", ok_l2w))
        b.dma("gpsimd", ln2b_sb, ln2b_d, writes=_wk("ln2b_sb", ok_l2b))
        b.dma("gpsimd", b1_sb[0:1, :], b1_row_d[:, 0:2048],
              writes=_wk("b1_sb:raw", ok_b1))
        b.dma("gpsimd", b1_sb[32:33, :], b1_row_d[:, 2048:4096],
              writes=["b1_sb:raw2"])
        b.dma("gpsimd", b1_sb[64:65, 0:1024], b2_row_d[:, 0:1024],
              writes=["b2_sb:raw"])
        b.op("vector", lambda e: e.tensor_scalar(
                 b1_sb[0:33, :], b1_sb[0:33, :], invma1[0:33, 0:1],
                 None, ALU.mult),
             reads=["b1_sb:raw", "b1_sb:raw2", "invma1"], writes=["b1_sb"],
             tag="b1s")
        b.op("vector", lambda e: e.tensor_scalar(
                 b1_sb[64:65, 0:1024], b1_sb[64:65, 0:1024],
                 invma2[64:65, 0:1], None, ALU.mult),
             reads=["b2_sb:raw", "invma2"], writes=["b2_sb"],
             tag="b2s")

        for _k in ("PO:m0", "PO:m1", "PO:m2", "PO:m3"):
            b.seed_pseudo(_k, b.accessors("PO"))
        b.seed_pseudo("PT0", b.accessors("PB2"))
        b.seed_pseudo("PT1", b.accessors("PB3"))
        _lnB_deps = [op for k in (ok_st6f, ok_ag2f, ok_lnscf) if k
                     for op in b.res.get(k, [None, []])[1]]
        for _g in ("lnL1", "lnL2", "lnF"):
            b.seed_pseudo(_g, _lnB_deps)

        okd = {"t3": ok_t3, "qbuf0": ok_qbuf_b[0], "qbuf1": ok_qbuf_b[1],
               "qhT": ok_qhT, "y1b": ok_y1}
        used = set()

        def fk2(name):
            ok = okd.get(name)
            if ok and name not in used:
                used.add(name)
                return ok
            return None

        def okbuf(oklist, i, nm):
            if oklist[i] and nm not in used:
                used.add(nm)
                return oklist[i]
            return None

        def quant_scales(t, src, src_keys, grp, c_dq, c_q, qkey, tag):
            b.op("vector", lambda e: (
                 e.reduce_max(lnscf[:, c_dq:c_dq + 1], src, AX.X,
                              apply_absolute_value=True),
                 e.drain(),
                 e.reciprocal(lnscf[:, c_q:c_q + 1], lnscf[:, c_dq:c_dq + 1]),
                 e.tensor_scalar(lnscf[:, c_dq:c_dq + 1],
                                 lnscf[:, c_dq:c_dq + 1], 1.0 / 127.0,
                                 None, ALU.mult),
                 e.tensor_scalar(lnscf[:, c_q:c_q + 1],
                                 lnscf[:, c_q:c_q + 1], 127.0, None,
                                 ALU.mult))[-1],
                 reads=list(src_keys) + [grp], writes=[qkey, grp], tag=tag)

        def quant_cvdq(src, ch, c_dq, c_q, qkey, src_keys, trtag, grp):
            b.op("scalar",
                 lambda e, _i=src[:, ch * 512:(ch + 1) * 512]:
                 e.activation(i32a, _i, AF.Identity,
                              scale=lnscf[:, c_q:c_q + 1]),
                 reads=list(src_keys) + [qkey, "i32s", grp], writes=["i32s"],
                 tag=f"{trtag}cv.{ch}")
            qb = qbuf_b[ch % 2]
            qbk = f"qbuf{ch % 2}"
            b.op("scalar",
                 lambda e, _o=qb: e.activation(_o, i32a, AF.Identity,
                                               scale=lnscf[:, c_dq:c_dq + 1]),
                 reads=["i32s", qkey, qbk, grp],
                 writes=_wk(qbk, fk2(qbk)), tag=f"{trtag}dq.{ch}")

        def quant_trs(dst, dkey, ok_dst, ch, trtag):
            qb = qbuf_b[ch % 2]
            qbk = f"qbuf{ch % 2}"
            for b4 in range(4):
                bb = ch * 4 + b4
                pt = PT[bb % 2]
                ptk = f"PT{bb % 2}"
                b.op("tensor",
                     lambda e, _p=pt, _i=qb[:, b4 * 128:(b4 + 1) * 128]:
                     e.transpose(_p, _i, ident_sb),
                     reads=[qbk, "ident_sb", ptk + ":rd"],
                     writes=[ptk], tag=f"{trtag}_{bb}")
                b.op("scalar",
                     lambda e, _o=dst[:, bb * 128:(bb + 1) * 128], _p=pt:
                     e.activation(_o, _p, AF.Identity),
                     reads=[ptk, dkey + ":mm"],
                     writes=_wk(dkey, ok_dst, ptk + ":rd"),
                     tag=f"{trtag}e.{bb}")

        def quant_tr(t, src, src_keys, grp, c_dq, c_q, qkey, dst, dkey,
                     ok_dst, nch, trtag, inter=None):
            quant_cvdq(src, 0, c_dq, c_q, qkey, src_keys, trtag, grp)
            for ch in range(nch):
                quant_trs(dst, dkey, ok_dst if ch == 0 else None, ch, trtag)
                if ch + 1 < nch:
                    quant_cvdq(src, ch + 1, c_dq, c_q, qkey, src_keys, trtag,
                               grp)
                if inter is not None:
                    inter(ch)

        def ln1_q1_tr1(t):
            x2sl = x2_t[:, t * E:(t + 1) * E]
            x2qTt = x2qT_b[t % 2]
            _ln_group(b, "lnL1", x2sl, st6_l1, wrk_l1, lnscf, 0, 1,
                      [f"x2:{t}"], f"l1n{t}", 2, eps_ap=eps_sb, pcol=10)
            _ln_center(b, "lnL1", x2sl, t3, lnscf, 0, 10,
                       [f"x2:{t}", "y1:rd"],
                       _wk("t3", fk2("y1b"), "y1:0", "y1:1"), f"l1n{t}")
            b.op("vector", lambda e: e.tensor_tensor(t3, t3, ln1w_sb, ALU.mult),
                 reads=["t3", "ln1w_sb"], writes=["t3"], tag=f"l1w{t}")
            b.op("vector", lambda e: e.tensor_tensor(t3, t3, ln1b_sb, ALU.add),
                 reads=["t3", "ln1b_sb"], writes=["t3"], tag=f"l1b{t}")
            quant_scales(t, t3, ["t3"], "lnL1", 2, 3, f"q1s{t}", f"l1qs{t}")
            quant_cvdq(t3, 0, 2, 3, f"q1s{t}", ["t3"], f"tr1_{t}", "lnL1")
            quant_cvdq(t3, 1, 2, 3, f"q1s{t}", ["t3"], f"tr1_{t}", "lnL1")

        def ln1_tr(t):
            x2qTt = x2qT_b[t % 2]
            ok = okbuf(ok_x2qT, t % 2, f"x2qT{t % 2}")
            quant_trs(x2qTt, f"x2qT{t % 2}", ok, 0, f"tr1_{t}")
            quant_trs(x2qTt, f"x2qT{t % 2}", None, 1, f"tr1_{t}")

        def mm1_mm(t, n, pb=None, pkey=None):
            x2qTt = x2qT_b[t % 2]
            if pb is None:
                pb = PB[n % 2]
                pkey = f"PB{n % 2}"
            for kb in range(NE):
                b.op("tensor",
                     lambda e, _p=pb, _l=x2qTt[:, kb * 128:(kb + 1) * 128],
                            _r=w1q[kb][:, n * 512:(n + 1) * 512],
                            _st=(kb == 0):
                     e.matmul(_p, _l, _r, start=_st, stop=False),
                     reads=[f"x2qT{t % 2}"] +
                           [f"w1q:{kb}.{cc}" for cc in range(4)],
                     writes=[pkey] if kb == 0 else [],
                     tag=f"mm1_{t}.{n}.{kb}")
            b1r = (b1_sb[0:1, n * 512:(n + 1) * 512] if n < 4 else
                   b1_sb[32:33, (n - 4) * 512:(n - 3) * 512])
            onesr = ones_sb[0:1, :] if n < 4 else ones_sb[32:33, :]
            b.op("tensor",
                 lambda e, _p=pb, _r=b1r, _l=onesr:
                 e.matmul(_p, _l, _r, start=False, stop=True),
                 reads=["ones_sb", "b1_sb"], writes=[pkey],
                 tag=f"mm1b_{t}.{n}")

        def mm1_gel(t, n, pb=None, pkey=None):
            hact = hact_b[t % 2]
            if pb is None:
                pb = PB[n % 2]
                pkey = f"PB{n % 2}"
            b.op("scalar",
                 lambda e, _o=hact[:, n * 512:(n + 1) * 512], _p=pb:
                 e.activation(_o, _p, AF.Gelu, scale=ma1rep[:, 0:1]),
                 reads=[pkey, "marep1", f"hact{t % 2}:rd"],
                 writes=_wk(f"hact{t % 2}:{n}",
                            okbuf(ok_hact, t % 2, f"hact{t % 2}"),
                            f"hact{t % 2}:n"),
                 tag=f"gel{t}.{n}")

        def mm1_block(t, n):
            mm1_mm(t, n)
            mm1_gel(t, n)

        def mm1_gelu(t):
            for n in range(8):
                mm1_block(t, n)

        def ln2_part1(t):
            hact = hact_b[t % 2]
            hreads = [f"hact{t % 2}:{n}" for n in range(8)]
            _ln_group(b, "lnL2", hact, st6_l2, wrk_l2, lnscf, 6, 7,
                      hreads, f"l2n{t}", 8, eps_ap=eps_sb, pcol=12)

        def p2_head(t):
            hact = hact_b[t % 2]
            hreads = [f"hact{t % 2}:{n}" for n in range(8)]
            _ln_center(b, "lnL2", hact, hact, lnscf, 6, 12, hreads,
                       [f"hact{t % 2}:n"], f"l2n{t}")
            b.op("vector", lambda e, _h=hact:
                 e.tensor_tensor(_h, _h, ln2w_sb, ALU.mult),
                 reads=[f"hact{t % 2}:n", "ln2w_sb"], writes=[f"hact{t % 2}:n"],
                 tag=f"l2w{t}")
            b.op("vector", lambda e, _h=hact:
                 e.tensor_tensor(_h, _h, ln2b_sb, ALU.add),
                 reads=[f"hact{t % 2}:n", "ln2b_sb"],
                 writes=[f"hact{t % 2}:n", f"hact{t % 2}:rd"],
                 tag=f"l2b{t}")
            quant_scales(t, hact, [f"hact{t % 2}:n"], "lnL2", 8, 9,
                         f"q2s{t}", f"l2qs{t}")
            quant_cvdq(hact, 0, 8, 9, f"q2s{t}", [f"hact{t % 2}:n"],
                       f"tr2_{t}", "lnL2")
            quant_cvdq(hact, 1, 8, 9, f"q2s{t}", [f"hact{t % 2}:n"],
                       f"tr2_{t}", "lnL2")

        def p2_chunks(t):
            hact = hact_b[t % 2]
            ok = fk2("qhT")
            tailp = [(PB[0], "PB0"), (PB[1], "PB1"),
                     (PO[:, 1024:1536], "PO:m2"), (PO[:, 1536:2048], "PO:m3")]
            for ch in range(8):
                quant_trs(qhT, "qhT", ok if ch == 0 else None, ch, f"tr2_{t}")
                if ch + 2 < 8:
                    quant_cvdq(hact, ch + 2, 8, 9, f"q2s{t}",
                               [f"hact{t % 2}:n"], f"tr2_{t}", "lnL2")
                if t + 2 < NT:
                    mm1_mm(t + 2, ch, *tailp[ch % 4])
                    if ch == 3:
                        for j in range(4):
                            mm1_gel(t + 2, j, *tailp[j])
            if t + 2 < NT:
                for j in range(4):
                    mm1_gel(t + 2, 4 + j, *tailp[j])

        def mm2_out(t):
            x2sl = x2_t[:, t * E:(t + 1) * E]
            for kb in range(NH):
                for c in range(2):
                    po = PO[:, c * 512:(c + 1) * 512]
                    pkey = f"PO:m{c}"
                    b.op("tensor",
                         lambda e, _p=po, _l=qhT[:, kb * 128:(kb + 1) * 128],
                                _r=w2q[kb][:, c * 512:(c + 1) * 512],
                                _st=(kb == 0):
                         e.matmul(_p, _l, _r, start=_st, stop=False),
                         reads=["qhT", f"w2q:{kb}.0"],
                         writes=[pkey] if kb == 0 else [],
                         tag=f"mm2_{t}.{c}.{kb}")
            for c in range(2):
                po = PO[:, c * 512:(c + 1) * 512]
                pkey = f"PO:m{c}"
                b2r = b1_sb[64:65, c * 512:(c + 1) * 512]
                onesr2 = ones_sb[64:65, :]
                b.op("tensor",
                     lambda e, _p=po, _r=b2r, _l=onesr2:
                     e.matmul(_p, _l, _r, start=False, stop=True),
                     reads=["ones_sb", "b2_sb"], writes=[pkey],
                     tag=f"mm2b_{t}.{c}")
                b.op("vector",
                     lambda e, _o=y1b[:, c * 512:(c + 1) * 512], _p=po,
                            _x=x2sl[:, c * 512:(c + 1) * 512]:
                     e.scalar_tensor_tensor(_o, _p, ma2rep[:, 0:1], _x,
                                            ALU.mult, ALU.add),
                     reads=[pkey, "marep2", f"x2:{t}", "y1:rd"],
                     writes=[f"y1:{c}", "t3"],
                     tag=f"m2ev{t}.{c}")
            _ln_group(b, "lnF", y1b, st6_f, wrk_f, lnscf, 4, 5,
                      [f"y1:{c}" for c in range(2)], f"fln{t}", 2,
                      eps_ap=eps_sb, pcol=11)
            _ln_center(b, "lnF", y1b, y1b, lnscf, 4, 11,
                       [f"y1:{c}" for c in range(2)],
                       _wk(f"ob:{t}", None, "y1:rd"), f"fln{t}")
            b.dma("sync", y_d[t * 128:(t + 1) * 128, :], y1b,
                  reads=[f"ob:{t}"], writes=["y1:rd"], tag=f"st_y{t}")

        if _P5_SEQ:
            for t in range(NT):
                ln1_q1_tr1(t)
                ln1_tr(t)
                mm1_gelu(t)
                ln2_part1(t)
                p2_head(t)
                p2_chunks(t)
                mm2_out(t)
        else:
            ln1_q1_tr1(0)
            ln1_tr(0)
            mm1_gelu(0)
            ln1_q1_tr1(1)
            ln1_tr(1)
            ln2_part1(0)
            mm1_gelu(1)
            ln1_q1_tr1(2)
            ln1_tr(2)
            p2_head(0)
            for t in range(NT):
                p2_chunks(t)
                if t + 3 < NT:
                    ln1_q1_tr1(t + 3)
                    ln1_tr(t + 3)
                if t + 1 < NT:
                    ln2_part1(t + 1)
                    p2_head(t + 1)
                mm2_out(t)

        store_tap("tap_qhT0", qhT, ["qhT"])

    def fin(eng):
        for q in ("sync", "gpsimd"):
            st = b.dma_state[q]
            for i, tot in enumerate(st["tot"]):
                if tot > 0:
                    eng.wait_ge(b.dma_pools[q][i], tot)
        return None

    b.op("sync", fin, reads=[], writes=[], tag="final_wait")

    b.finalize_and_emit()
    return nc


def _ln_group(b, grp, src, st6, wrk, lnsc, rcol, mcol, reads, tag, nchunk,
              extra_stats_w=(), eps_ap=None, pcol=None):
    g = 2 * nchunk
    W = 512 * nchunk

    def fn(e):
        for i in range(nchunk):
            e.bn_stats(st6[:, 6 * i:6 * i + 6], src[:, 512 * i:512 * (i + 1)])
        e.drain()
        tr = st6[:, 0:6 * nchunk].rearrange("p (a c) -> p a c", c=3)
        e.reduce_sum(wrk[:, 0:1], tr[:, :, 1], AX.X)
        e.tensor_tensor(wrk[:, 8:8 + g], tr[:, :, 1], tr[:, :, 1], ALU.mult)
        e.reduce_sum(wrk[:, 2:3], tr[:, :, 2], AX.X)
        e.reduce_sum(wrk[:, 1:2], wrk[:, 8:8 + g], AX.X)
        e.tensor_scalar(wrk[:, 3:4], wrk[:, 0:1], wrk[:, 0:1],
                        1.0 / (g * g), ALU.mult, ALU.mult)
        e.scalar_tensor_tensor(wrk[:, 4:5], wrk[:, 1:2], 256.0, wrk[:, 2:3],
                               ALU.mult, ALU.add)
        e.tensor_scalar(lnsc[:, mcol:mcol + 1], wrk[:, 0:1], -1.0 / g, None,
                        ALU.mult)
        return e.scalar_tensor_tensor(wrk[:, 5:6], wrk[:, 4:5], 1.0 / W,
                                      wrk[:, 3:4], ALU.mult, ALU.subtract)
    b.op("vector", fn, reads=list(reads) + [grp],
         writes=[grp, f"{tag}:st", f"{tag}:mu"] + list(extra_stats_w),
         tag=f"{tag}_st")

    def fsc(e):
        e.activation(lnsc[:, rcol:rcol + 1], wrk[:, 5:6], AF.Ln, bias=eps_ap)
        r = e.activation(lnsc[:, rcol:rcol + 1], lnsc[:, rcol:rcol + 1],
                         AF.Exp, scale=-0.5)
        if pcol is not None:
            e.activation(lnsc[:, pcol:pcol + 1], lnsc[:, mcol:mcol + 1],
                         AF.Identity, scale=lnsc[:, rcol:rcol + 1])
            e.activation(wrk[:, 6:7], lnsc[:, mcol:mcol + 1], AF.Identity)
            r = e.activation(wrk[:, 7:8], lnsc[:, mcol:mcol + 1], AF.Identity)
        return r
    b.op("scalar", fsc,
         reads=[f"{tag}:st", f"{tag}:mu", grp, "eps_sb"],
         writes=[grp, f"{tag}:sc"], tag=f"{tag}_sc")


def _ln_center(b, grp, src, out_sl, lnsc, rcol, pcol, reads, wkeys, tag):
    b.op("scalar", lambda e:
         e.activation(out_sl, src, AF.Identity,
                      scale=lnsc[:, rcol:rcol + 1],
                      bias=lnsc[:, pcol:pcol + 1]),
         reads=[f"{tag}:sc", grp] + list(reads),
         writes=[grp] + list(wkeys), tag=f"{tag}_ap")


def kernel(**inputs):
    in_maps = _host_prep(inputs)
    nc = build_program()
    res = run_bass_kernel_spmd(nc, in_maps, core_ids=list(range(N_CORES)))
    outs = [r["y"] for r in res.results]
    out = np.stack(outs, 0).astype(np.float32)
    if _DEBUG_TAPS:
        kernel._tap_results = res.results
    return out
